# version 05
# baseline (speedup 1.0000x reference)
import numpy as np
import ml_dtypes
from contextlib import ExitStack
import concourse.bass as bass
import concourse.mybir as mybir
from concourse.bass_utils import run_bass_kernel_spmd

F32 = mybir.dt.float32
BF16 = mybir.dt.bfloat16
I32 = mybir.dt.int32
AF = mybir.ActivationFunctionType
ALU = mybir.AluOpType
AX = mybir.AxisListType
BF = ml_dtypes.bfloat16

N_CORES = 8
D = 2048
T = 2048
HID = 5632
DC = D // 128
NH = 16
NJC = HID // 128
EPS_LN = 1e-5
INVD = 1.0 / D
NS = 4
QB = 512
KC = 128
MPS = QB // KC
RSQHD = float(1.0 / np.sqrt(128.0))

ENGINES = ("tensor", "scalar", "vector", "gpsimd", "sync")


class K:
    def __init__(self, name):
        self.nc = bass.Bass("TRN2", target_bir_lowering=False, debug=False)
        self.name = name
        self.q = {e: [] for e in ENGINES}
        self.ctr = {}
        self._semh = {}
        self._waited = {e: {} for e in ENGINES}
        self.es = ExitStack()
        self._uid = 0

    def sem(self, name):
        if name not in self._semh:
            self._semh[name] = self.es.enter_context(self.nc.semaphore(name))
            self.ctr[name] = 0
        return self._semh[name]

    def uname(self, nm):
        self._uid += 1
        return f"{nm}_{self._uid}"

    def sbuf(self, name, shape, dt):
        return self.es.enter_context(self.nc.sbuf_tensor(self.uname(name), list(shape), dt))

    def psum(self, name, shape, dt=F32):
        return self.es.enter_context(self.nc.psum_tensor(self.uname(name), list(shape), dt))

    def psum_pool(self):
        quad = self.psum("pbq", [128, 2048])
        pb = [quad[:, i * 512:(i + 1) * 512] for i in range(4)]
        pb += [self.psum("pb", [128, 512]) for _ in range(4)]
        return pb, quad

    def dram(self, name, shape, dt, kind="Internal"):
        return self.nc.dram_tensor(name, list(shape), dt, kind=kind)

    def op(self, eng, build, waits=(), inc=None, inc_n=1):
        wl = []
        for s, v in waits:
            if v <= 0:
                continue
            assert v <= self.ctr.get(s, 0), \
                f"wait {s}>={v} exceeds planned {self.ctr.get(s, 0)}"
            if self._waited[eng].get(s, -1) >= v:
                continue
            self._waited[eng][s] = v
            wl.append((self.sem(s), v))
        inch = None
        if inc is not None:
            inch = self.sem(inc)
            self.ctr[inc] += inc_n

        def fn(e, wl=wl, build=build, inch=inch, inc_n=inc_n):
            for h, v in wl:
                e.wait_ge(h, v)
            r = build(e)
            if inch is not None:
                r.then_inc(inch, inc_n)
        self.q[eng].append(fn)

    def wait(self, eng, waits):
        self.op(eng, lambda e: None, waits=waits)

    def fence_all(self):
        w = [(s, self.ctr[s]) for s in self.ctr]
        for e in ENGINES:
            self.wait(e, w)

    def constf(self, value):
        if not hasattr(self, "_cstbuf"):
            self._cstbuf = self.sbuf("cst", [1, 16], F32)
            self._csts = {}
        if value not in self._csts:
            i = len(self._csts)
            assert i < 16
            self._csts[value] = i
            ap = self._cstbuf[0:1, i:i + 1]
            self.nc.const_aps.register(F32, value, ap)
            self.op("vector", lambda e, ap=ap, value=value: e.memset(ap, value),
                    inc="init")

    def n(self, semname):
        return self.ctr.get(semname, 0)

    def finish(self):
        with self.nc.Block() as block:
            for e in ENGINES:
                lst = self.q[e]
                if not lst:
                    continue
                dec = getattr(block, e)

                def body(eng, lst=lst):
                    for fn in lst:
                        fn(eng)
                dec(body)
        self.es.close()
        return self.nc


def _r(ap, spec, **kw):
    return ap.rearrange(spec, **kw)



def emit_scales_l0(p, W):
    sgs = p.sbuf("sgs", [128, NJC], F32)
    sus = p.sbuf("sus", [128, NJC], F32)
    sds = p.sbuf("sds", [128, DC], F32)
    p.op("sync", lambda e: e.dma_start(out=sgs[:, :], in_=W["sg"][:, :]), inc="sload", inc_n=16)
    p.op("sync", lambda e: e.dma_start(out=sus[:, :], in_=W["su"][:, :]), inc="sload", inc_n=16)
    p.op("sync", lambda e: e.dma_start(out=sds[:, :], in_=W["sd"][:, :]), inc="sload", inc_n=16)
    p.op("vector", lambda e: e.tensor_scalar(out=sds[:, :], in0=sds[:, :],
                                             scalar1=0.5, scalar2=None, op0=ALU.mult),
         waits=[("sload", p.n("sload"))], inc="init")
    return {"sgs": sgs, "sus": sus, "sds": sds,
            "ready": [("sload", p.n("sload")), ("init", p.n("init"))]}



def emit_quant_body(p, pb, specs):
    es2 = ExitStack()

    def sb(nm, shape, dt):
        return es2.enter_context(p.nc.sbuf_tensor(p.uname(nm), list(shape), dt))
    qw = sb("qw", [128, 2 * 2816], F32)
    qp = sb("qp", [128, 2816], BF16)
    qn = sb("qn", [128, 2816], BF16)
    qo = sb("qo", [128, 2 * 2816], BF16)
    strips = sb("strips", [128, 64], F32)
    onef = sb("onef", [128, 1], F32)
    onefr = sb("onefr", [1, 128], F32)
    gval = sb("gval", [128, 4], F32)
    p.op("vector", lambda e: e.memset(onef[:, :], 1.0))
    p.op("vector", lambda e: e.memset(onefr[:, :], 1.0), inc="qrow")
    base_qbc = p.n("qbc")

    def tiles_of(rows, cols):
        w = 2816 if cols == HID else 2048
        return [(rc * 128, cc * w, w) for rc in range(rows // 128)
                for cc in range(cols // w)]

    for mi, (src, dst, rows, cols) in enumerate(specs):
        tl = tiles_of(rows, cols)
        nt = len(tl)
        base_abs = p.n("qabs")
        for i, (r0, c0, w) in enumerate(tl):
            sl = i % 2
            qeng = "scalar" if sl == 0 else "sync"
            qsem = "qlA" if sl == 0 else "qlS"
            p.op(qeng, lambda e, src=src, r0=r0, c0=c0, w=w, sl=sl: e.dma_start(
                out=qw[:, sl * 2816:sl * 2816 + w], in_=src[r0:r0 + 128, c0:c0 + w]),
                waits=[("qabs", base_abs + i - 1), ("qbc", base_qbc + mi),
                       ("qdv", p.n("qdv"))],
                inc=qsem, inc_n=16)
            p.op("scalar", lambda e, w=w, sl=sl, i=i: e.activation(
                out=qw[:, sl * 2816:sl * 2816 + w], in_=qw[:, sl * 2816:sl * 2816 + w],
                func=AF.Abs, bias=0.0, scale=1.0, accum_out=strips[:, i:i + 1]),
                waits=[(qsem, p.n(qsem))], inc="qabs")
        p.op("vector", lambda e, nt=nt: e.tensor_reduce(
            out=strips[:, 0:1], in_=strips[:, 0:nt], axis=AX.X, op=ALU.add),
            waits=[("qabs", base_abs + nt)], inc="qrow")
        p.op("tensor", lambda e: e.matmul(pb[7][0:1, 0:1], strips[:, 0:1], onef[:, :],
             start=True, stop=True), waits=[("qrow", p.n("qrow"))], inc="qmm")
        nelem = float(rows * cols)
        p.op("vector", lambda e, nelem=nelem: e.tensor_scalar(
            out=gval[0:1, 0:1], in0=pb[7][0:1, 0:1],
            scalar1=0.5 / nelem, scalar2=0.5e-8, op0=ALU.mult, op1=ALU.add),
            waits=[("qmm", p.n("qmm"))], inc="qrow")
        p.op("tensor", lambda e: e.matmul(pb[6][:, 0:1], onefr[:, :], gval[0:1, 0:1],
             start=True, stop=True), waits=[("qrow", p.n("qrow"))], inc="qmm")
        p.op("vector", lambda e: e.tensor_copy(out=gval[:, 0:1], in_=pb[6][:, 0:1]),
             waits=[("qmm", p.n("qmm"))])
        p.op("vector", lambda e: e.tensor_scalar(
            out=gval[:, 1:2], in0=pb[6][:, 0:1], scalar1=-1.0, scalar2=None,
            op0=ALU.mult), inc="qbc")
        base_dv = p.n("qdv")
        for i, (r0, c0, w) in enumerate(tl):
            sl = i % 2
            qeng = "scalar" if sl == 0 else "sync"
            qsem = "qlA" if sl == 0 else "qlS"
            p.op(qeng, lambda e, src=src, r0=r0, c0=c0, w=w, sl=sl: e.dma_start(
                out=qw[:, sl * 2816:sl * 2816 + w], in_=src[r0:r0 + 128, c0:c0 + w]),
                waits=[("qbc", base_qbc + mi + 1), ("qdv", base_dv + 3 * (i - 1))],
                inc=qsem, inc_n=16)
            ld = p.n(qsem)
            p.op("vector", lambda e, w=w, sl=sl: e.tensor_scalar(
                out=qp[:, 0:w], in0=qw[:, sl * 2816:sl * 2816 + w],
                scalar1=gval[:, 0:1], scalar2=None, op0=ALU.is_ge),
                waits=[(qsem, ld)], inc="qdv")
            p.op("vector", lambda e, w=w, sl=sl: e.tensor_scalar(
                out=qn[:, 0:w], in0=qw[:, sl * 2816:sl * 2816 + w],
                scalar1=gval[:, 1:2], scalar2=None, op0=ALU.is_le), inc="qdv")
            p.op("vector", lambda e, w=w, sl=sl: e.tensor_tensor(
                out=qo[:, sl * 2816:sl * 2816 + w], in0=qp[:, 0:w], in1=qn[:, 0:w],
                op=ALU.subtract),
                waits=[(f"qst{sl}", p.n(f"qst{sl}"))], inc="qdv")
            p.op("gpsimd", lambda e, dst=dst, r0=r0, c0=c0, w=w, sl=sl: e.dma_start(
                out=dst[r0:r0 + 128, c0:c0 + w], in_=qo[:, sl * 2816:sl * 2816 + w]),
                waits=[("qdv", p.n("qdv"))], inc=f"qst{sl}", inc_n=16)
    p.wait("sync", [("qst0", p.n("qst0")), ("qst1", p.n("qst1"))])
    es2.close()



def emit_ffn_body(p, layer, xT, yT, W, pb, S=None):
    es = ExitStack()

    def sb(nm, shape, dt):
        return es.enter_context(p.nc.sbuf_tensor(p.uname(nm), list(shape), dt))
    xf = sb("xf", [128, 16 * 512], F32)
    xs = sb("xs", [128, 2 * 512], BF16)
    xtmp = sb("xtmp", [128, 512], F32)
    xq = xtmp[:, :].bitcast(BF16)
    xl = sb("xl", [128, DC * 1024], BF16)
    hh = sb("hh", [128, NJC * 1024], BF16)
    wgb = sb("wgb", [128, 2 * DC * 256], BF16)
    wub = sb("wub", [128, 2 * DC * 256], BF16)
    wdb = sb("wdb", [128, 2 * 4 * 512], BF16)
    wdbf = wdb[:, :].bitcast(F32)
    gsb = sb("gsb", [128, 2 * 512], BF16)
    usb = sb("usb", [128, 2 * 512], BF16)
    xsf = xs[:, :].bitcast(F32)
    yv = xf[:, 8 * 512:10 * 512]
    rows = sb("rows", [1, 2 * 512], F32)
    brow = sb("brow", [1, 2 * 512], BF16)
    onec = sb("onec", [128, 1], BF16)
    oner = sb("oner", [1, 128], BF16)

    if layer == 0:
        wsrc_g, wsrc_u, wsrc_d = W["gwT"], W["uwT"], W["dwT"]
        wq_eng = "gpsimd"
        sgs, sus, sds = S["sgs"], S["sus"], S["sds"]
        s_ready = S["ready"]
    else:
        wsrc_g, wsrc_u, wsrc_d = W["w1q"], W["w2q"], W["w3q"]
        wq_eng = "sync"
        s_ready = []

    p.op("vector", lambda e: e.memset(onec[:, :], 1.0))
    p.op("vector", lambda e: e.memset(oner[:, :], 1.0), inc="init")

    b_dn = p.n("rel_dn")
    b_gmm = p.n("gu_mm")
    b_dmm = p.n("dn_mm")
    b_hok = p.n("h_ok")
    ln_ts = [0, 0]

    for tb in range(2):
        t0 = tb * 1024
        for tsub in range(2):
            ts4 = tb * 2 + tsub
            tc0 = ts4 * 512
            base_mm = p.n("ln_mm")
            base_row = p.n("ln_row")
            base_act = p.n("ln_act")
            base_bc = p.n("ln_bc")
            p.op("sync", lambda e, tc0=tc0: e.dma_start(
                out=_r(xf[:, :], "p (c t) -> p c t", t=512),
                in_=_r(xT[:, tc0:tc0 + 512], "(c p) t -> p c t", p=128)),
                waits=[("ln_ap", p.n("ln_ap")), ("rel_dn", b_dn + 32 * tb),
                       ("sst0", p.n("sst0")), ("sst1", p.n("sst1"))],
                inc="xload", inc_n=16)
            xld = p.n("xload")
            for dc in range(DC):
                sl = dc % 2
                p.op("vector", lambda e, dc=dc, sl=sl: e.tensor_copy(
                    out=xq[:, sl * 512:(sl + 1) * 512], in_=xf[:, dc * 512:(dc + 1) * 512]),
                    waits=[("xload", xld), ("ln_mm", base_mm + dc - 1)], inc="ln_xq")
                p.op("scalar", lambda e, dc=dc, sl=sl: e.activation(
                    out=xs[:, sl * 512:(sl + 1) * 512], in_=xf[:, dc * 512:(dc + 1) * 512],
                    func=AF.Square, bias=0.0, scale=1.0),
                    waits=[("xload", xld), ("ln_mm", base_mm + dc - 1)], inc="ln_xs")
                w0 = [("ln_xq", p.n("ln_xq")), ("ln_xs", p.n("ln_xs"))]
                if dc == 0:
                    w0 += [("ln_row", base_row), ("rel_dn", b_dn + 32 * tb),
                           ("init", p.n("init"))]
                p.op("tensor", lambda e, sl=sl, dc=dc: e.matmul(
                    pb[0][0:1, :], onec[:, :], xq[:, sl * 512:(sl + 1) * 512],
                    start=(dc == 0), stop=(dc == DC - 1)), waits=w0)
                p.op("tensor", lambda e, sl=sl, dc=dc: e.matmul(
                    pb[1][0:1, :], onec[:, :], xs[:, sl * 512:(sl + 1) * 512],
                    start=(dc == 0), stop=(dc == DC - 1)), inc="ln_mm")
            m_, msq = rows[0:1, 0:512], rows[0:1, 512:1024]
            var, lnv = rows[0:1, 0:512], rows[0:1, 512:1024]
            bm, br = brow[0:1, 0:512], brow[0:1, 512:1024]
            p.op("vector", lambda e: e.tensor_scalar(out=m_, in0=pb[0][0:1, :],
                 scalar1=INVD, scalar2=None, op0=ALU.mult),
                 waits=[("ln_mm", base_mm + DC), ("ln_act", p.n("ln_act"))])
            p.op("vector", lambda e: e.tensor_copy(out=bm, in_=m_), inc="ln_row")
            p.op("vector", lambda e: e.tensor_tensor(out=msq, in0=m_, in1=m_, op=ALU.mult))
            p.op("vector", lambda e: e.scalar_tensor_tensor(out=var, in0=pb[1][0:1, :],
                 scalar=INVD, in1=msq, op0=ALU.mult, op1=ALU.subtract), inc="ln_row")
            p.op("scalar", lambda e: e.activation(out=lnv, in_=var, func=AF.Ln,
                 bias=EPS_LN, scale=1.0),
                 waits=[("ln_row", base_row + 2), ("init", 1)])
            p.op("scalar", lambda e: e.activation(out=br, in_=lnv, func=AF.Exp,
                 bias=0.0, scale=-0.5), inc="ln_act")
            p.op("tensor", lambda e: e.matmul(pb[2][:, :], oner[:, :], bm, start=True, stop=True),
                 waits=[("ln_row", base_row + 2), ("ln_ap", p.n("ln_ap"))], inc="ln_bc")
            p.op("tensor", lambda e: e.matmul(pb[3][:, :], oner[:, :], br, start=True, stop=True),
                 waits=[("ln_act", base_act + 1)], inc="ln_bc")
            mb_sb = xsf[:, 0:512]
            rb_sb = wdbf[:, 0:512]
            pslot = [xtmp[:, :], wdbf[:, 512:1024]]
            p.op("vector", lambda e: e.tensor_copy(out=mb_sb, in_=pb[2][:, :]),
                 waits=[("ln_bc", base_bc + 2)])
            p.op("vector", lambda e: e.tensor_copy(out=rb_sb, in_=pb[3][:, :]),
                 inc="ln_ev")
            base_pl = p.n("ln_pl")
            evv = p.n("ln_ev")
            for dc in range(DC):
                p.op("gpsimd", lambda e, dc=dc, sl=pslot[dc % 2]: e.tensor_tensor(
                    out=sl, in0=xf[:, dc * 512:(dc + 1) * 512], in1=mb_sb,
                    op=ALU.subtract),
                    waits=[("ln_ev", evv), ("ln_ap", p.n("ln_ap") - 1),
                           ("gu_mm", b_gmm + 44 * tb)], inc="ln_pl")
                p.op("vector", lambda e, dc=dc, sl=pslot[dc % 2], off=dc * 1024 + tsub * 512: e.tensor_tensor(
                    out=xl[:, off:off + 512], in0=sl, in1=rb_sb,
                    op=ALU.mult), waits=[("ln_pl", base_pl + dc + 1)], inc="ln_ap")
            ln_ts[ts4 % 2] = p.n("ln_ap")
            p.wait("tensor", [("ln_ap", p.n("ln_ap"))])

        base_gu = p.n("gu_mm")
        base_gug = p.n("gu_g")
        base_rg = p.n("rel_g")
        base_ru = p.n("rel_u")
        base_wf = p.n("wfix")
        base_h = p.n("h_ok")
        ln_done = p.n("ln_ap")
        for jp in range(NJC // 2):
          buf = jp % 2
          wsem = f"wld{buf}"
          woff = buf * DC * 256
          c0 = jp * 256
          p.op(wq_eng, lambda e, c0=c0, woff=woff: e.dma_start(
              out=_r(wgb[:, woff:woff + DC * 256], "p (c j) -> p c j", j=256),
              in_=_r(wsrc_g[:, c0:c0 + 256], "(c p) j -> p c j", p=128)),
              waits=[("gu_mm", max(base_gu + 2 * jp - 2, base_gu))], inc=wsem, inc_n=16)
          p.op(wq_eng, lambda e, c0=c0, woff=woff: e.dma_start(
              out=_r(wub[:, woff:woff + DC * 256], "p (c j) -> p c j", j=256),
              in_=_r(wsrc_u[:, c0:c0 + 256], "(c p) j -> p c j", p=128)),
              inc=wsem, inc_n=16)
          wld = p.n(wsem)
          if layer == 0:
              p.op("vector", lambda e, woff=woff: e.tensor_scalar(
                  out=wgb[:, woff:woff + DC * 256], in0=wgb[:, woff:woff + DC * 256],
                  scalar1=-128.0, scalar2=None, op0=ALU.add),
                  waits=[(wsem, wld)], inc="wfix")
              p.op("vector", lambda e, woff=woff: e.tensor_scalar(
                  out=wub[:, woff:woff + DC * 256], in0=wub[:, woff:woff + DC * 256],
                  scalar1=-128.0, scalar2=None, op0=ALU.add), inc="wfix")
          for jcl in range(2):
            jc = 2 * jp + jcl
            jb = jc % 2
            gb = [pb[0 + 4 * jb], pb[1 + 4 * jb]]
            ub = [pb[2 + 4 * jb], pb[3 + 4 * jb]]
            w0 = [("rel_g", base_rg + 2 * (jc - 1)),
                  ("rel_u", base_ru + 2 * (jc - 1))]
            if layer == 0:
                w0.append(("wfix", base_wf + 2 * (jp + 1)))
            else:
                w0.append((wsem, wld))
            if jc == 0:
                w0.append(("rel_dn", b_dn + 32 * tb))
            for which, banks, wbuf in (("g", gb, wgb), ("u", ub, wub)):
                for tsub in range(2):
                    w0t = w0 + [("ln_ap", ln_ts[tsub])]
                    for dc in range(DC):
                        p.op("tensor", lambda e, banks=banks, tsub=tsub, dc=dc, wbuf=wbuf, woff=woff, jcl=jcl: e.matmul(
                            banks[tsub][:, :],
                            wbuf[:, woff + dc * 256 + jcl * 128:woff + dc * 256 + jcl * 128 + 128],
                            xl[:, dc * 1024 + tsub * 512:dc * 1024 + tsub * 512 + 512],
                            start=(dc == 0), stop=(dc == DC - 1)),
                            waits=(w0t if dc == 0 else ()),
                            inc=("gu_g" if (which == "g" and dc == DC - 1) else
                                 ("gu_mm" if (which == "u" and tsub == 1 and dc == DC - 1) else None)))
            for tsub in range(2):
                gslot = gsb[:, tsub * 512:(tsub + 1) * 512]
                uslot = usb[:, tsub * 512:(tsub + 1) * 512]
                if layer == 0:
                    p.op("scalar", lambda e, gslot=gslot, b=gb[tsub], jc=jc: e.activation(
                        out=gslot, in_=b[:, :], func=AF.Silu, bias=0.0,
                        scale=sgs[:, jc:jc + 1]),
                        waits=[("gu_g", base_gug + 2 * jc + tsub + 1),
                               ("h_ok", base_h + 2 * jc + tsub - 1)] + s_ready,
                        inc="rel_g")
                    p.op("vector", lambda e, uslot=uslot, b=ub[tsub], jc=jc: e.tensor_scalar(
                        out=uslot, in0=b[:, :], scalar1=sus[:, jc:jc + 1], scalar2=None,
                        op0=ALU.mult), waits=[("gu_mm", base_gu + jc + 1)], inc="rel_u")
                else:
                    p.op("scalar", lambda e, gslot=gslot, b=gb[tsub]: e.activation(
                        out=gslot, in_=b[:, :], func=AF.Sigmoid, bias=0.0, scale=1.0),
                        waits=[("gu_g", base_gug + 2 * jc + tsub + 1),
                               ("h_ok", base_h + 2 * jc + tsub - 1)], inc="rel_g")
                    p.op("vector", lambda e, uslot=uslot, b=ub[tsub]: e.tensor_copy(
                        out=uslot, in_=b[:, :]),
                        waits=[("gu_mm", base_gu + jc + 1)], inc="rel_u")
                p.op("vector", lambda e, gslot=gslot, uslot=uslot, off=jc * 1024 + tsub * 512: e.tensor_tensor(
                    out=hh[:, off:off + 512], in0=gslot, in1=uslot, op=ALU.mult),
                    waits=[("rel_g", base_rg + 2 * jc + tsub + 1),
                           ("dn_mm", b_dmm + 176 * tb)], inc="h_ok")

        for ocg in range(4):
            base_dn = p.n("dn_mm")
            for i in range(8):
                oc = ocg * 4 + i // 2
                tss = i % 2
                p.op("sync", lambda e, oc=oc, tss=tss, i=i, t0=t0: e.dma_start(
                    out=xf[:, i * 512:(i + 1) * 512],
                    in_=xT[oc * 128:(oc + 1) * 128, t0 + tss * 512:t0 + tss * 512 + 512]),
                    waits=[("ln_ap", ln_done), ("rel_dn", b_dn + 8 * (4 * tb + ocg))],
                    inc="xload", inc_n=16)
            xld = p.n("xload")
            for jg in range(NJC // 4):
              buf = jg % 2
              wsem = f"wdl{buf}"
              c0 = ocg * 512
              p.op(wq_eng, lambda e, jg=jg, c0=c0, buf=buf: e.dma_start(
                  out=_r(wdb[:, buf * 2048:(buf + 1) * 2048], "p (j c) -> p j c", c=512),
                  in_=_r(wsrc_d[jg * 512:(jg + 1) * 512, c0:c0 + 512], "(j p) c -> p j c", p=128)),
                  waits=[("dn_mm", max(base_dn + 4 * jg - 4, base_dn)),
                         ("ln_ap", ln_done)], inc=wsem, inc_n=16)
              wld = p.n(wsem)
              if layer == 0:
                  p.op("vector", lambda e, buf=buf: e.tensor_scalar(
                      out=wdb[:, buf * 2048:(buf + 1) * 2048],
                      in0=wdb[:, buf * 2048:(buf + 1) * 2048],
                      scalar1=-128.0, scalar2=None, op0=ALU.add),
                      waits=[(wsem, wld)], inc="wdfix")
                  wfx = p.n("wdfix")
              for jcl in range(4):
                jc = 4 * jg + jcl
                if layer == 0:
                    w0 = [("wdfix", wfx)]
                else:
                    w0 = [(wsem, wld)]
                if jc == 0:
                    w0 = w0 + [("rel_dn", b_dn + 8 * (4 * tb + ocg)),
                               ("h_ok", b_hok + 88 * (tb + 1))]
                for oc4 in range(4):
                    for tss in range(2):
                        p.op("tensor", lambda e, oc4=oc4, tss=tss, jc=jc, buf=buf, jcl=jcl: e.matmul(
                            pb[oc4 * 2 + tss][:, :],
                            wdb[:, buf * 2048 + jcl * 512 + oc4 * 128:buf * 2048 + jcl * 512 + oc4 * 128 + 128],
                            hh[:, jc * 1024 + tss * 512:jc * 1024 + tss * 512 + 512],
                            start=(jc == 0), stop=(jc == NJC - 1)),
                            waits=(w0 if (oc4 == 0 and tss == 0) else ()),
                            inc=("dn_mm" if (oc4 == 3 and tss == 1) else None))
            mm_dn = p.n("dn_mm")
            for i in range(8):
                oc = ocg * 4 + i // 2
                tss = i % 2
                slot = i % 2
                sa = (lambda oc=oc: sds[:, oc:oc + 1])() if layer == 0 else 0.5
                ssem = f"sst{slot}"
                p.op("vector", lambda e, i=i, slot=slot, sa=sa: e.scalar_tensor_tensor(
                    out=yv[:, slot * 512:(slot + 1) * 512], in0=pb[i][:, :],
                    scalar=sa, in1=xf[:, i * 512:(i + 1) * 512],
                    op0=ALU.mult, op1=ALU.add),
                    waits=[("dn_mm", mm_dn), ("xload", xld), (ssem, p.n(ssem))],
                    inc="rel_dn")
                p.op("scalar", lambda e, oc=oc, tss=tss, slot=slot, t0=t0: e.dma_start(
                    out=yT[oc * 128:(oc + 1) * 128, t0 + tss * 512:t0 + tss * 512 + 512],
                    in_=yv[:, slot * 512:(slot + 1) * 512]),
                    waits=[("rel_dn", p.n("rel_dn"))], inc=ssem, inc_n=16)
    es.close()



def emit_attn_body(p, layer, xT_own, yT, W, pbq, fused, cc_tag=""):
    pb, pbquad = pbq
    nh = NH
    hk = 2048 if layer == 0 else 512
    hkc = hk // 128
    rope = (layer == 0)
    alibi = (layer == 1)
    groups = [[h] for h in range(nh)] if layer == 0 else \
             [[g, g + 4, g + 8, g + 12] for g in range(4)]

    wq, wo = W["wq"], W["wo"]
    maskpat = W["maskpat"]
    ksrc = W["ksrc"]
    vsrc = W["vsrc"]

    if fused:
        kvKin = p.dram(f"kvKin{cc_tag}", [hk, T], BF16)
        kvVin = p.dram(f"kvVin{cc_tag}", [16, hkc, 128, 128], BF16)
        kvKout = p.dram(f"kvKout{cc_tag}", [2, hk, T], BF16)
        kvVout = p.dram(f"kvVout{cc_tag}", [2, 16, hkc, 128, 128], BF16)
    else:
        kvKout = p.dram(f"kvK{cc_tag}", [2, hk, T], BF16)
        kvVout = p.dram(f"kvV{cc_tag}", [2, 16, hkc, 128, 128], BF16)
    ctxd = p.dram(f"ctxd{cc_tag}", [D, T], BF16)

    esQ = ExitStack()
    QT = esQ.enter_context(p.nc.sbuf_tensor(p.uname("QT"), [128, nh * T], BF16))

    esA = ExitStack()

    def sbA(nm, shape, dt):
        return esA.enter_context(p.nc.sbuf_tensor(p.uname(nm), list(shape), dt))
    xf = sbA("xf", [128, 16 * 512], F32)
    xs = sbA("xs", [128, 2 * 512], BF16)
    xtmp = sbA("xtmp", [128, 512], F32)
    xq = xtmp[:, :].bitcast(BF16)
    xl = sbA("xl", [128, DC * T], BF16)
    wkb = sbA("wkb", [128, DC * 128], BF16)
    wvb = sbA("wvb", [128, DC * 256], BF16)
    ksg = sbA("ksg", [128, 2 * 512], BF16)
    vsg = sbA("vsg", [128, 2 * 512], BF16)
    rows = sbA("rows", [1, 2 * 512], F32)
    brow = sbA("brow", [1, 2 * 512], BF16)
    onec = sbA("onecA", [128, 1], BF16)
    oner = sbA("onerA", [1, 128], BF16)
    rta = sbA("rta", [64, 512], F32)
    rtb = sbA("rtb", [64, 512], F32)
    if rope:
        csb = sbA("csb", [64, 2 * T], BF16)

    p.op("vector", lambda e: e.memset(onec[:, :], 1.0))
    p.op("vector", lambda e: e.memset(oner[:, :], 1.0), inc="init")

    def ln_tsub(src, ts4):
        tc0 = ts4 * 512
        base_mm = p.n("ln_mm")
        base_row = p.n("ln_row")
        base_act = p.n("ln_act")
        base_bc = p.n("ln_bc")
        p.op("sync", lambda e, src=src, tc0=tc0: e.dma_start(
            out=_r(xf[:, :], "p (c t) -> p c t", t=512),
            in_=_r(src[:, tc0:tc0 + 512], "(c p) t -> p c t", p=128)),
            waits=[("ln_ap", p.n("ln_ap"))], inc="xload", inc_n=16)
        xld = p.n("xload")
        for dc in range(DC):
            sl = dc % 2
            p.op("vector", lambda e, dc=dc, sl=sl: e.tensor_copy(
                out=xq[:, sl * 512:(sl + 1) * 512], in_=xf[:, dc * 512:(dc + 1) * 512]),
                waits=[("xload", xld), ("ln_mm", base_mm + dc - 1)], inc="ln_xq")
            p.op("scalar", lambda e, dc=dc, sl=sl: e.activation(
                out=xs[:, sl * 512:(sl + 1) * 512], in_=xf[:, dc * 512:(dc + 1) * 512],
                func=AF.Square, bias=0.0, scale=1.0),
                waits=[("xload", xld), ("ln_mm", base_mm + dc - 1)], inc="ln_xs")
            w0 = [("ln_xq", p.n("ln_xq")), ("ln_xs", p.n("ln_xs"))]
            if dc == 0:
                w0 += [("ln_row", base_row), ("init", p.n("init"))]
            p.op("tensor", lambda e, sl=sl, dc=dc: e.matmul(
                pb[0][0:1, :], onec[:, :], xq[:, sl * 512:(sl + 1) * 512],
                start=(dc == 0), stop=(dc == DC - 1)), waits=w0)
            p.op("tensor", lambda e, sl=sl, dc=dc: e.matmul(
                pb[1][0:1, :], onec[:, :], xs[:, sl * 512:(sl + 1) * 512],
                start=(dc == 0), stop=(dc == DC - 1)), inc="ln_mm")
        m_, msq = rows[0:1, 0:512], rows[0:1, 512:1024]
        var, lnv = rows[0:1, 0:512], rows[0:1, 512:1024]
        bm, br = brow[0:1, 0:512], brow[0:1, 512:1024]
        p.op("vector", lambda e: e.tensor_scalar(out=m_, in0=pb[0][0:1, :],
             scalar1=INVD, scalar2=None, op0=ALU.mult),
             waits=[("ln_mm", base_mm + DC), ("ln_act", p.n("ln_act"))])
        p.op("vector", lambda e: e.tensor_copy(out=bm, in_=m_), inc="ln_row")
        p.op("vector", lambda e: e.tensor_tensor(out=msq, in0=m_, in1=m_, op=ALU.mult))
        p.op("vector", lambda e: e.scalar_tensor_tensor(out=var, in0=pb[1][0:1, :],
             scalar=INVD, in1=msq, op0=ALU.mult, op1=ALU.subtract), inc="ln_row")
        p.op("scalar", lambda e: e.activation(out=lnv, in_=var, func=AF.Ln,
             bias=EPS_LN, scale=1.0),
             waits=[("ln_row", base_row + 2), ("init", 1)])
        p.op("scalar", lambda e: e.activation(out=br, in_=lnv, func=AF.Exp,
             bias=0.0, scale=-0.5), inc="ln_act")
        p.op("tensor", lambda e: e.matmul(pb[2][:, :], oner[:, :], bm, start=True, stop=True),
             waits=[("ln_row", base_row + 2), ("ln_ap", p.n("ln_ap"))], inc="ln_bc")
        p.op("tensor", lambda e: e.matmul(pb[3][:, :], oner[:, :], br, start=True, stop=True),
             waits=[("ln_act", base_act + 1)], inc="ln_bc")
        for dc in range(DC):
            p.op("vector", lambda e, dc=dc: e.tensor_tensor(
                out=xtmp[:, :], in0=xf[:, dc * 512:(dc + 1) * 512], in1=pb[2][:, :],
                op=ALU.subtract),
                waits=[("ln_bc", base_bc + 2), ("ap_mm", p.n("ap_mm"))])
            p.op("vector", lambda e, dc=dc, off=dc * T + ts4 * 512: e.tensor_tensor(
                out=xl[:, off:off + 512], in0=xtmp[:, :], in1=pb[3][:, :],
                op=ALU.mult), inc="ln_ap")
        p.wait("tensor", [("ln_ap", p.n("ln_ap"))])

    def rope_evac(bank, csoff, dst_ap_fn, extra_waits, sem_inc):
        C = (lambda: csb[0:64, csoff:csoff + 512])()
        S_ = (lambda: csb[0:64, T + csoff:T + csoff + 512])()
        p.op("vector", lambda e, bank=bank, C=C: e.tensor_tensor(
            out=rta[:, :], in0=bank[0:64, :], in1=C, op=ALU.mult), waits=extra_waits)
        p.op("vector", lambda e, bank=bank, S_=S_: e.tensor_tensor(
            out=rtb[:, :], in0=bank[64:128, :], in1=S_, op=ALU.mult))
        p.op("vector", lambda e: e.tensor_tensor(
            out=dst_ap_fn(0), in0=rta[:, :], in1=rtb[:, :], op=ALU.subtract))
        p.op("vector", lambda e, bank=bank, C=C: e.tensor_tensor(
            out=rta[:, :], in0=bank[64:128, :], in1=C, op=ALU.mult))
        p.op("vector", lambda e, bank=bank, S_=S_: e.tensor_tensor(
            out=rtb[:, :], in0=bank[0:64, :], in1=S_, op=ALU.mult))
        p.op("vector", lambda e: e.tensor_tensor(
            out=dst_ap_fn(1), in0=rta[:, :], in1=rtb[:, :], op=ALU.add), inc=sem_inc)

    def kv_proj(kdst, vdst):
        ln_done = p.n("ln_ap")
        for kc in range(hkc):
            p.op("gpsimd", lambda e, kc=kc: e.dma_start(
                out=_r(wkb[:, :], "p (c j) -> p c j", j=128),
                in_=_r(ksrc(kc), "(c p) j -> p c j", p=128)),
                waits=[("ap_mm", p.n("ap_mm"))], inc="wpld", inc_n=16)
            wld = p.n("wpld")
            for ts4 in range(NS):
                ei = kc * NS + ts4
                bank = pb[4 + ei % 2]
                for dc in range(DC):
                    w0 = []
                    if dc == 0:
                        w0 = [("wpld", wld), ("ln_ap", ln_done),
                              ("ap_ev", p.n("ap_ev") - 1)]
                    p.op("tensor", lambda e, bank=bank, dc=dc, ts4=ts4: e.matmul(
                        bank[:, :], wkb[:, dc * 128:(dc + 1) * 128],
                        xl[:, dc * T + ts4 * 512:dc * T + ts4 * 512 + 512],
                        start=(dc == 0), stop=(dc == DC - 1)),
                        waits=w0, inc=("ap_mm" if dc == DC - 1 else None))
                kpar = ei % 2
                ksl = (lambda kpar=kpar: ksg[:, kpar * 512:(kpar + 1) * 512])()
                ew = [("ap_mm", p.n("ap_mm")), (f"kst{kpar}", p.n(f"kst{kpar}"))]
                if rope:
                    ew.append(("csld", p.n("csld")))
                    rope_evac(bank, ts4 * 512,
                              lambda half, ksl=ksl: ksl[half * 64:(half + 1) * 64, :],
                              ew, "ap_ev")
                else:
                    p.op("vector", lambda e, bank=bank, ksl=ksl: e.tensor_copy(
                        out=ksl, in_=bank[:, :]), waits=ew, inc="ap_ev")
                p.op("scalar", lambda e, kc=kc, ts4=ts4, ksl=ksl, kdst=kdst, kpar=kpar: e.dma_start(
                    out=kdst[kc * 128:(kc + 1) * 128, ts4 * 512:ts4 * 512 + 512],
                    in_=ksl),
                    waits=[("ap_ev", p.n("ap_ev"))], inc=f"kst{kpar}", inc_n=16)
        for vf in range(hk // 256):
            p.op("gpsimd", lambda e, vf=vf: e.dma_start(
                out=_r(wvb[:, :], "p (c j) -> p c j", j=256),
                in_=_r(vsrc(vf), "(c p) j -> p c j", p=128)),
                waits=[("ap_mm", p.n("ap_mm"))], inc="wpld", inc_n=16)
            wld = p.n("wpld")
            for tc in range(16):
                bank = pb[6 + tc % 2]
                for dc in range(DC):
                    w0 = []
                    if dc == 0:
                        w0 = [("wpld", wld), ("ln_ap", ln_done),
                              ("ap_ev2", p.n("ap_ev2") - 1)]
                    p.op("tensor", lambda e, bank=bank, dc=dc, tc=tc: e.matmul(
                        bank[:, 0:256], xl[:, dc * T + tc * 128:dc * T + tc * 128 + 128],
                        wvb[:, dc * 256:(dc + 1) * 256],
                        start=(dc == 0), stop=(dc == DC - 1)),
                        waits=w0, inc=("ap_mm" if dc == DC - 1 else None))
                vpar = tc % 2
                vsl = (lambda vpar=vpar: vsg[:, vpar * 512:vpar * 512 + 256])()
                p.op("vector", lambda e, bank=bank, vsl=vsl: e.tensor_copy(
                    out=vsl, in_=bank[:, 0:256]),
                    waits=[("ap_mm", p.n("ap_mm")), (f"vst{vpar}", p.n(f"vst{vpar}"))],
                    inc="ap_ev2")
                p.op("scalar", lambda e, tc=tc, vf=vf, vsl=vsl, vdst=vdst, vpar=vpar: e.dma_start(
                    out=_r(vdst[tc, vf * 2:(vf + 1) * 2, :, :], "v k h -> k v h"),
                    in_=_r(vsl, "k (v h) -> k v h", h=128)),
                    waits=[("ap_ev2", p.n("ap_ev2"))], inc=f"vst{vpar}", inc_n=16)

    def q_proj():
        ln_done = p.n("ln_ap")
        for h in range(nh):
            p.op("gpsimd", lambda e, h=h: e.dma_start(
                out=_r(wkb[:, :], "p (c j) -> p c j", j=128),
                in_=_r(wq[:, h * 128:(h + 1) * 128], "(c p) j -> p c j", p=128)),
                waits=[("ap_mm", p.n("ap_mm"))], inc="wpld", inc_n=16)
            wld = p.n("wpld")
            for ts4 in range(NS):
                bank = pb[4 + ts4 % 2]
                for dc in range(DC):
                    w0 = []
                    if dc == 0:
                        w0 = [("wpld", wld), ("ln_ap", ln_done),
                              ("ap_ev", p.n("ap_ev") - 1)]
                    p.op("tensor", lambda e, bank=bank, dc=dc, ts4=ts4: e.matmul(
                        bank[:, :], wkb[:, dc * 128:(dc + 1) * 128],
                        xl[:, dc * T + ts4 * 512:dc * T + ts4 * 512 + 512],
                        start=(dc == 0), stop=(dc == DC - 1)),
                        waits=w0, inc=("ap_mm" if dc == DC - 1 else None))
                qoff = h * T + ts4 * 512
                ew = [("ap_mm", p.n("ap_mm"))]
                if rope:
                    ew.append(("csld", p.n("csld")))
                    rope_evac(bank, ts4 * 512,
                              lambda half, qoff=qoff: QT[half * 64:(half + 1) * 64,
                                                         qoff:qoff + 512],
                              ew, "ap_ev")
                else:
                    p.op("vector", lambda e, bank=bank, qoff=qoff: e.tensor_copy(
                        out=QT[:, qoff:qoff + 512], in_=bank[:, :]),
                        waits=ew, inc="ap_ev")

    def load_cs(r):
        if not rope:
            return
        cosd, sind = W["cs"][r]
        p.op("gpsimd", lambda e, cosd=cosd: e.dma_start(out=csb[:, 0:T], in_=cosd[:, :]),
             waits=[("ap_ev", p.n("ap_ev")), ("ap_ev2", p.n("ap_ev2"))],
             inc="csld", inc_n=16)
        p.op("gpsimd", lambda e, sind=sind: e.dma_start(out=csb[:, T:2 * T], in_=sind[:, :]),
             inc="csld", inc_n=16)

    if fused:
        load_cs(2)
        for ts4 in range(NS):
            ln_tsub(xT_own, ts4)
        kv_proj(kvKin, kvVin)
        cw = [(f"kst{i}", p.n(f"kst{i}")) for i in range(2)] + \
             [(f"vst{i}", p.n(f"vst{i}")) for i in range(2)]
        p.op("gpsimd", lambda e: e.collective_compute(
            "AllGather", ALU.bypass,
            replica_groups=[[0, 1], [2, 3], [4, 5], [6, 7]],
            ins=[kvKin.ap().opt()],
            outs=[kvKout.ap().rearrange("r a b -> (r a) b").opt()]),
            waits=cw, inc="ccg")
        p.op("gpsimd", lambda e: e.collective_compute(
            "AllGather", ALU.bypass,
            replica_groups=[[0, 1], [2, 3], [4, 5], [6, 7]],
            ins=[kvVin.ap().rearrange("a b c d -> (a b c) d").opt()],
            outs=[kvVout.ap().rearrange("r a b c d -> (r a b c) d").opt()]),
            inc="ccg")
        ccv = p.n("ccg")
        q_proj()
    else:
        srcs = [W["xT_p0"], W["xT_p1"], xT_own]
        for r in range(2):
            load_cs(r)
            for ts4 in range(NS):
                ln_tsub(srcs[r], ts4)
            kv_proj(kvKout[r], kvVout[r])
        load_cs(2)
        for ts4 in range(NS):
            ln_tsub(srcs[2], ts4)
        q_proj()
        ccv = 0

    p.fence_all()
    esA.close()

    esB = ExitStack()

    def sbB(nm, shape, dt):
        return esB.enter_context(p.nc.sbuf_tensor(p.uname(nm), list(shape), dt))
    pat = sbB("pat", [128, 2 * MPS * QB], BF16)
    ksb = sbB("ksb", [128, 2 * 2 * T], BF16)
    vsb = sbB("vsb", [128, 2 * 2 * 32 * 128], BF16)
    esb = sbB("esb", [128, 2 * 2 * QB], BF16)
    onecB = sbB("onecB", [128, 1], BF16)
    onefr = sbB("onefr", [1, 128], F32)
    rrow = sbB("rrow", [1, 512], F32)
    lnrow = sbB("lnrow", [1, 512], F32)
    ctx_sb = sbB("ctx_sb", [128, 2 * 512], F32)
    den_sb = sbB("den_sb", [1, 2 * 512], F32)
    bcs = sbB("bcs", [128, 512], F32)
    csg = sbB("csg", [128, 2 * 512], BF16)
    if alibi:
        ba = sbB("ba", [6, 2 * T], BF16)
        bb = sbB("bb", [6, T], BF16)

    p.op("vector", lambda e: e.memset(onecB[:, :], 1.0))
    p.op("vector", lambda e: e.memset(onefr[:, :], 1.0), inc="binit")
    b_binit = p.n("binit")
    p.op("sync", lambda e: e.dma_start(
        out=_r(pat[:, :], "p (r d q) -> p r d q", d=MPS, q=QB),
        in_=_r(maskpat[:, :, :, :], "r d p q -> p r d q")),
        waits=[("ccg", ccv)], inc="pld", inc_n=16)
    b_pld = p.n("pld")

    kv_iters = [(gi, s) for gi in range(len(groups)) for s in range(NS)]
    sc_after, mm_after = [], []
    kld_at, vld_at = {}, {}

    def emit_kv_load(j):
        gi, s = kv_iters[j]
        nm = MPS * (s + 1)
        par = j % 2
        kw = [("ccg", ccv)]
        if j >= 2:
            kw += [("at_sc", sc_after[j - 2]), ("at_mm", mm_after[j - 2])]
        for r01 in range(2):
            p.op("sync", lambda e, r01=r01, gi=gi, nm=nm, par=par: e.dma_start(
                out=ksb[:, (par * 2 + r01) * T:(par * 2 + r01) * T + nm * 128],
                in_=kvKout[r01, gi * 128:(gi + 1) * 128, 0:nm * 128]),
                waits=kw, inc=f"kld{par}", inc_n=16)
            p.op("sync", lambda e, r01=r01, gi=gi, nm=nm, par=par: e.dma_start(
                out=_r(vsb[:, (par * 2 + r01) * 4096:(par * 2 + r01) * 4096 + nm * 128],
                       "k (m h) -> k m h", h=128),
                in_=_r(kvVout[r01, 0:nm, gi, :, :], "m k h -> k m h")),
                waits=kw, inc=f"vld{par}", inc_n=16)
        kld_at[j] = p.n(f"kld{par}")
        vld_at[j] = p.n(f"vld{par}")

    emit_kv_load(0)
    for j, (gi, s) in enumerate(kv_iters):
        if j + 1 < len(kv_iters):
            emit_kv_load(j + 1)
        hs = groups[gi]
        nm = MPS * (s + 1)
        par = j % 2
        kldv, vldv = kld_at[j], vld_at[j]
        ksem, vsem = f"kld{par}", f"vld{par}"
        koff = par * 2 * T
        voff = par * 2 * 4096
        for h in hs:
            if alibi:
                p.op("gpsimd", lambda e, h=h: e.dma_start(
                    out=_r(ba[:, :], "b (r c) -> b r c", r=2),
                    in_=_r(W["baT"][h, :, :, :], "r b c -> b r c")),
                    waits=[("at_sc", p.n("at_sc"))], inc="bld", inc_n=16)
                p.op("gpsimd", lambda e, h=h: e.dma_start(
                    out=bb[:, :], in_=W["bbT"][h, :, :]),
                    inc="bld", inc_n=16)
            bldv = p.n("bld") if alibi else 0
            n_i = 2 * nm
            tiles = [(r01, m) for r01 in range(2) for m in range(nm)]
            for pk in range(n_i // 2):
                for t_i in range(2):
                    r01, m = tiles[2 * pk + t_i]
                    i_g = 2 * pk + t_i
                    si = i_g % 4
                    bank = pb[si]
                    w0 = [(ksem, kldv), ("at_ex", p.n("at_ex") - 1)]
                    if alibi:
                        w0.append(("bld", bldv))
                    p.op("tensor", lambda e, bank=bank, r01=r01, m=m, h=h, s=s, koff=koff: e.matmul(
                        bank[:, :], ksb[:, koff + r01 * T + m * 128:koff + r01 * T + m * 128 + 128],
                        QT[:, h * T + s * 512:h * T + s * 512 + 512],
                        start=True, stop=(not alibi)),
                        waits=w0, inc=(None if alibi else "at_sc"))
                    if alibi:
                        p.op("tensor", lambda e, bank=bank, r01=r01, m=m, s=s: e.matmul(
                            bank[:, :], ba[:, r01 * T + m * 128:r01 * T + m * 128 + 128],
                            bb[:, s * 512:s * 512 + 512],
                            start=False, stop=True), inc="at_sc")
                    if m >= MPS * s:
                        d = m - MPS * s
                        p.op("vector", lambda e, bank=bank, r01=r01, d=d: e.tensor_tensor(
                            out=bank[:, :], in0=bank[:, :],
                            in1=pat[:, (r01 * MPS + d) * QB:(r01 * MPS + d + 1) * QB],
                            op=ALU.min),
                            waits=[("at_sc", p.n("at_sc")), ("pld", b_pld)], inc="at_mn")
                ew = [("at_sc", p.n("at_sc")), ("at_mn", p.n("at_mn")),
                      ("at_mm", p.n("at_mm") - 2)]
                q0 = (2 * pk % 4) * 512
                eoff = (pk % 2) * 1024
                p.op("scalar", lambda e, q0=q0, eoff=eoff: e.activation(
                    out=esb[:, eoff:eoff + 1024], in_=pbquad[:, q0:q0 + 1024],
                    func=AF.Exp, bias=0.0, scale=RSQHD),
                    waits=ew, inc="at_ex")
                for t_i in range(2):
                    r01, m = tiles[2 * pk + t_i]
                    i_g = 2 * pk + t_i
                    cw2 = [("at_ex", p.n("at_ex")), (vsem, vldv)]
                    if i_g == 0:
                        cw2.append(("at_cx", p.n("at_cx")))
                    p.op("tensor", lambda e, r01=r01, m=m, st=(i_g == 0), sp=(i_g == n_i - 1),
                         eo=eoff + t_i * 512, voff=voff: e.matmul(
                        pb[4][:, :], vsb[:, voff + r01 * 4096 + m * 128:voff + r01 * 4096 + m * 128 + 128],
                        esb[:, eo:eo + 512],
                        start=st, stop=sp), waits=cw2)
                    ow = [("binit", b_binit)]
                    if i_g == 0:
                        ow.append(("at_rc", p.n("at_rc")))
                    p.op("tensor", lambda e, st=(i_g == 0), sp=(i_g == n_i - 1),
                         eo=eoff + t_i * 512: e.matmul(
                        pb[5][0:1, :], onecB[:, :], esb[:, eo:eo + 512],
                        start=st, stop=sp), waits=ow, inc="at_mm")
            hpar = p.n("at_cx") % 2
            p.op("vector", lambda e, hpar=hpar: e.tensor_copy(
                out=ctx_sb[:, hpar * 512:(hpar + 1) * 512], in_=pb[4][:, :]),
                waits=[("at_mm", p.n("at_mm")), ("at_nm", p.n("at_nm") - 1)],
                inc="at_cx")
            p.op("vector", lambda e, hpar=hpar: e.tensor_copy(
                out=den_sb[0:1, hpar * 512:(hpar + 1) * 512], in_=pb[5][0:1, :]),
                waits=[("at_rp", p.n("at_rp") - 1)], inc="at_rc")
            p.op("scalar", lambda e, hpar=hpar: e.activation(
                out=lnrow[:, :], in_=den_sb[0:1, hpar * 512:(hpar + 1) * 512],
                func=AF.Ln, bias=0.0, scale=1.0),
                waits=[("at_rc", p.n("at_rc")), ("at_bcm", p.n("at_bcm"))])
            p.op("scalar", lambda e: e.activation(
                out=rrow[:, :], in_=lnrow[:, :], func=AF.Exp, bias=0.0, scale=-1.0),
                inc="at_rp")
            p.op("tensor", lambda e: e.matmul(pb[6][:, :], onefr[:, :], rrow[:, :],
                 start=True, stop=True),
                 waits=[("at_rp", p.n("at_rp")), ("at_bc", p.n("at_bc"))], inc="at_bcm")
            p.op("vector", lambda e: e.tensor_copy(out=bcs[:, :], in_=pb[6][:, :]),
                 waits=[("at_bcm", p.n("at_bcm"))], inc="at_bc")
            cpar = p.n("at_nm") % 2
            p.op("vector", lambda e, cpar=cpar, hpar=hpar: e.tensor_tensor(
                out=csg[:, cpar * 512:(cpar + 1) * 512],
                in0=ctx_sb[:, hpar * 512:(hpar + 1) * 512],
                in1=bcs[:, :], op=ALU.mult),
                waits=[(f"cst{cpar}", p.n(f"cst{cpar}"))], inc="at_nm")
            p.op("gpsimd", lambda e, h=h, s=s, cpar=cpar: e.dma_start(
                out=ctxd[h * 128:(h + 1) * 128, s * 512:s * 512 + 512],
                in_=csg[:, cpar * 512:(cpar + 1) * 512]),
                waits=[("at_nm", p.n("at_nm"))], inc=f"cst{cpar}", inc_n=16)
        sc_after.append(p.n("at_sc"))
        mm_after.append(p.n("at_mm"))

    p.fence_all()
    esB.close()
    esQ.close()

    esC = ExitStack()

    def sbC(nm, shape, dt):
        return esC.enter_context(p.nc.sbuf_tensor(p.uname(nm), list(shape), dt))
    cbuf = sbC("cbuf", [128, DC * T], BF16)
    wob = sbC("wob", [128, 2 * DC * 128], BF16)
    yvC = sbC("yvC", [128, 2 * 512], F32)
    xres = sbC("xres", [128, 2 * 512], F32)
    if layer == 1:
        boS = sbC("boS", [128, DC], F32)
        p.op("sync", lambda e: e.dma_start(out=boS[:, :], in_=W["boR"][:, :]),
             inc="cld", inc_n=16)
    p.op("sync", lambda e: e.dma_start(
        out=_r(cbuf[:, :], "p (c t) -> p c t", t=T),
        in_=_r(ctxd[:, :], "(c p) t -> p c t", p=128)), inc="cld", inc_n=16)
    cldv = p.n("cld")
    ec0 = p.n("oc_ev")
    ec = 0
    for oc in range(DC):
        opar = oc % 2
        wsemo = f"wold{opar}"
        woffo = opar * DC * 128
        p.op("gpsimd", lambda e, oc=oc, woffo=woffo: e.dma_start(
            out=_r(wob[:, woffo:woffo + DC * 128], "p (c j) -> p c j", j=128),
            in_=_r(wo[:, oc * 128:(oc + 1) * 128], "(c p) j -> p c j", p=128)),
            waits=[("oc_mm", max(0, p.n("oc_mm") - 4))], inc=wsemo, inc_n=16)
        wldv = p.n(wsemo)
        for ts4 in range(NS):
            bank = pb[ec % 4]
            for hc in range(DC):
                w0 = []
                if hc == 0:
                    w0 = [(wsemo, wldv), ("cld", cldv),
                          ("oc_ev", max(0, ec0 + ec - 3))]
                p.op("tensor", lambda e, bank=bank, hc=hc, ts4=ts4, woffo=woffo: e.matmul(
                    bank[:, :], wob[:, woffo + hc * 128:woffo + hc * 128 + 128],
                    cbuf[:, hc * T + ts4 * 512:hc * T + ts4 * 512 + 512],
                    start=(hc == 0), stop=(hc == DC - 1)),
                    waits=w0, inc=("oc_mm" if hc == DC - 1 else None))
            par = ec % 2
            p.op("sync", lambda e, oc=oc, ts4=ts4, par=par: e.dma_start(
                out=xres[:, par * 512:(par + 1) * 512],
                in_=xT_own[oc * 128:(oc + 1) * 128, ts4 * 512:ts4 * 512 + 512]),
                waits=[("oc_ev", max(0, ec0 + ec - 1))], inc=f"xrl{par}", inc_n=16)
            sa = (lambda oc=oc: boS[:, oc:oc + 1])() if layer == 1 else 0.0
            p.op("vector", lambda e, bank=bank, par=par, sa=sa: e.scalar_tensor_tensor(
                out=yvC[:, par * 512:(par + 1) * 512], in0=bank[:, :], scalar=sa,
                in1=xres[:, par * 512:(par + 1) * 512], op0=ALU.add, op1=ALU.add),
                waits=[("oc_mm", p.n("oc_mm")), (f"xrl{par}", p.n(f"xrl{par}")),
                       (f"yst{par}", p.n(f"yst{par}")), ("cld", cldv)],
                inc="oc_ev")
            p.op("scalar", lambda e, oc=oc, ts4=ts4, par=par: e.dma_start(
                out=yT[oc * 128:(oc + 1) * 128, ts4 * 512:ts4 * 512 + 512],
                in_=yvC[:, par * 512:(par + 1) * 512]),
                waits=[("oc_ev", p.n("oc_ev"))], inc=f"yst{par}", inc_n=16)
            ec += 1
    esC.close()



def attn_weight_handles(p, layer, pref=""):
    W = {}
    if layer == 0:
        W["wq"] = p.dram(pref + "wq", [D, D], F32, kind="ExternalInput")
        wk = p.dram(pref + "wk", [D, D], F32, kind="ExternalInput")
        wv = p.dram(pref + "wv", [D, D], F32, kind="ExternalInput")
        W["wo"] = p.dram(pref + "wo", [D, D], F32, kind="ExternalInput")
        W["ksrc"] = lambda kc: wk[:, kc * 128:(kc + 1) * 128]
        W["vsrc"] = lambda vf: wv[:, vf * 256:(vf + 1) * 256]
        W["cs"] = [(p.dram(pref + "cosk0", [64, T], F32, kind="ExternalInput"),
                    p.dram(pref + "sink0", [64, T], F32, kind="ExternalInput")),
                   (p.dram(pref + "cosk1", [64, T], F32, kind="ExternalInput"),
                    p.dram(pref + "sink1", [64, T], F32, kind="ExternalInput")),
                   (p.dram(pref + "cosq", [64, T], F32, kind="ExternalInput"),
                    p.dram(pref + "sinq", [64, T], F32, kind="ExternalInput"))]
    else:
        W["wq"] = p.dram(pref + "wq", [D, D], F32, kind="ExternalInput")
        wkv = p.dram(pref + "wkv", [D, 1024], F32, kind="ExternalInput")
        W["wo"] = p.dram(pref + "wo", [D, D], F32, kind="ExternalInput")
        W["boR"] = p.dram(pref + "boR", [128, DC], F32, kind="ExternalInput")
        W["baT"] = p.dram(pref + "baT", [NH, 2, 6, T], BF16, kind="ExternalInput")
        W["bbT"] = p.dram(pref + "bbT", [NH, 6, T], BF16, kind="ExternalInput")
        W["ksrc"] = lambda kc: wkv[:, kc * 128:(kc + 1) * 128]
        W["vsrc"] = lambda vf: wkv[:, 512 + vf * 256:512 + (vf + 1) * 256]
    W["maskpat"] = p.dram(pref + "maskpat", [2, MPS, 128, QB], BF16, kind="ExternalInput")
    return W


def ffn_weight_handles(p, layer, pref=""):
    if layer == 0:
        return {
            "gwT": p.dram(pref + "gwT", [D, HID], I32, kind="ExternalInput"),
            "uwT": p.dram(pref + "uwT", [D, HID], I32, kind="ExternalInput"),
            "dwT": p.dram(pref + "dwT", [HID, D], I32, kind="ExternalInput"),
            "sg": p.dram(pref + "sg", [128, NJC], F32, kind="ExternalInput"),
            "su": p.dram(pref + "su", [128, NJC], F32, kind="ExternalInput"),
            "sd": p.dram(pref + "sd", [128, DC], F32, kind="ExternalInput"),
        }
    return {
        "w1T": p.dram(pref + "w1T", [D, HID], F32, kind="ExternalInput"),
        "w2T": p.dram(pref + "w2T", [D, HID], F32, kind="ExternalInput"),
        "w3T": p.dram(pref + "w3T", [HID, D], F32, kind="ExternalInput"),
        "w1q": p.dram(pref + "w1q", [D, HID], BF16),
        "w2q": p.dram(pref + "w2q", [D, HID], BF16),
        "w3q": p.dram(pref + "w3q", [HID, D], BF16),
    }


def build_ffn(layer):
    p = K(f"ffn{layer}")
    xT = p.dram("xT", [D, T], F32, kind="ExternalInput")
    yT = p.dram("yT", [D, T], F32, kind="ExternalOutput")
    W = ffn_weight_handles(p, layer)
    pb, pbquad = p.psum_pool()
    p.constf(EPS_LN)
    S = emit_scales_l0(p, W) if layer == 0 else None
    if layer == 1:
        emit_quant_body(p, pb, [(W["w1T"], W["w1q"], D, HID),
                                (W["w2T"], W["w2q"], D, HID),
                                (W["w3T"], W["w3q"], HID, D)])
        p.fence_all()
    emit_ffn_body(p, layer, xT, yT, W, pb, S)
    dma_sems = [s for s in p.ctr if s.startswith(("sst", "wld", "wdl", "xload",
                                                  "qld", "qst", "sload"))]
    p.wait("gpsimd", [(s, p.n(s)) for s in dma_sems])
    return p


def build_attn(layer):
    p = K(f"att{layer}")
    xT_own = p.dram("xT_own", [D, T], F32, kind="ExternalInput")
    yT = p.dram("yT", [D, T], F32, kind="ExternalOutput")
    W = attn_weight_handles(p, layer)
    W["xT_p0"] = p.dram("xT_p0", [D, T], F32, kind="ExternalInput")
    W["xT_p1"] = p.dram("xT_p1", [D, T], F32, kind="ExternalInput")
    pb, pbquad = p.psum_pool()
    p.constf(EPS_LN)
    emit_attn_body(p, layer, xT_own, yT, W, (pb, pbquad), fused=False)
    dma_sems = [s for s in p.ctr if s.startswith(("yst", "xrl", "wold", "cld"))]
    p.wait("gpsimd", [(s, p.n(s)) for s in dma_sems])
    return p


def build_mega():
    p = K("mega")
    xT = p.dram("xT", [D, T], F32, kind="ExternalInput")
    yT = p.dram("yT", [D, T], F32, kind="ExternalOutput")
    Wf0 = ffn_weight_handles(p, 0, "f0_")
    Wf1 = ffn_weight_handles(p, 1, "f1_")
    Wa0 = attn_weight_handles(p, 0, "a0_")
    Wa1 = attn_weight_handles(p, 1, "a1_")
    xA = p.dram("xA", [D, T], F32)
    xB = p.dram("xB", [D, T], F32)
    xC = p.dram("xC", [D, T], F32)
    xD = p.dram("xD", [D, T], F32)
    xE = p.dram("xE", [D, T], F32)
    pb, pbquad = p.psum_pool()
    p.constf(EPS_LN)
    S0 = emit_scales_l0(p, Wf0)
    emit_quant_body(p, pb, [(Wf1["w1T"], Wf1["w1q"], D, HID),
                            (Wf1["w2T"], Wf1["w2q"], D, HID),
                            (Wf1["w3T"], Wf1["w3q"], HID, D)])
    p.fence_all()
    emit_ffn_body(p, 0, xT, xA, Wf0, pb, S0)
    p.fence_all()
    emit_attn_body(p, 0, xA, xB, Wa0, (pb, pbquad), fused=True, cc_tag="a0")
    p.fence_all()
    emit_ffn_body(p, 0, xB, xC, Wf0, pb, S0)
    p.fence_all()
    emit_ffn_body(p, 1, xC, xD, Wf1, pb)
    p.fence_all()
    emit_attn_body(p, 1, xD, xE, Wa1, (pb, pbquad), fused=True, cc_tag="a1")
    p.fence_all()
    emit_ffn_body(p, 1, xE, yT, Wf1, pb)
    p.fence_all()
    return p



def shard_x(x):
    out = []
    for c in range(N_CORES):
        b, par = c // 2, c % 2
        out.append(np.ascontiguousarray(x[b, par::2, :].T))
    return out


def unshard_x(shards):
    x = np.empty((4, 4096, 2048), np.float32)
    for c in range(N_CORES):
        b, par = c // 2, c % 2
        x[b, par::2, :] = shards[c].T
    return x


def ffn_inputs(layer, inp, pref=""):
    if layer == 0:
        return {
            pref + "gwT": np.ascontiguousarray(np.asarray(inp["L0_ffn_wq_gate"]).T),
            pref + "uwT": np.ascontiguousarray(np.asarray(inp["L0_ffn_wq_up"]).T),
            pref + "dwT": np.ascontiguousarray(np.asarray(inp["L0_ffn_wq_down"]).T),
            pref + "sg": np.ascontiguousarray(np.asarray(inp["L0_ffn_s_gate"]).reshape(NJC, 128).T),
            pref + "su": np.ascontiguousarray(np.asarray(inp["L0_ffn_s_up"]).reshape(NJC, 128).T),
            pref + "sd": np.ascontiguousarray(np.asarray(inp["L0_ffn_s_down"]).reshape(DC, 128).T),
        }
    return {
        pref + "w1T": np.ascontiguousarray(np.asarray(inp["L1_ffn_w1"]).T),
        pref + "w2T": np.ascontiguousarray(np.asarray(inp["L1_ffn_w2"]).T),
        pref + "w3T": np.ascontiguousarray(np.asarray(inp["L1_ffn_w3"]).T),
    }


def _split3(a):
    a = a.astype(np.float32)
    a0 = a.astype(BF).astype(np.float32)
    a1 = (a - a0).astype(BF).astype(np.float32)
    a2 = (a - a0 - a1).astype(BF)
    return np.stack([a0.astype(BF), a1.astype(BF), a2], 0)


def _alibi_slopes(n):
    start = 2.0 ** (-2.0 ** (-(np.log2(n) - 3.0)))
    return np.array([start ** (i + 1) for i in range(n)], np.float64)


def attn_inputs(layer, inp, par, pref=""):
    out = {}
    if layer == 0:
        out[pref + "wq"] = np.ascontiguousarray(np.asarray(inp["L0_attn_Wq"]).T)
        out[pref + "wk"] = np.ascontiguousarray(np.asarray(inp["L0_attn_Wk"]).T)
        out[pref + "wv"] = np.ascontiguousarray(np.asarray(inp["L0_attn_Wv"]).T)
        out[pref + "wo"] = np.ascontiguousarray(np.asarray(inp["L0_attn_Wo"]).T)
        inv_freq = 1.0 / 10000.0 ** (np.arange(0, 128, 2, dtype=np.float32) / 128.0)
        tpos = np.arange(4096, dtype=np.float32)
        fr = np.outer(tpos, inv_freq)
        cos = np.cos(fr).astype(np.float32)
        sin = np.sin(fr).astype(np.float32)
        out[pref + "cosk0"] = np.ascontiguousarray(cos[0::2].T)
        out[pref + "sink0"] = np.ascontiguousarray(sin[0::2].T)
        out[pref + "cosk1"] = np.ascontiguousarray(cos[1::2].T)
        out[pref + "sink1"] = np.ascontiguousarray(sin[1::2].T)
        out[pref + "cosq"] = np.ascontiguousarray(cos[par::2].T)
        out[pref + "sinq"] = np.ascontiguousarray(sin[par::2].T)
    else:
        out[pref + "wq"] = np.ascontiguousarray(np.asarray(inp["L1_attn_Wq"]))
        out[pref + "wkv"] = np.ascontiguousarray(np.asarray(inp["L1_attn_Wkv"]))
        out[pref + "wo"] = np.ascontiguousarray(np.asarray(inp["L1_attn_Wo"]))
        out[pref + "boR"] = np.ascontiguousarray(
            np.asarray(inp["L1_attn_bo"]).reshape(DC, 128).T.astype(np.float32))
        sl = _alibi_slopes(NH)
        S128 = np.sqrt(128.0)
        l = np.arange(T, dtype=np.float64)
        baT = np.zeros((NH, 2, 6, T), BF)
        bbT = np.zeros((NH, 6, T), BF)
        for h in range(NH):
            for r in range(2):
                A = S128 * sl[h] * (2 * l + r)
                baT[h, r, 0:3] = _split3(A)
                baT[h, r, 3:6] = np.ones((3, T), BF)
            Bv = -S128 * sl[h] * (2 * l + par)
            bbT[h, 0:3] = np.ones((3, T), BF)
            bbT[h, 3:6] = _split3(Bv)
        out[pref + "baT"] = baT
        out[pref + "bbT"] = bbT
    kk = np.arange(128)[:, None]
    qq = np.arange(QB)[None, :]
    mp = np.zeros((2, MPS, 128, QB), BF)
    for r in range(2):
        eps = -1 if (par == 0 and r == 1) else 0
        for d in range(MPS):
            valid = kk <= (qq - 128 * d + eps)
            mp[r, d] = np.where(valid, np.float32(1e30), np.float32(-1e30)).astype(BF)
    out[pref + "maskpat"] = mp
    return out


_prog_cache = {}


def get_prog(kind, layer=0):
    key = (kind, layer)
    if key not in _prog_cache:
        if kind == "ffn":
            _prog_cache[key] = build_ffn(layer).finish()
        elif kind == "att":
            _prog_cache[key] = build_attn(layer).finish()
        elif kind == "mega":
            _prog_cache[key] = build_mega().finish()
        else:
            raise KeyError(kind)
    return _prog_cache[key]


def run_ffn(layer, x_shards, inp):
    nc = get_prog("ffn", layer)
    wmap = ffn_inputs(layer, inp)
    in_maps = [dict(wmap, xT=xs) for xs in x_shards]
    res = run_bass_kernel_spmd(nc, in_maps, core_ids=list(range(N_CORES)))
    return [np.asarray(res.results[c]["yT"]) for c in range(N_CORES)]


def run_attn(layer, x_shards, inp):
    nc = get_prog("att", layer)
    wmaps = [attn_inputs(layer, inp, 0), attn_inputs(layer, inp, 1)]
    in_maps = []
    for c in range(N_CORES):
        b, par = c // 2, c % 2
        in_maps.append(dict(wmaps[par], xT_own=x_shards[c],
                            xT_p0=x_shards[2 * b], xT_p1=x_shards[2 * b + 1]))
    res = run_bass_kernel_spmd(nc, in_maps, core_ids=list(range(N_CORES)))
    return [np.asarray(res.results[c]["yT"]) for c in range(N_CORES)]


def kernel(**inputs):
    inputs = {k: np.asarray(v) for k, v in inputs.items()}
    nc = get_prog("mega")
    base = {}
    base.update(ffn_inputs(0, inputs, "f0_"))
    base.update(ffn_inputs(1, inputs, "f1_"))
    wa = [dict(attn_inputs(0, inputs, par, "a0_"),
               **attn_inputs(1, inputs, par, "a1_")) for par in range(2)]
    x_shards = shard_x(inputs["L0_attn_hidden_states"])
    in_maps = []
    for c in range(N_CORES):
        par = c % 2
        m = dict(base)
        m.update(wa[par])
        m["xT"] = x_shards[c]
        in_maps.append(m)
    res = run_bass_kernel_spmd(nc, in_maps, core_ids=list(range(N_CORES)))
    return unshard_x([np.asarray(res.results[c]["yT"]) for c in range(N_CORES)])


def kernel_unfused(**inputs):
    inputs = {k: np.asarray(v) for k, v in inputs.items()}
    x = shard_x(inputs["L0_attn_hidden_states"])
    x = run_ffn(0, x, inputs)
    x = run_attn(0, x, inputs)
    x = run_ffn(0, x, inputs)
    x = run_ffn(1, x, inputs)
    x = run_attn(1, x, inputs)
    x = run_ffn(1, x, inputs)
    return unshard_x(x)


# version 06
# speedup vs baseline: 1.1485x; 1.1485x over previous
import numpy as np
import ml_dtypes
from contextlib import ExitStack
import concourse.bass as bass
import concourse.mybir as mybir
from concourse.bass_utils import run_bass_kernel_spmd

F32 = mybir.dt.float32
BF16 = mybir.dt.bfloat16
I32 = mybir.dt.int32
AF = mybir.ActivationFunctionType
ALU = mybir.AluOpType
AX = mybir.AxisListType
BF = ml_dtypes.bfloat16

N_CORES = 8
D = 2048
T = 2048
HID = 5632
DC = D // 128
NH = 16
NJC = HID // 128
EPS_LN = 1e-5
INVD = 1.0 / D
NS = 4
QB = 512
KC = 128
MPS = QB // KC
RSQHD = float(1.0 / np.sqrt(128.0))

ENGINES = ("tensor", "scalar", "vector", "gpsimd", "sync")


class K:
    def __init__(self, name):
        self.nc = bass.Bass("TRN2", target_bir_lowering=False, debug=False)
        self.name = name
        self.q = {e: [] for e in ENGINES}
        self.ctr = {}
        self._semh = {}
        self._waited = {e: {} for e in ENGINES}
        self.es = ExitStack()
        self._uid = 0

    def sem(self, name):
        if name not in self._semh:
            self._semh[name] = self.es.enter_context(self.nc.semaphore(name))
            self.ctr[name] = 0
        return self._semh[name]

    def uname(self, nm):
        self._uid += 1
        return f"{nm}_{self._uid}"

    def sbuf(self, name, shape, dt):
        return self.es.enter_context(self.nc.sbuf_tensor(self.uname(name), list(shape), dt))

    def psum(self, name, shape, dt=F32):
        return self.es.enter_context(self.nc.psum_tensor(self.uname(name), list(shape), dt))

    def psum_pool(self):
        quad = self.psum("pbq", [128, 2048])
        pb = [quad[:, i * 512:(i + 1) * 512] for i in range(4)]
        pb += [self.psum("pb", [128, 512]) for _ in range(4)]
        return pb, quad

    def dram(self, name, shape, dt, kind="Internal"):
        return self.nc.dram_tensor(name, list(shape), dt, kind=kind)

    def op(self, eng, build, waits=(), inc=None, inc_n=1):
        wl = []
        for s, v in waits:
            if v <= 0:
                continue
            assert v <= self.ctr.get(s, 0), \
                f"wait {s}>={v} exceeds planned {self.ctr.get(s, 0)}"
            if self._waited[eng].get(s, -1) >= v:
                continue
            self._waited[eng][s] = v
            wl.append((self.sem(s), v))
        inch = None
        if inc is not None:
            inch = self.sem(inc)
            self.ctr[inc] += inc_n

        def fn(e, wl=wl, build=build, inch=inch, inc_n=inc_n):
            for h, v in wl:
                e.wait_ge(h, v)
            r = build(e)
            if inch is not None:
                r.then_inc(inch, inc_n)
        self.q[eng].append(fn)

    def wait(self, eng, waits):
        self.op(eng, lambda e: None, waits=waits)

    def fence_all(self):
        w = [(s, self.ctr[s]) for s in self.ctr]
        for e in ENGINES:
            self.wait(e, w)

    def constf(self, value):
        if not hasattr(self, "_cstbuf"):
            self._cstbuf = self.sbuf("cst", [1, 16], F32)
            self._csts = {}
        if value not in self._csts:
            i = len(self._csts)
            assert i < 16
            self._csts[value] = i
            ap = self._cstbuf[0:1, i:i + 1]
            self.nc.const_aps.register(F32, value, ap)
            self.op("vector", lambda e, ap=ap, value=value: e.memset(ap, value),
                    inc="init")

    def n(self, semname):
        return self.ctr.get(semname, 0)

    def finish(self):
        with self.nc.Block() as block:
            for e in ENGINES:
                lst = self.q[e]
                if not lst:
                    continue
                dec = getattr(block, e)

                def body(eng, lst=lst):
                    for fn in lst:
                        fn(eng)
                dec(body)
        self.es.close()
        return self.nc


def _r(ap, spec, **kw):
    return ap.rearrange(spec, **kw)



def emit_scales_l0(p, W):
    sgs = p.sbuf("sgs", [128, NJC], F32)
    sus = p.sbuf("sus", [128, NJC], F32)
    sds = p.sbuf("sds", [128, DC], F32)
    p.op("sync", lambda e: e.dma_start(out=sgs[:, :], in_=W["sg"][:, :]), inc="sload", inc_n=16)
    p.op("sync", lambda e: e.dma_start(out=sus[:, :], in_=W["su"][:, :]), inc="sload", inc_n=16)
    p.op("sync", lambda e: e.dma_start(out=sds[:, :], in_=W["sd"][:, :]), inc="sload", inc_n=16)
    p.op("vector", lambda e: e.tensor_scalar(out=sds[:, :], in0=sds[:, :],
                                             scalar1=0.5, scalar2=None, op0=ALU.mult),
         waits=[("sload", p.n("sload"))], inc="init")
    return {"sgs": sgs, "sus": sus, "sds": sds,
            "ready": [("sload", p.n("sload")), ("init", p.n("init"))]}



def emit_quant_body(p, pb, specs):
    es2 = ExitStack()

    def sb(nm, shape, dt):
        return es2.enter_context(p.nc.sbuf_tensor(p.uname(nm), list(shape), dt))
    qw = sb("qw", [128, 2 * 2816], F32)
    qp = sb("qp", [128, 2816], BF16)
    qn = sb("qn", [128, 2816], BF16)
    qo = sb("qo", [128, 2 * 2816], BF16)
    strips = sb("strips", [128, 64], F32)
    onef = sb("onef", [128, 1], F32)
    onefr = sb("onefr", [1, 128], F32)
    gval = sb("gval", [128, 4], F32)
    p.op("vector", lambda e: e.memset(onef[:, :], 1.0))
    p.op("vector", lambda e: e.memset(onefr[:, :], 1.0), inc="qrow")
    base_qbc = p.n("qbc")

    def tiles_of(rows, cols):
        w = 2816 if cols == HID else 2048
        return [(rc * 128, cc * w, w) for rc in range(rows // 128)
                for cc in range(cols // w)]

    for mi, (src, dst, rows, cols) in enumerate(specs):
        tl = tiles_of(rows, cols)
        nt = len(tl)
        base_abs = p.n("qabs")
        for i, (r0, c0, w) in enumerate(tl):
            sl = i % 2
            qeng = "scalar" if sl == 0 else "sync"
            qsem = "qlA" if sl == 0 else "qlS"
            p.op(qeng, lambda e, src=src, r0=r0, c0=c0, w=w, sl=sl: e.dma_start(
                out=qw[:, sl * 2816:sl * 2816 + w], in_=src[r0:r0 + 128, c0:c0 + w]),
                waits=[("qabs", base_abs + i - 1), ("qbc", base_qbc + mi),
                       ("qdv", p.n("qdv"))],
                inc=qsem, inc_n=16)
            p.op("scalar", lambda e, w=w, sl=sl, i=i: e.activation(
                out=qw[:, sl * 2816:sl * 2816 + w], in_=qw[:, sl * 2816:sl * 2816 + w],
                func=AF.Abs, bias=0.0, scale=1.0, accum_out=strips[:, i:i + 1]),
                waits=[(qsem, p.n(qsem))], inc="qabs")
        p.op("vector", lambda e, nt=nt: e.tensor_reduce(
            out=strips[:, 0:1], in_=strips[:, 0:nt], axis=AX.X, op=ALU.add),
            waits=[("qabs", base_abs + nt)], inc="qrow")
        p.op("tensor", lambda e: e.matmul(pb[7][0:1, 0:1], strips[:, 0:1], onef[:, :],
             start=True, stop=True), waits=[("qrow", p.n("qrow"))], inc="qmm")
        nelem = float(rows * cols)
        p.op("vector", lambda e, nelem=nelem: e.tensor_scalar(
            out=gval[0:1, 0:1], in0=pb[7][0:1, 0:1],
            scalar1=0.5 / nelem, scalar2=0.5e-8, op0=ALU.mult, op1=ALU.add),
            waits=[("qmm", p.n("qmm"))], inc="qrow")
        p.op("tensor", lambda e: e.matmul(pb[6][:, 0:1], onefr[:, :], gval[0:1, 0:1],
             start=True, stop=True), waits=[("qrow", p.n("qrow"))], inc="qmm")
        p.op("vector", lambda e: e.tensor_copy(out=gval[:, 0:1], in_=pb[6][:, 0:1]),
             waits=[("qmm", p.n("qmm"))])
        p.op("vector", lambda e: e.tensor_scalar(
            out=gval[:, 1:2], in0=pb[6][:, 0:1], scalar1=-1.0, scalar2=None,
            op0=ALU.mult), inc="qbc")
        base_dv = p.n("qdv")
        for i, (r0, c0, w) in enumerate(tl):
            sl = i % 2
            qeng = "scalar" if sl == 0 else "sync"
            qsem = "qlA" if sl == 0 else "qlS"
            p.op(qeng, lambda e, src=src, r0=r0, c0=c0, w=w, sl=sl: e.dma_start(
                out=qw[:, sl * 2816:sl * 2816 + w], in_=src[r0:r0 + 128, c0:c0 + w]),
                waits=[("qbc", base_qbc + mi + 1), ("qdv", base_dv + 3 * (i - 1))],
                inc=qsem, inc_n=16)
            ld = p.n(qsem)
            p.op("vector", lambda e, w=w, sl=sl: e.tensor_scalar(
                out=qp[:, 0:w], in0=qw[:, sl * 2816:sl * 2816 + w],
                scalar1=gval[:, 0:1], scalar2=None, op0=ALU.is_ge),
                waits=[(qsem, ld)], inc="qdv")
            p.op("vector", lambda e, w=w, sl=sl: e.tensor_scalar(
                out=qn[:, 0:w], in0=qw[:, sl * 2816:sl * 2816 + w],
                scalar1=gval[:, 1:2], scalar2=None, op0=ALU.is_le), inc="qdv")
            p.op("vector", lambda e, w=w, sl=sl: e.tensor_tensor(
                out=qo[:, sl * 2816:sl * 2816 + w], in0=qp[:, 0:w], in1=qn[:, 0:w],
                op=ALU.subtract),
                waits=[(f"qst{sl}", p.n(f"qst{sl}"))], inc="qdv")
            p.op("gpsimd", lambda e, dst=dst, r0=r0, c0=c0, w=w, sl=sl: e.dma_start(
                out=dst[r0:r0 + 128, c0:c0 + w], in_=qo[:, sl * 2816:sl * 2816 + w]),
                waits=[("qdv", p.n("qdv"))], inc=f"qst{sl}", inc_n=16)
    p.wait("sync", [("qst0", p.n("qst0")), ("qst1", p.n("qst1"))])
    es2.close()



def emit_ffn_body(p, layer, xT, yT, W, pb, S=None):
    es = ExitStack()

    def sb(nm, shape, dt):
        return es.enter_context(p.nc.sbuf_tensor(p.uname(nm), list(shape), dt))
    xf = sb("xf", [128, 16 * 512], F32)
    xs = sb("xs", [128, 2 * 512], BF16)
    xtmp = sb("xtmp", [128, 512], F32)
    xq = xtmp[:, :].bitcast(BF16)
    xl = sb("xl", [128, DC * 1024], BF16)
    hh = sb("hh", [128, NJC * 1024], BF16)
    wgb = sb("wgb", [128, 2 * DC * 256], BF16)
    wub = sb("wub", [128, 2 * DC * 256], BF16)
    wdb = sb("wdb", [128, 2 * 4 * 512], BF16)
    wdbf = wdb[:, :].bitcast(F32)
    gsb = sb("gsb", [128, 2 * 512], BF16)
    usb = sb("usb", [128, 2 * 512], BF16)
    xsf = xs[:, :].bitcast(F32)
    yv = xf[:, 8 * 512:10 * 512]
    rows = sb("rows", [1, 2 * 512], F32)
    brow = sb("brow", [1, 2 * 512], BF16)
    onec = sb("onec", [128, 1], BF16)
    oner = sb("oner", [1, 128], BF16)

    if layer == 0:
        wsrc_g, wsrc_u, wsrc_d = W["gwT"], W["uwT"], W["dwT"]
        wq_eng = "gpsimd"
        sgs, sus, sds = S["sgs"], S["sus"], S["sds"]
        s_ready = S["ready"]
    else:
        wsrc_g, wsrc_u, wsrc_d = W["w1q"], W["w2q"], W["w3q"]
        wq_eng = "sync"
        s_ready = []

    p.op("vector", lambda e: e.memset(onec[:, :], 1.0))
    p.op("vector", lambda e: e.memset(oner[:, :], 1.0), inc="init")

    b_dn = p.n("rel_dn")
    b_gmm = p.n("gu_mm")
    b_dmm = p.n("dn_mm")
    b_hok = p.n("h_ok")
    ln_ts = [0, 0]

    for tb in range(2):
        t0 = tb * 1024
        for tsub in range(2):
            ts4 = tb * 2 + tsub
            tc0 = ts4 * 512
            base_mm = p.n("ln_mm")
            base_row = p.n("ln_row")
            base_act = p.n("ln_act")
            base_bc = p.n("ln_bc")
            p.op("sync", lambda e, tc0=tc0: e.dma_start(
                out=_r(xf[:, :], "p (c t) -> p c t", t=512),
                in_=_r(xT[:, tc0:tc0 + 512], "(c p) t -> p c t", p=128)),
                waits=[("ln_ap", p.n("ln_ap")), ("rel_dn", b_dn + 32 * tb),
                       ("sst0", p.n("sst0")), ("sst1", p.n("sst1"))],
                inc="xload", inc_n=16)
            xld = p.n("xload")
            for dc in range(DC):
                sl = dc % 2
                p.op("vector", lambda e, dc=dc, sl=sl: e.tensor_copy(
                    out=xq[:, sl * 512:(sl + 1) * 512], in_=xf[:, dc * 512:(dc + 1) * 512]),
                    waits=[("xload", xld), ("ln_mm", base_mm + dc - 1)], inc="ln_xq")
                p.op("scalar", lambda e, dc=dc, sl=sl: e.activation(
                    out=xs[:, sl * 512:(sl + 1) * 512], in_=xf[:, dc * 512:(dc + 1) * 512],
                    func=AF.Square, bias=0.0, scale=1.0),
                    waits=[("xload", xld), ("ln_mm", base_mm + dc - 1)], inc="ln_xs")
                w0 = [("ln_xq", p.n("ln_xq")), ("ln_xs", p.n("ln_xs"))]
                if dc == 0:
                    w0 += [("ln_row", base_row), ("rel_dn", b_dn + 32 * tb),
                           ("init", p.n("init"))]
                p.op("tensor", lambda e, sl=sl, dc=dc: e.matmul(
                    pb[0][0:1, :], onec[:, :], xq[:, sl * 512:(sl + 1) * 512],
                    start=(dc == 0), stop=(dc == DC - 1)), waits=w0)
                p.op("tensor", lambda e, sl=sl, dc=dc: e.matmul(
                    pb[1][0:1, :], onec[:, :], xs[:, sl * 512:(sl + 1) * 512],
                    start=(dc == 0), stop=(dc == DC - 1)), inc="ln_mm")
            m_, msq = rows[0:1, 0:512], rows[0:1, 512:1024]
            var, lnv = rows[0:1, 0:512], rows[0:1, 512:1024]
            bm, br = brow[0:1, 0:512], brow[0:1, 512:1024]
            p.op("vector", lambda e: e.tensor_scalar(out=m_, in0=pb[0][0:1, :],
                 scalar1=INVD, scalar2=None, op0=ALU.mult),
                 waits=[("ln_mm", base_mm + DC), ("ln_act", p.n("ln_act"))])
            p.op("vector", lambda e: e.tensor_copy(out=bm, in_=m_), inc="ln_row")
            p.op("vector", lambda e: e.tensor_tensor(out=msq, in0=m_, in1=m_, op=ALU.mult))
            p.op("vector", lambda e: e.scalar_tensor_tensor(out=var, in0=pb[1][0:1, :],
                 scalar=INVD, in1=msq, op0=ALU.mult, op1=ALU.subtract), inc="ln_row")
            p.op("scalar", lambda e: e.activation(out=lnv, in_=var, func=AF.Ln,
                 bias=EPS_LN, scale=1.0),
                 waits=[("ln_row", base_row + 2), ("init", 1)])
            p.op("scalar", lambda e: e.activation(out=br, in_=lnv, func=AF.Exp,
                 bias=0.0, scale=-0.5), inc="ln_act")
            p.op("tensor", lambda e: e.matmul(pb[2][:, :], oner[:, :], bm, start=True, stop=True),
                 waits=[("ln_row", base_row + 2), ("ln_ap", p.n("ln_ap"))], inc="ln_bc")
            p.op("tensor", lambda e: e.matmul(pb[3][:, :], oner[:, :], br, start=True, stop=True),
                 waits=[("ln_act", base_act + 1)], inc="ln_bc")
            mb_sb = xsf[:, 0:512]
            rb_sb = wdbf[:, 0:512]
            pslot = [xtmp[:, :], wdbf[:, 512:1024]]
            p.op("vector", lambda e: e.tensor_copy(out=mb_sb, in_=pb[2][:, :]),
                 waits=[("ln_bc", base_bc + 2)])
            p.op("vector", lambda e: e.tensor_copy(out=rb_sb, in_=pb[3][:, :]),
                 inc="ln_ev")
            base_pl = p.n("ln_pl")
            evv = p.n("ln_ev")
            for dc in range(DC):
                p.op("gpsimd", lambda e, dc=dc, sl=pslot[dc % 2]: e.tensor_tensor(
                    out=sl, in0=xf[:, dc * 512:(dc + 1) * 512], in1=mb_sb,
                    op=ALU.subtract),
                    waits=[("ln_ev", evv), ("ln_ap", p.n("ln_ap") - 1),
                           ("gu_mm", b_gmm + 44 * tb)], inc="ln_pl")
                p.op("vector", lambda e, dc=dc, sl=pslot[dc % 2], off=dc * 1024 + tsub * 512: e.tensor_tensor(
                    out=xl[:, off:off + 512], in0=sl, in1=rb_sb,
                    op=ALU.mult), waits=[("ln_pl", base_pl + dc + 1)], inc="ln_ap")
            ln_ts[ts4 % 2] = p.n("ln_ap")
            p.wait("tensor", [("ln_ap", p.n("ln_ap"))])

        base_gu = p.n("gu_mm")
        base_gug = p.n("gu_g")
        base_rg = p.n("rel_g")
        base_ru = p.n("rel_u")
        base_wf = p.n("wfix")
        base_h = p.n("h_ok")
        ln_done = p.n("ln_ap")
        for jp in range(NJC // 2):
          buf = jp % 2
          wsem = f"wld{buf}"
          woff = buf * DC * 256
          c0 = jp * 256
          p.op(wq_eng, lambda e, c0=c0, woff=woff: e.dma_start(
              out=_r(wgb[:, woff:woff + DC * 256], "p (c j) -> p c j", j=256),
              in_=_r(wsrc_g[:, c0:c0 + 256], "(c p) j -> p c j", p=128)),
              waits=[("gu_mm", max(base_gu + 2 * jp - 2, base_gu))], inc=wsem, inc_n=16)
          p.op(wq_eng, lambda e, c0=c0, woff=woff: e.dma_start(
              out=_r(wub[:, woff:woff + DC * 256], "p (c j) -> p c j", j=256),
              in_=_r(wsrc_u[:, c0:c0 + 256], "(c p) j -> p c j", p=128)),
              inc=wsem, inc_n=16)
          wld = p.n(wsem)
          if layer == 0:
              p.op("vector", lambda e, woff=woff: e.tensor_scalar(
                  out=wgb[:, woff:woff + DC * 256], in0=wgb[:, woff:woff + DC * 256],
                  scalar1=-128.0, scalar2=None, op0=ALU.add),
                  waits=[(wsem, wld)], inc="wfix")
              p.op("vector", lambda e, woff=woff: e.tensor_scalar(
                  out=wub[:, woff:woff + DC * 256], in0=wub[:, woff:woff + DC * 256],
                  scalar1=-128.0, scalar2=None, op0=ALU.add), inc="wfix")
          for jcl in range(2):
            jc = 2 * jp + jcl
            jb = jc % 2
            gb = [pb[0 + 4 * jb], pb[1 + 4 * jb]]
            ub = [pb[2 + 4 * jb], pb[3 + 4 * jb]]
            w0 = [("rel_g", base_rg + 2 * (jc - 1)),
                  ("rel_u", base_ru + 2 * (jc - 1))]
            if layer == 0:
                w0.append(("wfix", base_wf + 2 * (jp + 1)))
            else:
                w0.append((wsem, wld))
            if jc == 0:
                w0.append(("rel_dn", b_dn + 32 * tb))
            for which, banks, wbuf in (("g", gb, wgb), ("u", ub, wub)):
                for tsub in range(2):
                    w0t = w0 + [("ln_ap", ln_ts[tsub])]
                    for dc in range(DC):
                        p.op("tensor", lambda e, banks=banks, tsub=tsub, dc=dc, wbuf=wbuf, woff=woff, jcl=jcl: e.matmul(
                            banks[tsub][:, :],
                            wbuf[:, woff + dc * 256 + jcl * 128:woff + dc * 256 + jcl * 128 + 128],
                            xl[:, dc * 1024 + tsub * 512:dc * 1024 + tsub * 512 + 512],
                            start=(dc == 0), stop=(dc == DC - 1)),
                            waits=(w0t if dc == 0 else ()),
                            inc=("gu_g" if (which == "g" and dc == DC - 1) else
                                 ("gu_mm" if (which == "u" and tsub == 1 and dc == DC - 1) else None)))
            for tsub in range(2):
                gslot = gsb[:, tsub * 512:(tsub + 1) * 512]
                uslot = usb[:, tsub * 512:(tsub + 1) * 512]
                if layer == 0:
                    p.op("scalar", lambda e, gslot=gslot, b=gb[tsub], jc=jc: e.activation(
                        out=gslot, in_=b[:, :], func=AF.Silu, bias=0.0,
                        scale=sgs[:, jc:jc + 1]),
                        waits=[("gu_g", base_gug + 2 * jc + tsub + 1),
                               ("h_ok", base_h + 2 * jc + tsub - 1)] + s_ready,
                        inc="rel_g")
                    p.op("vector", lambda e, uslot=uslot, b=ub[tsub], jc=jc: e.tensor_scalar(
                        out=uslot, in0=b[:, :], scalar1=sus[:, jc:jc + 1], scalar2=None,
                        op0=ALU.mult), waits=[("gu_mm", base_gu + jc + 1)], inc="rel_u")
                else:
                    p.op("scalar", lambda e, gslot=gslot, b=gb[tsub]: e.activation(
                        out=gslot, in_=b[:, :], func=AF.Sigmoid, bias=0.0, scale=1.0),
                        waits=[("gu_g", base_gug + 2 * jc + tsub + 1),
                               ("h_ok", base_h + 2 * jc + tsub - 1)], inc="rel_g")
                    p.op("vector", lambda e, uslot=uslot, b=ub[tsub]: e.tensor_copy(
                        out=uslot, in_=b[:, :]),
                        waits=[("gu_mm", base_gu + jc + 1)], inc="rel_u")
                p.op("vector", lambda e, gslot=gslot, uslot=uslot, off=jc * 1024 + tsub * 512: e.tensor_tensor(
                    out=hh[:, off:off + 512], in0=gslot, in1=uslot, op=ALU.mult),
                    waits=[("rel_g", base_rg + 2 * jc + tsub + 1),
                           ("dn_mm", b_dmm + 176 * tb)], inc="h_ok")

        for ocg in range(4):
            base_dn = p.n("dn_mm")
            for i in range(8):
                oc = ocg * 4 + i // 2
                tss = i % 2
                p.op("sync", lambda e, oc=oc, tss=tss, i=i, t0=t0: e.dma_start(
                    out=xf[:, i * 512:(i + 1) * 512],
                    in_=xT[oc * 128:(oc + 1) * 128, t0 + tss * 512:t0 + tss * 512 + 512]),
                    waits=[("ln_ap", ln_done), ("rel_dn", b_dn + 8 * (4 * tb + ocg))],
                    inc="xload", inc_n=16)
            xld = p.n("xload")
            for jg in range(NJC // 4):
              buf = jg % 2
              wsem = f"wdl{buf}"
              c0 = ocg * 512
              p.op(wq_eng, lambda e, jg=jg, c0=c0, buf=buf: e.dma_start(
                  out=_r(wdb[:, buf * 2048:(buf + 1) * 2048], "p (j c) -> p j c", c=512),
                  in_=_r(wsrc_d[jg * 512:(jg + 1) * 512, c0:c0 + 512], "(j p) c -> p j c", p=128)),
                  waits=[("dn_mm", max(base_dn + 4 * jg - 4, base_dn)),
                         ("ln_ap", ln_done)], inc=wsem, inc_n=16)
              wld = p.n(wsem)
              if layer == 0:
                  p.op("vector", lambda e, buf=buf: e.tensor_scalar(
                      out=wdb[:, buf * 2048:(buf + 1) * 2048],
                      in0=wdb[:, buf * 2048:(buf + 1) * 2048],
                      scalar1=-128.0, scalar2=None, op0=ALU.add),
                      waits=[(wsem, wld)], inc="wdfix")
                  wfx = p.n("wdfix")
              for jcl in range(4):
                jc = 4 * jg + jcl
                if layer == 0:
                    w0 = [("wdfix", wfx)]
                else:
                    w0 = [(wsem, wld)]
                if jc == 0:
                    w0 = w0 + [("rel_dn", b_dn + 8 * (4 * tb + ocg)),
                               ("h_ok", b_hok + 88 * (tb + 1))]
                for oc4 in range(4):
                    for tss in range(2):
                        p.op("tensor", lambda e, oc4=oc4, tss=tss, jc=jc, buf=buf, jcl=jcl: e.matmul(
                            pb[oc4 * 2 + tss][:, :],
                            wdb[:, buf * 2048 + jcl * 512 + oc4 * 128:buf * 2048 + jcl * 512 + oc4 * 128 + 128],
                            hh[:, jc * 1024 + tss * 512:jc * 1024 + tss * 512 + 512],
                            start=(jc == 0), stop=(jc == NJC - 1)),
                            waits=(w0 if (oc4 == 0 and tss == 0) else ()),
                            inc=("dn_mm" if (oc4 == 3 and tss == 1) else None))
            mm_dn = p.n("dn_mm")
            for i in range(8):
                oc = ocg * 4 + i // 2
                tss = i % 2
                slot = i % 2
                sa = (lambda oc=oc: sds[:, oc:oc + 1])() if layer == 0 else 0.5
                ssem = f"sst{slot}"
                p.op("vector", lambda e, i=i, slot=slot, sa=sa: e.scalar_tensor_tensor(
                    out=yv[:, slot * 512:(slot + 1) * 512], in0=pb[i][:, :],
                    scalar=sa, in1=xf[:, i * 512:(i + 1) * 512],
                    op0=ALU.mult, op1=ALU.add),
                    waits=[("dn_mm", mm_dn), ("xload", xld), (ssem, p.n(ssem))],
                    inc="rel_dn")
                p.op("scalar", lambda e, oc=oc, tss=tss, slot=slot, t0=t0: e.dma_start(
                    out=yT[oc * 128:(oc + 1) * 128, t0 + tss * 512:t0 + tss * 512 + 512],
                    in_=yv[:, slot * 512:(slot + 1) * 512]),
                    waits=[("rel_dn", p.n("rel_dn"))], inc=ssem, inc_n=16)
    es.close()



def emit_attn_body(p, layer, xT_own, yT, W, pbq, fused, cc_tag=""):
    pb, pbquad = pbq
    nh = NH
    hk = 2048 if layer == 0 else 512
    hkc = hk // 128
    rope = (layer == 0)
    alibi = (layer == 1)
    groups = [[h] for h in range(nh)] if layer == 0 else \
             [[g, g + 4, g + 8, g + 12] for g in range(4)]

    wq, wo = W["wq"], W["wo"]
    maskpat = W["maskpat"]
    ksrc = W["ksrc"]
    vsrc = W["vsrc"]

    if fused:
        kvKin = p.dram(f"kvKin{cc_tag}", [hk, T], BF16)
        kvVin = p.dram(f"kvVin{cc_tag}", [16, hkc, 128, 128], BF16)
        kvKout = p.dram(f"kvKout{cc_tag}", [2, hk, T], BF16)
        kvVout = p.dram(f"kvVout{cc_tag}", [2, 16, hkc, 128, 128], BF16)
    else:
        kvKout = p.dram(f"kvK{cc_tag}", [2, hk, T], BF16)
        kvVout = p.dram(f"kvV{cc_tag}", [2, 16, hkc, 128, 128], BF16)
    ctxd = p.dram(f"ctxd{cc_tag}", [D, T], BF16)

    esQ = ExitStack()
    QT = esQ.enter_context(p.nc.sbuf_tensor(p.uname("QT"), [128, nh * T], BF16))

    esA = ExitStack()

    def sbA(nm, shape, dt):
        return esA.enter_context(p.nc.sbuf_tensor(p.uname(nm), list(shape), dt))
    xf = sbA("xf", [128, 16 * 512], F32)
    xs = sbA("xs", [128, 2 * 512], BF16)
    xtmp = sbA("xtmp", [128, 512], F32)
    xq = xtmp[:, :].bitcast(BF16)
    xl = sbA("xl", [128, DC * T], BF16)
    wkb = sbA("wkb", [128, DC * 128], BF16)
    wvb = sbA("wvb", [128, DC * 256], BF16)
    ksg = sbA("ksg", [128, 2 * 512], BF16)
    vsg = sbA("vsg", [128, 2 * 512], BF16)
    rows = sbA("rows", [1, 2 * 512], F32)
    brow = sbA("brow", [1, 2 * 512], BF16)
    onec = sbA("onecA", [128, 1], BF16)
    oner = sbA("onerA", [1, 128], BF16)
    rta = sbA("rta", [64, 512], F32)
    rtb = sbA("rtb", [64, 512], F32)
    if rope:
        csb = sbA("csb", [64, 2 * T], BF16)

    p.op("vector", lambda e: e.memset(onec[:, :], 1.0))
    p.op("vector", lambda e: e.memset(oner[:, :], 1.0), inc="init")

    def ln_tsub(src, ts4):
        tc0 = ts4 * 512
        base_mm = p.n("ln_mm")
        base_row = p.n("ln_row")
        base_act = p.n("ln_act")
        base_bc = p.n("ln_bc")
        p.op("sync", lambda e, src=src, tc0=tc0: e.dma_start(
            out=_r(xf[:, :], "p (c t) -> p c t", t=512),
            in_=_r(src[:, tc0:tc0 + 512], "(c p) t -> p c t", p=128)),
            waits=[("ln_ap", p.n("ln_ap"))], inc="xload", inc_n=16)
        xld = p.n("xload")
        for dc in range(DC):
            sl = dc % 2
            p.op("vector", lambda e, dc=dc, sl=sl: e.tensor_copy(
                out=xq[:, sl * 512:(sl + 1) * 512], in_=xf[:, dc * 512:(dc + 1) * 512]),
                waits=[("xload", xld), ("ln_mm", base_mm + dc - 1)], inc="ln_xq")
            p.op("scalar", lambda e, dc=dc, sl=sl: e.activation(
                out=xs[:, sl * 512:(sl + 1) * 512], in_=xf[:, dc * 512:(dc + 1) * 512],
                func=AF.Square, bias=0.0, scale=1.0),
                waits=[("xload", xld), ("ln_mm", base_mm + dc - 1)], inc="ln_xs")
            w0 = [("ln_xq", p.n("ln_xq")), ("ln_xs", p.n("ln_xs"))]
            if dc == 0:
                w0 += [("ln_row", base_row), ("init", p.n("init"))]
            p.op("tensor", lambda e, sl=sl, dc=dc: e.matmul(
                pb[0][0:1, :], onec[:, :], xq[:, sl * 512:(sl + 1) * 512],
                start=(dc == 0), stop=(dc == DC - 1)), waits=w0)
            p.op("tensor", lambda e, sl=sl, dc=dc: e.matmul(
                pb[1][0:1, :], onec[:, :], xs[:, sl * 512:(sl + 1) * 512],
                start=(dc == 0), stop=(dc == DC - 1)), inc="ln_mm")
        m_, msq = rows[0:1, 0:512], rows[0:1, 512:1024]
        var, lnv = rows[0:1, 0:512], rows[0:1, 512:1024]
        bm, br = brow[0:1, 0:512], brow[0:1, 512:1024]
        p.op("vector", lambda e: e.tensor_scalar(out=m_, in0=pb[0][0:1, :],
             scalar1=INVD, scalar2=None, op0=ALU.mult),
             waits=[("ln_mm", base_mm + DC), ("ln_act", p.n("ln_act"))])
        p.op("vector", lambda e: e.tensor_copy(out=bm, in_=m_), inc="ln_row")
        p.op("vector", lambda e: e.tensor_tensor(out=msq, in0=m_, in1=m_, op=ALU.mult))
        p.op("vector", lambda e: e.scalar_tensor_tensor(out=var, in0=pb[1][0:1, :],
             scalar=INVD, in1=msq, op0=ALU.mult, op1=ALU.subtract), inc="ln_row")
        p.op("scalar", lambda e: e.activation(out=lnv, in_=var, func=AF.Ln,
             bias=EPS_LN, scale=1.0),
             waits=[("ln_row", base_row + 2), ("init", 1)])
        p.op("scalar", lambda e: e.activation(out=br, in_=lnv, func=AF.Exp,
             bias=0.0, scale=-0.5), inc="ln_act")
        p.op("tensor", lambda e: e.matmul(pb[2][:, :], oner[:, :], bm, start=True, stop=True),
             waits=[("ln_row", base_row + 2), ("ln_ap", p.n("ln_ap"))], inc="ln_bc")
        p.op("tensor", lambda e: e.matmul(pb[3][:, :], oner[:, :], br, start=True, stop=True),
             waits=[("ln_act", base_act + 1)], inc="ln_bc")
        for dc in range(DC):
            p.op("vector", lambda e, dc=dc: e.tensor_tensor(
                out=xtmp[:, :], in0=xf[:, dc * 512:(dc + 1) * 512], in1=pb[2][:, :],
                op=ALU.subtract),
                waits=[("ln_bc", base_bc + 2), ("ap_mm", p.n("ap_mm"))])
            p.op("vector", lambda e, dc=dc, off=dc * T + ts4 * 512: e.tensor_tensor(
                out=xl[:, off:off + 512], in0=xtmp[:, :], in1=pb[3][:, :],
                op=ALU.mult), inc="ln_ap")
        p.wait("tensor", [("ln_ap", p.n("ln_ap"))])

    def rope_evac(bank, csoff, dst_ap_fn, extra_waits, sem_inc):
        C = (lambda: csb[0:64, csoff:csoff + 512])()
        S_ = (lambda: csb[0:64, T + csoff:T + csoff + 512])()
        p.op("vector", lambda e, bank=bank, C=C: e.tensor_tensor(
            out=rta[:, :], in0=bank[0:64, :], in1=C, op=ALU.mult), waits=extra_waits)
        p.op("vector", lambda e, bank=bank, S_=S_: e.tensor_tensor(
            out=rtb[:, :], in0=bank[64:128, :], in1=S_, op=ALU.mult))
        p.op("vector", lambda e: e.tensor_tensor(
            out=dst_ap_fn(0), in0=rta[:, :], in1=rtb[:, :], op=ALU.subtract))
        p.op("vector", lambda e, bank=bank, C=C: e.tensor_tensor(
            out=rta[:, :], in0=bank[64:128, :], in1=C, op=ALU.mult))
        p.op("vector", lambda e, bank=bank, S_=S_: e.tensor_tensor(
            out=rtb[:, :], in0=bank[0:64, :], in1=S_, op=ALU.mult))
        p.op("vector", lambda e: e.tensor_tensor(
            out=dst_ap_fn(1), in0=rta[:, :], in1=rtb[:, :], op=ALU.add), inc=sem_inc)

    def kv_proj(kdst, vdst):
        ln_done = p.n("ln_ap")
        for kc in range(hkc):
            p.op("gpsimd", lambda e, kc=kc: e.dma_start(
                out=_r(wkb[:, :], "p (c j) -> p c j", j=128),
                in_=_r(ksrc(kc), "(c p) j -> p c j", p=128)),
                waits=[("ap_mm", p.n("ap_mm"))], inc="wpld", inc_n=16)
            wld = p.n("wpld")
            for ts4 in range(NS):
                ei = kc * NS + ts4
                bank = pb[4 + ei % 2]
                for dc in range(DC):
                    w0 = []
                    if dc == 0:
                        w0 = [("wpld", wld), ("ln_ap", ln_done),
                              ("ap_ev", p.n("ap_ev") - 1)]
                    p.op("tensor", lambda e, bank=bank, dc=dc, ts4=ts4: e.matmul(
                        bank[:, :], wkb[:, dc * 128:(dc + 1) * 128],
                        xl[:, dc * T + ts4 * 512:dc * T + ts4 * 512 + 512],
                        start=(dc == 0), stop=(dc == DC - 1)),
                        waits=w0, inc=("ap_mm" if dc == DC - 1 else None))
                kpar = ei % 2
                ksl = (lambda kpar=kpar: ksg[:, kpar * 512:(kpar + 1) * 512])()
                ew = [("ap_mm", p.n("ap_mm")), (f"kst{kpar}", p.n(f"kst{kpar}"))]
                if rope:
                    ew.append(("csld", p.n("csld")))
                    rope_evac(bank, ts4 * 512,
                              lambda half, ksl=ksl: ksl[half * 64:(half + 1) * 64, :],
                              ew, "ap_ev")
                else:
                    p.op("vector", lambda e, bank=bank, ksl=ksl: e.tensor_copy(
                        out=ksl, in_=bank[:, :]), waits=ew, inc="ap_ev")
                p.op("scalar", lambda e, kc=kc, ts4=ts4, ksl=ksl, kdst=kdst, kpar=kpar: e.dma_start(
                    out=kdst[kc * 128:(kc + 1) * 128, ts4 * 512:ts4 * 512 + 512],
                    in_=ksl),
                    waits=[("ap_ev", p.n("ap_ev"))], inc=f"kst{kpar}", inc_n=16)
        for vf in range(hk // 256):
            p.op("gpsimd", lambda e, vf=vf: e.dma_start(
                out=_r(wvb[:, :], "p (c j) -> p c j", j=256),
                in_=_r(vsrc(vf), "(c p) j -> p c j", p=128)),
                waits=[("ap_mm", p.n("ap_mm"))], inc="wpld", inc_n=16)
            wld = p.n("wpld")
            for tc in range(16):
                bank = pb[6 + tc % 2]
                for dc in range(DC):
                    w0 = []
                    if dc == 0:
                        w0 = [("wpld", wld), ("ln_ap", ln_done),
                              ("ap_ev2", p.n("ap_ev2") - 1)]
                    p.op("tensor", lambda e, bank=bank, dc=dc, tc=tc: e.matmul(
                        bank[:, 0:256], xl[:, dc * T + tc * 128:dc * T + tc * 128 + 128],
                        wvb[:, dc * 256:(dc + 1) * 256],
                        start=(dc == 0), stop=(dc == DC - 1)),
                        waits=w0, inc=("ap_mm" if dc == DC - 1 else None))
                vpar = tc % 2
                vsl = (lambda vpar=vpar: vsg[:, vpar * 512:vpar * 512 + 256])()
                p.op("vector", lambda e, bank=bank, vsl=vsl: e.tensor_copy(
                    out=vsl, in_=bank[:, 0:256]),
                    waits=[("ap_mm", p.n("ap_mm")), (f"vst{vpar}", p.n(f"vst{vpar}"))],
                    inc="ap_ev2")
                p.op("scalar", lambda e, tc=tc, vf=vf, vsl=vsl, vdst=vdst, vpar=vpar: e.dma_start(
                    out=_r(vdst[tc, vf * 2:(vf + 1) * 2, :, :], "v k h -> k v h"),
                    in_=_r(vsl, "k (v h) -> k v h", h=128)),
                    waits=[("ap_ev2", p.n("ap_ev2"))], inc=f"vst{vpar}", inc_n=16)

    def q_proj():
        ln_done = p.n("ln_ap")
        for h in range(nh):
            p.op("gpsimd", lambda e, h=h: e.dma_start(
                out=_r(wkb[:, :], "p (c j) -> p c j", j=128),
                in_=_r(wq[:, h * 128:(h + 1) * 128], "(c p) j -> p c j", p=128)),
                waits=[("ap_mm", p.n("ap_mm"))], inc="wpld", inc_n=16)
            wld = p.n("wpld")
            for ts4 in range(NS):
                bank = pb[4 + ts4 % 2]
                for dc in range(DC):
                    w0 = []
                    if dc == 0:
                        w0 = [("wpld", wld), ("ln_ap", ln_done),
                              ("ap_ev", p.n("ap_ev") - 1)]
                    p.op("tensor", lambda e, bank=bank, dc=dc, ts4=ts4: e.matmul(
                        bank[:, :], wkb[:, dc * 128:(dc + 1) * 128],
                        xl[:, dc * T + ts4 * 512:dc * T + ts4 * 512 + 512],
                        start=(dc == 0), stop=(dc == DC - 1)),
                        waits=w0, inc=("ap_mm" if dc == DC - 1 else None))
                qoff = h * T + ts4 * 512
                ew = [("ap_mm", p.n("ap_mm"))]
                if rope:
                    ew.append(("csld", p.n("csld")))
                    rope_evac(bank, ts4 * 512,
                              lambda half, qoff=qoff: QT[half * 64:(half + 1) * 64,
                                                         qoff:qoff + 512],
                              ew, "ap_ev")
                else:
                    p.op("vector", lambda e, bank=bank, qoff=qoff: e.tensor_copy(
                        out=QT[:, qoff:qoff + 512], in_=bank[:, :]),
                        waits=ew, inc="ap_ev")

    def load_cs(r):
        if not rope:
            return
        cosd, sind = W["cs"][r]
        p.op("gpsimd", lambda e, cosd=cosd: e.dma_start(out=csb[:, 0:T], in_=cosd[:, :]),
             waits=[("ap_ev", p.n("ap_ev")), ("ap_ev2", p.n("ap_ev2"))],
             inc="csld", inc_n=16)
        p.op("gpsimd", lambda e, sind=sind: e.dma_start(out=csb[:, T:2 * T], in_=sind[:, :]),
             inc="csld", inc_n=16)

    if fused:
        load_cs(2)
        for ts4 in range(NS):
            ln_tsub(xT_own, ts4)
        kv_proj(kvKin, kvVin)
        cw = [(f"kst{i}", p.n(f"kst{i}")) for i in range(2)] + \
             [(f"vst{i}", p.n(f"vst{i}")) for i in range(2)]
        p.op("gpsimd", lambda e: e.collective_compute(
            "AllGather", ALU.bypass,
            replica_groups=[[0, 1], [2, 3], [4, 5], [6, 7]],
            ins=[kvKin.ap().opt()],
            outs=[kvKout.ap().rearrange("r a b -> (r a) b").opt()]),
            waits=cw, inc="ccg")
        p.op("gpsimd", lambda e: e.collective_compute(
            "AllGather", ALU.bypass,
            replica_groups=[[0, 1], [2, 3], [4, 5], [6, 7]],
            ins=[kvVin.ap().rearrange("a b c d -> (a b c) d").opt()],
            outs=[kvVout.ap().rearrange("r a b c d -> (r a b c) d").opt()]),
            inc="ccg")
        ccv = p.n("ccg")
        q_proj()
    else:
        srcs = [W["xT_p0"], W["xT_p1"], xT_own]
        for r in range(2):
            load_cs(r)
            for ts4 in range(NS):
                ln_tsub(srcs[r], ts4)
            kv_proj(kvKout[r], kvVout[r])
        load_cs(2)
        for ts4 in range(NS):
            ln_tsub(srcs[2], ts4)
        q_proj()
        ccv = 0

    p.fence_all()
    esA.close()

    esB = ExitStack()

    def sbB(nm, shape, dt):
        return esB.enter_context(p.nc.sbuf_tensor(p.uname(nm), list(shape), dt))
    pat = sbB("pat", [128, 2 * MPS * QB], BF16)
    ksb = sbB("ksb", [128, 2 * 2 * T], BF16)
    vsb = sbB("vsb", [128, 2 * 2 * 32 * 128], BF16)
    esb = sbB("esb", [128, 2 * 2 * QB], BF16)
    onecB = sbB("onecB", [128, 1], BF16)
    onefr = sbB("onefr", [1, 128], F32)
    rrow = sbB("rrow", [1, 512], F32)
    lnrow = sbB("lnrow", [1, 512], F32)
    ctx_sb = sbB("ctx_sb", [128, 2 * 512], F32)
    den_sb = sbB("den_sb", [1, 2 * 512], F32)
    bcs = sbB("bcs", [128, 512], F32)
    csg = sbB("csg", [128, 2 * 512], BF16)
    if alibi:
        ba = sbB("ba", [6, 2 * T], BF16)
        bb = sbB("bb", [6, T], BF16)

    p.op("vector", lambda e: e.memset(onecB[:, :], 1.0))
    p.op("vector", lambda e: e.memset(onefr[:, :], 1.0), inc="binit")
    b_binit = p.n("binit")
    p.op("sync", lambda e: e.dma_start(
        out=_r(pat[:, :], "p (r d q) -> p r d q", d=MPS, q=QB),
        in_=_r(maskpat[:, :, :, :], "r d p q -> p r d q")),
        waits=[("ccg", ccv)], inc="pld", inc_n=16)
    b_pld = p.n("pld")

    kv_iters = [(gi, s) for gi in range(len(groups)) for s in range(NS)]
    pend_tail = [None]
    sc_after, mm_after = [], []
    kld_at, vld_at = {}, {}

    def emit_kv_load(j):
        gi, s = kv_iters[j]
        nm = MPS * (s + 1)
        par = j % 2
        kw = [("ccg", ccv)]
        if j >= 2:
            kw += [("at_sc", sc_after[j - 2]), ("at_mm", mm_after[j - 2])]
        for r01 in range(2):
            p.op("sync", lambda e, r01=r01, gi=gi, nm=nm, par=par: e.dma_start(
                out=ksb[:, (par * 2 + r01) * T:(par * 2 + r01) * T + nm * 128],
                in_=kvKout[r01, gi * 128:(gi + 1) * 128, 0:nm * 128]),
                waits=kw, inc=f"kld{par}", inc_n=16)
            p.op("sync", lambda e, r01=r01, gi=gi, nm=nm, par=par: e.dma_start(
                out=_r(vsb[:, (par * 2 + r01) * 4096:(par * 2 + r01) * 4096 + nm * 128],
                       "k (m h) -> k m h", h=128),
                in_=_r(kvVout[r01, 0:nm, gi, :, :], "m k h -> k m h")),
                waits=kw, inc=f"vld{par}", inc_n=16)
        kld_at[j] = p.n(f"kld{par}")
        vld_at[j] = p.n(f"vld{par}")

    emit_kv_load(0)
    for j, (gi, s) in enumerate(kv_iters):
        if j + 1 < len(kv_iters):
            emit_kv_load(j + 1)
        hs = groups[gi]
        nm = MPS * (s + 1)
        par = j % 2
        kldv, vldv = kld_at[j], vld_at[j]
        ksem, vsem = f"kld{par}", f"vld{par}"
        koff = par * 2 * T
        voff = par * 2 * 4096
        for h in hs:
            if alibi:
                p.op("gpsimd", lambda e, h=h: e.dma_start(
                    out=_r(ba[:, :], "b (r c) -> b r c", r=2),
                    in_=_r(W["baT"][h, :, :, :], "r b c -> b r c")),
                    waits=[("at_sc", p.n("at_sc"))], inc="bld", inc_n=16)
                p.op("gpsimd", lambda e, h=h: e.dma_start(
                    out=bb[:, :], in_=W["bbT"][h, :, :]),
                    inc="bld", inc_n=16)
            bldv = p.n("bld") if alibi else 0
            n_i = 2 * nm
            npk = n_i // 2
            tiles = [(r01, m) for r01 in range(2) for m in range(nm)]
            sc_pair, mn_pair = {}, {}

            def emit_scores_pair(pk):
                for t_i in range(2):
                    r01, m = tiles[2 * pk + t_i]
                    i_g = 2 * pk + t_i
                    si = i_g % 4
                    bank = pb[si]
                    w0 = [(ksem, kldv), ("at_ex", p.n("at_ex"))]
                    if alibi:
                        w0.append(("bld", bldv))
                    p.op("tensor", lambda e, bank=bank, r01=r01, m=m, h=h, s=s, koff=koff: e.matmul(
                        bank[:, :], ksb[:, koff + r01 * T + m * 128:koff + r01 * T + m * 128 + 128],
                        QT[:, h * T + s * 512:h * T + s * 512 + 512],
                        start=True, stop=(not alibi)),
                        waits=w0, inc=(None if alibi else "at_sc"))
                    if alibi:
                        p.op("tensor", lambda e, bank=bank, r01=r01, m=m, s=s: e.matmul(
                            bank[:, :], ba[:, r01 * T + m * 128:r01 * T + m * 128 + 128],
                            bb[:, s * 512:s * 512 + 512],
                            start=False, stop=True), inc="at_sc")
                    if m >= MPS * s:
                        d = m - MPS * s
                        p.op("vector", lambda e, bank=bank, r01=r01, d=d: e.tensor_tensor(
                            out=bank[:, :], in0=bank[:, :],
                            in1=pat[:, (r01 * MPS + d) * QB:(r01 * MPS + d + 1) * QB],
                            op=ALU.min),
                            waits=[("at_sc", p.n("at_sc")), ("pld", b_pld)], inc="at_mn")
                sc_pair[pk] = p.n("at_sc")
                mn_pair[pk] = p.n("at_mn")

            emit_scores_pair(0)
            if pend_tail[0] is not None:
                pend_tail[0]()
                pend_tail[0] = None
            for pk in range(npk):
                if pk + 1 < npk:
                    emit_scores_pair(pk + 1)
                ew = [("at_sc", sc_pair[pk]), ("at_mn", mn_pair[pk]),
                      ("at_mm", p.n("at_mm") - 2)]
                q0 = (2 * pk % 4) * 512
                eoff = (pk % 2) * 1024
                p.op("scalar", lambda e, q0=q0, eoff=eoff: e.activation(
                    out=esb[:, eoff:eoff + 1024], in_=pbquad[:, q0:q0 + 1024],
                    func=AF.Exp, bias=0.0, scale=RSQHD),
                    waits=ew, inc="at_ex")
                for t_i in range(2):
                    r01, m = tiles[2 * pk + t_i]
                    i_g = 2 * pk + t_i
                    cw2 = [("at_ex", p.n("at_ex")), (vsem, vldv)]
                    if i_g == 0:
                        cw2.append(("at_cx", p.n("at_cx")))
                    p.op("tensor", lambda e, r01=r01, m=m, st=(i_g == 0), sp=(i_g == n_i - 1),
                         eo=eoff + t_i * 512, voff=voff: e.matmul(
                        pb[4][:, :], vsb[:, voff + r01 * 4096 + m * 128:voff + r01 * 4096 + m * 128 + 128],
                        esb[:, eo:eo + 512],
                        start=st, stop=sp), waits=cw2)
                    ow = [("binit", b_binit)]
                    if i_g == 0:
                        ow.append(("at_rc", p.n("at_rc")))
                    p.op("tensor", lambda e, st=(i_g == 0), sp=(i_g == n_i - 1),
                         eo=eoff + t_i * 512: e.matmul(
                        pb[5][0:1, :], onecB[:, :], esb[:, eo:eo + 512],
                        start=st, stop=sp), waits=ow, inc="at_mm")
            hpar = p.n("at_cx") % 2
            p.op("vector", lambda e, hpar=hpar: e.tensor_copy(
                out=ctx_sb[:, hpar * 512:(hpar + 1) * 512], in_=pb[4][:, :]),
                waits=[("at_mm", p.n("at_mm")), ("at_nm", p.n("at_nm") - 1)],
                inc="at_cx")
            p.op("vector", lambda e, hpar=hpar: e.tensor_copy(
                out=den_sb[0:1, hpar * 512:(hpar + 1) * 512], in_=pb[5][0:1, :]),
                waits=[("at_rp", p.n("at_rp") - 1)], inc="at_rc")
            p.op("scalar", lambda e, hpar=hpar: e.activation(
                out=lnrow[:, :], in_=den_sb[0:1, hpar * 512:(hpar + 1) * 512],
                func=AF.Ln, bias=0.0, scale=1.0),
                waits=[("at_rc", p.n("at_rc")), ("at_bcm", p.n("at_bcm"))])
            p.op("scalar", lambda e: e.activation(
                out=rrow[:, :], in_=lnrow[:, :], func=AF.Exp, bias=0.0, scale=-1.0),
                inc="at_rp")

            def tail_post(h=h, s=s, hpar=hpar, rp=p.n("at_rp")):
                p.op("tensor", lambda e: e.matmul(pb[6][:, :], onefr[:, :], rrow[:, :],
                     start=True, stop=True),
                     waits=[("at_rp", rp), ("at_bc", p.n("at_bc"))], inc="at_bcm")
                p.op("vector", lambda e: e.tensor_copy(out=bcs[:, :], in_=pb[6][:, :]),
                     waits=[("at_bcm", p.n("at_bcm"))], inc="at_bc")
                cpar = p.n("at_nm") % 2
                p.op("vector", lambda e, cpar=cpar, hpar=hpar: e.tensor_tensor(
                    out=csg[:, cpar * 512:(cpar + 1) * 512],
                    in0=ctx_sb[:, hpar * 512:(hpar + 1) * 512],
                    in1=bcs[:, :], op=ALU.mult),
                    waits=[(f"cst{cpar}", p.n(f"cst{cpar}"))], inc="at_nm")
                p.op("gpsimd", lambda e, h=h, s=s, cpar=cpar: e.dma_start(
                    out=ctxd[h * 128:(h + 1) * 128, s * 512:s * 512 + 512],
                    in_=csg[:, cpar * 512:(cpar + 1) * 512]),
                    waits=[("at_nm", p.n("at_nm"))], inc=f"cst{cpar}", inc_n=16)
            pend_tail[0] = tail_post
        sc_after.append(p.n("at_sc"))
        mm_after.append(p.n("at_mm"))
    if pend_tail[0] is not None:
        pend_tail[0]()
        pend_tail[0] = None

    p.fence_all()
    esB.close()
    esQ.close()

    esC = ExitStack()

    def sbC(nm, shape, dt):
        return esC.enter_context(p.nc.sbuf_tensor(p.uname(nm), list(shape), dt))
    cbuf = sbC("cbuf", [128, DC * T], BF16)
    wob = sbC("wob", [128, 2 * DC * 128], BF16)
    yvC = sbC("yvC", [128, 2 * 512], F32)
    xres = sbC("xres", [128, 2 * 512], F32)
    if layer == 1:
        boS = sbC("boS", [128, DC], F32)
        p.op("sync", lambda e: e.dma_start(out=boS[:, :], in_=W["boR"][:, :]),
             inc="cld", inc_n=16)
    p.op("sync", lambda e: e.dma_start(
        out=_r(cbuf[:, :], "p (c t) -> p c t", t=T),
        in_=_r(ctxd[:, :], "(c p) t -> p c t", p=128)), inc="cld", inc_n=16)
    cldv = p.n("cld")
    ec0 = p.n("oc_ev")
    ec = 0
    for oc in range(DC):
        opar = oc % 2
        wsemo = f"wold{opar}"
        woffo = opar * DC * 128
        p.op("gpsimd", lambda e, oc=oc, woffo=woffo: e.dma_start(
            out=_r(wob[:, woffo:woffo + DC * 128], "p (c j) -> p c j", j=128),
            in_=_r(wo[:, oc * 128:(oc + 1) * 128], "(c p) j -> p c j", p=128)),
            waits=[("oc_mm", max(0, p.n("oc_mm") - 4))], inc=wsemo, inc_n=16)
        wldv = p.n(wsemo)
        for ts4 in range(NS):
            bank = pb[ec % 4]
            for hc in range(DC):
                w0 = []
                if hc == 0:
                    w0 = [(wsemo, wldv), ("cld", cldv),
                          ("oc_ev", max(0, ec0 + ec - 3))]
                p.op("tensor", lambda e, bank=bank, hc=hc, ts4=ts4, woffo=woffo: e.matmul(
                    bank[:, :], wob[:, woffo + hc * 128:woffo + hc * 128 + 128],
                    cbuf[:, hc * T + ts4 * 512:hc * T + ts4 * 512 + 512],
                    start=(hc == 0), stop=(hc == DC - 1)),
                    waits=w0, inc=("oc_mm" if hc == DC - 1 else None))
            par = ec % 2
            p.op("sync", lambda e, oc=oc, ts4=ts4, par=par: e.dma_start(
                out=xres[:, par * 512:(par + 1) * 512],
                in_=xT_own[oc * 128:(oc + 1) * 128, ts4 * 512:ts4 * 512 + 512]),
                waits=[("oc_ev", max(0, ec0 + ec - 1))], inc=f"xrl{par}", inc_n=16)
            sa = (lambda oc=oc: boS[:, oc:oc + 1])() if layer == 1 else 0.0
            p.op("vector", lambda e, bank=bank, par=par, sa=sa: e.scalar_tensor_tensor(
                out=yvC[:, par * 512:(par + 1) * 512], in0=bank[:, :], scalar=sa,
                in1=xres[:, par * 512:(par + 1) * 512], op0=ALU.add, op1=ALU.add),
                waits=[("oc_mm", p.n("oc_mm")), (f"xrl{par}", p.n(f"xrl{par}")),
                       (f"yst{par}", p.n(f"yst{par}")), ("cld", cldv)],
                inc="oc_ev")
            p.op("scalar", lambda e, oc=oc, ts4=ts4, par=par: e.dma_start(
                out=yT[oc * 128:(oc + 1) * 128, ts4 * 512:ts4 * 512 + 512],
                in_=yvC[:, par * 512:(par + 1) * 512]),
                waits=[("oc_ev", p.n("oc_ev"))], inc=f"yst{par}", inc_n=16)
            ec += 1
    esC.close()



def attn_weight_handles(p, layer, pref=""):
    W = {}
    if layer == 0:
        W["wq"] = p.dram(pref + "wq", [D, D], F32, kind="ExternalInput")
        wk = p.dram(pref + "wk", [D, D], F32, kind="ExternalInput")
        wv = p.dram(pref + "wv", [D, D], F32, kind="ExternalInput")
        W["wo"] = p.dram(pref + "wo", [D, D], F32, kind="ExternalInput")
        W["ksrc"] = lambda kc: wk[:, kc * 128:(kc + 1) * 128]
        W["vsrc"] = lambda vf: wv[:, vf * 256:(vf + 1) * 256]
        W["cs"] = [(p.dram(pref + "cosk0", [64, T], F32, kind="ExternalInput"),
                    p.dram(pref + "sink0", [64, T], F32, kind="ExternalInput")),
                   (p.dram(pref + "cosk1", [64, T], F32, kind="ExternalInput"),
                    p.dram(pref + "sink1", [64, T], F32, kind="ExternalInput")),
                   (p.dram(pref + "cosq", [64, T], F32, kind="ExternalInput"),
                    p.dram(pref + "sinq", [64, T], F32, kind="ExternalInput"))]
    else:
        W["wq"] = p.dram(pref + "wq", [D, D], F32, kind="ExternalInput")
        wkv = p.dram(pref + "wkv", [D, 1024], F32, kind="ExternalInput")
        W["wo"] = p.dram(pref + "wo", [D, D], F32, kind="ExternalInput")
        W["boR"] = p.dram(pref + "boR", [128, DC], F32, kind="ExternalInput")
        W["baT"] = p.dram(pref + "baT", [NH, 2, 6, T], BF16, kind="ExternalInput")
        W["bbT"] = p.dram(pref + "bbT", [NH, 6, T], BF16, kind="ExternalInput")
        W["ksrc"] = lambda kc: wkv[:, kc * 128:(kc + 1) * 128]
        W["vsrc"] = lambda vf: wkv[:, 512 + vf * 256:512 + (vf + 1) * 256]
    W["maskpat"] = p.dram(pref + "maskpat", [2, MPS, 128, QB], BF16, kind="ExternalInput")
    return W


def ffn_weight_handles(p, layer, pref=""):
    if layer == 0:
        return {
            "gwT": p.dram(pref + "gwT", [D, HID], I32, kind="ExternalInput"),
            "uwT": p.dram(pref + "uwT", [D, HID], I32, kind="ExternalInput"),
            "dwT": p.dram(pref + "dwT", [HID, D], I32, kind="ExternalInput"),
            "sg": p.dram(pref + "sg", [128, NJC], F32, kind="ExternalInput"),
            "su": p.dram(pref + "su", [128, NJC], F32, kind="ExternalInput"),
            "sd": p.dram(pref + "sd", [128, DC], F32, kind="ExternalInput"),
        }
    return {
        "w1T": p.dram(pref + "w1T", [D, HID], F32, kind="ExternalInput"),
        "w2T": p.dram(pref + "w2T", [D, HID], F32, kind="ExternalInput"),
        "w3T": p.dram(pref + "w3T", [HID, D], F32, kind="ExternalInput"),
        "w1q": p.dram(pref + "w1q", [D, HID], BF16),
        "w2q": p.dram(pref + "w2q", [D, HID], BF16),
        "w3q": p.dram(pref + "w3q", [HID, D], BF16),
    }


def build_ffn(layer):
    p = K(f"ffn{layer}")
    xT = p.dram("xT", [D, T], F32, kind="ExternalInput")
    yT = p.dram("yT", [D, T], F32, kind="ExternalOutput")
    W = ffn_weight_handles(p, layer)
    pb, pbquad = p.psum_pool()
    p.constf(EPS_LN)
    S = emit_scales_l0(p, W) if layer == 0 else None
    if layer == 1:
        emit_quant_body(p, pb, [(W["w1T"], W["w1q"], D, HID),
                                (W["w2T"], W["w2q"], D, HID),
                                (W["w3T"], W["w3q"], HID, D)])
        p.fence_all()
    emit_ffn_body(p, layer, xT, yT, W, pb, S)
    dma_sems = [s for s in p.ctr if s.startswith(("sst", "wld", "wdl", "xload",
                                                  "qld", "qst", "sload"))]
    p.wait("gpsimd", [(s, p.n(s)) for s in dma_sems])
    return p


def build_attn(layer):
    p = K(f"att{layer}")
    xT_own = p.dram("xT_own", [D, T], F32, kind="ExternalInput")
    yT = p.dram("yT", [D, T], F32, kind="ExternalOutput")
    W = attn_weight_handles(p, layer)
    W["xT_p0"] = p.dram("xT_p0", [D, T], F32, kind="ExternalInput")
    W["xT_p1"] = p.dram("xT_p1", [D, T], F32, kind="ExternalInput")
    pb, pbquad = p.psum_pool()
    p.constf(EPS_LN)
    emit_attn_body(p, layer, xT_own, yT, W, (pb, pbquad), fused=False)
    dma_sems = [s for s in p.ctr if s.startswith(("yst", "xrl", "wold", "cld"))]
    p.wait("gpsimd", [(s, p.n(s)) for s in dma_sems])
    return p


def build_mega():
    p = K("mega")
    xT = p.dram("xT", [D, T], F32, kind="ExternalInput")
    yT = p.dram("yT", [D, T], F32, kind="ExternalOutput")
    Wf0 = ffn_weight_handles(p, 0, "f0_")
    Wf1 = ffn_weight_handles(p, 1, "f1_")
    Wa0 = attn_weight_handles(p, 0, "a0_")
    Wa1 = attn_weight_handles(p, 1, "a1_")
    xA = p.dram("xA", [D, T], F32)
    xB = p.dram("xB", [D, T], F32)
    xC = p.dram("xC", [D, T], F32)
    xD = p.dram("xD", [D, T], F32)
    xE = p.dram("xE", [D, T], F32)
    pb, pbquad = p.psum_pool()
    p.constf(EPS_LN)
    S0 = emit_scales_l0(p, Wf0)
    emit_quant_body(p, pb, [(Wf1["w1T"], Wf1["w1q"], D, HID),
                            (Wf1["w2T"], Wf1["w2q"], D, HID),
                            (Wf1["w3T"], Wf1["w3q"], HID, D)])
    p.fence_all()
    emit_ffn_body(p, 0, xT, xA, Wf0, pb, S0)
    p.fence_all()
    emit_attn_body(p, 0, xA, xB, Wa0, (pb, pbquad), fused=True, cc_tag="a0")
    p.fence_all()
    emit_ffn_body(p, 0, xB, xC, Wf0, pb, S0)
    p.fence_all()
    emit_ffn_body(p, 1, xC, xD, Wf1, pb)
    p.fence_all()
    emit_attn_body(p, 1, xD, xE, Wa1, (pb, pbquad), fused=True, cc_tag="a1")
    p.fence_all()
    emit_ffn_body(p, 1, xE, yT, Wf1, pb)
    p.fence_all()
    return p



def shard_x(x):
    out = []
    for c in range(N_CORES):
        b, par = c // 2, c % 2
        out.append(np.ascontiguousarray(x[b, par::2, :].T))
    return out


def unshard_x(shards):
    x = np.empty((4, 4096, 2048), np.float32)
    for c in range(N_CORES):
        b, par = c // 2, c % 2
        x[b, par::2, :] = shards[c].T
    return x


def ffn_inputs(layer, inp, pref=""):
    if layer == 0:
        return {
            pref + "gwT": np.ascontiguousarray(np.asarray(inp["L0_ffn_wq_gate"]).T),
            pref + "uwT": np.ascontiguousarray(np.asarray(inp["L0_ffn_wq_up"]).T),
            pref + "dwT": np.ascontiguousarray(np.asarray(inp["L0_ffn_wq_down"]).T),
            pref + "sg": np.ascontiguousarray(np.asarray(inp["L0_ffn_s_gate"]).reshape(NJC, 128).T),
            pref + "su": np.ascontiguousarray(np.asarray(inp["L0_ffn_s_up"]).reshape(NJC, 128).T),
            pref + "sd": np.ascontiguousarray(np.asarray(inp["L0_ffn_s_down"]).reshape(DC, 128).T),
        }
    return {
        pref + "w1T": np.ascontiguousarray(np.asarray(inp["L1_ffn_w1"]).T),
        pref + "w2T": np.ascontiguousarray(np.asarray(inp["L1_ffn_w2"]).T),
        pref + "w3T": np.ascontiguousarray(np.asarray(inp["L1_ffn_w3"]).T),
    }


def _split3(a):
    a = a.astype(np.float32)
    a0 = a.astype(BF).astype(np.float32)
    a1 = (a - a0).astype(BF).astype(np.float32)
    a2 = (a - a0 - a1).astype(BF)
    return np.stack([a0.astype(BF), a1.astype(BF), a2], 0)


def _alibi_slopes(n):
    start = 2.0 ** (-2.0 ** (-(np.log2(n) - 3.0)))
    return np.array([start ** (i + 1) for i in range(n)], np.float64)


def attn_inputs(layer, inp, par, pref=""):
    out = {}
    if layer == 0:
        out[pref + "wq"] = np.ascontiguousarray(np.asarray(inp["L0_attn_Wq"]).T)
        out[pref + "wk"] = np.ascontiguousarray(np.asarray(inp["L0_attn_Wk"]).T)
        out[pref + "wv"] = np.ascontiguousarray(np.asarray(inp["L0_attn_Wv"]).T)
        out[pref + "wo"] = np.ascontiguousarray(np.asarray(inp["L0_attn_Wo"]).T)
        inv_freq = 1.0 / 10000.0 ** (np.arange(0, 128, 2, dtype=np.float32) / 128.0)
        tpos = np.arange(4096, dtype=np.float32)
        fr = np.outer(tpos, inv_freq)
        cos = np.cos(fr).astype(np.float32)
        sin = np.sin(fr).astype(np.float32)
        out[pref + "cosk0"] = np.ascontiguousarray(cos[0::2].T)
        out[pref + "sink0"] = np.ascontiguousarray(sin[0::2].T)
        out[pref + "cosk1"] = np.ascontiguousarray(cos[1::2].T)
        out[pref + "sink1"] = np.ascontiguousarray(sin[1::2].T)
        out[pref + "cosq"] = np.ascontiguousarray(cos[par::2].T)
        out[pref + "sinq"] = np.ascontiguousarray(sin[par::2].T)
    else:
        out[pref + "wq"] = np.ascontiguousarray(np.asarray(inp["L1_attn_Wq"]))
        out[pref + "wkv"] = np.ascontiguousarray(np.asarray(inp["L1_attn_Wkv"]))
        out[pref + "wo"] = np.ascontiguousarray(np.asarray(inp["L1_attn_Wo"]))
        out[pref + "boR"] = np.ascontiguousarray(
            np.asarray(inp["L1_attn_bo"]).reshape(DC, 128).T.astype(np.float32))
        sl = _alibi_slopes(NH)
        S128 = np.sqrt(128.0)
        l = np.arange(T, dtype=np.float64)
        baT = np.zeros((NH, 2, 6, T), BF)
        bbT = np.zeros((NH, 6, T), BF)
        for h in range(NH):
            for r in range(2):
                A = S128 * sl[h] * (2 * l + r)
                baT[h, r, 0:3] = _split3(A)
                baT[h, r, 3:6] = np.ones((3, T), BF)
            Bv = -S128 * sl[h] * (2 * l + par)
            bbT[h, 0:3] = np.ones((3, T), BF)
            bbT[h, 3:6] = _split3(Bv)
        out[pref + "baT"] = baT
        out[pref + "bbT"] = bbT
    kk = np.arange(128)[:, None]
    qq = np.arange(QB)[None, :]
    mp = np.zeros((2, MPS, 128, QB), BF)
    for r in range(2):
        eps = -1 if (par == 0 and r == 1) else 0
        for d in range(MPS):
            valid = kk <= (qq - 128 * d + eps)
            mp[r, d] = np.where(valid, np.float32(1e30), np.float32(-1e30)).astype(BF)
    out[pref + "maskpat"] = mp
    return out


_prog_cache = {}


def get_prog(kind, layer=0):
    key = (kind, layer)
    if key not in _prog_cache:
        if kind == "ffn":
            _prog_cache[key] = build_ffn(layer).finish()
        elif kind == "att":
            _prog_cache[key] = build_attn(layer).finish()
        elif kind == "mega":
            _prog_cache[key] = build_mega().finish()
        else:
            raise KeyError(kind)
    return _prog_cache[key]


def run_ffn(layer, x_shards, inp):
    nc = get_prog("ffn", layer)
    wmap = ffn_inputs(layer, inp)
    in_maps = [dict(wmap, xT=xs) for xs in x_shards]
    res = run_bass_kernel_spmd(nc, in_maps, core_ids=list(range(N_CORES)))
    return [np.asarray(res.results[c]["yT"]) for c in range(N_CORES)]


def run_attn(layer, x_shards, inp):
    nc = get_prog("att", layer)
    wmaps = [attn_inputs(layer, inp, 0), attn_inputs(layer, inp, 1)]
    in_maps = []
    for c in range(N_CORES):
        b, par = c // 2, c % 2
        in_maps.append(dict(wmaps[par], xT_own=x_shards[c],
                            xT_p0=x_shards[2 * b], xT_p1=x_shards[2 * b + 1]))
    res = run_bass_kernel_spmd(nc, in_maps, core_ids=list(range(N_CORES)))
    return [np.asarray(res.results[c]["yT"]) for c in range(N_CORES)]


def kernel(**inputs):
    inputs = {k: np.asarray(v) for k, v in inputs.items()}
    nc = get_prog("mega")
    base = {}
    base.update(ffn_inputs(0, inputs, "f0_"))
    base.update(ffn_inputs(1, inputs, "f1_"))
    wa = [dict(attn_inputs(0, inputs, par, "a0_"),
               **attn_inputs(1, inputs, par, "a1_")) for par in range(2)]
    x_shards = shard_x(inputs["L0_attn_hidden_states"])
    in_maps = []
    for c in range(N_CORES):
        par = c % 2
        m = dict(base)
        m.update(wa[par])
        m["xT"] = x_shards[c]
        in_maps.append(m)
    res = run_bass_kernel_spmd(nc, in_maps, core_ids=list(range(N_CORES)))
    return unshard_x([np.asarray(res.results[c]["yT"]) for c in range(N_CORES)])


def kernel_unfused(**inputs):
    inputs = {k: np.asarray(v) for k, v in inputs.items()}
    x = shard_x(inputs["L0_attn_hidden_states"])
    x = run_ffn(0, x, inputs)
    x = run_attn(0, x, inputs)
    x = run_ffn(0, x, inputs)
    x = run_ffn(1, x, inputs)
    x = run_attn(1, x, inputs)
    x = run_ffn(1, x, inputs)
    return unshard_x(x)


# version 07
# speedup vs baseline: 1.1662x; 1.0155x over previous
import numpy as np
import ml_dtypes
from contextlib import ExitStack
import concourse.bass as bass
import concourse.mybir as mybir
from concourse.bass_utils import run_bass_kernel_spmd

F32 = mybir.dt.float32
BF16 = mybir.dt.bfloat16
I32 = mybir.dt.int32
AF = mybir.ActivationFunctionType
ALU = mybir.AluOpType
AX = mybir.AxisListType
BF = ml_dtypes.bfloat16

N_CORES = 8
D = 2048
T = 2048
HID = 5632
DC = D // 128
NH = 16
NJC = HID // 128
EPS_LN = 1e-5
INVD = 1.0 / D
NS = 4
QB = 512
KC = 128
MPS = QB // KC
RSQHD = float(1.0 / np.sqrt(128.0))

ENGINES = ("tensor", "scalar", "vector", "gpsimd", "sync")


class K:
    def __init__(self, name):
        self.nc = bass.Bass("TRN2", target_bir_lowering=False, debug=False)
        self.name = name
        self.q = {e: [] for e in ENGINES}
        self.ctr = {}
        self._semh = {}
        self._waited = {e: {} for e in ENGINES}
        self.es = ExitStack()
        self._uid = 0

    def sem(self, name):
        if name not in self._semh:
            self._semh[name] = self.es.enter_context(self.nc.semaphore(name))
            self.ctr[name] = 0
        return self._semh[name]

    def uname(self, nm):
        self._uid += 1
        return f"{nm}_{self._uid}"

    def sbuf(self, name, shape, dt):
        return self.es.enter_context(self.nc.sbuf_tensor(self.uname(name), list(shape), dt))

    def psum(self, name, shape, dt=F32):
        return self.es.enter_context(self.nc.psum_tensor(self.uname(name), list(shape), dt))

    def psum_pool(self):
        quad = self.psum("pbq", [128, 2048])
        pb = [quad[:, i * 512:(i + 1) * 512] for i in range(4)]
        pb += [self.psum("pb", [128, 512]) for _ in range(4)]
        return pb, quad

    def dram(self, name, shape, dt, kind="Internal"):
        return self.nc.dram_tensor(name, list(shape), dt, kind=kind)

    def op(self, eng, build, waits=(), inc=None, inc_n=1):
        wl = []
        for s, v in waits:
            if v <= 0:
                continue
            assert v <= self.ctr.get(s, 0), \
                f"wait {s}>={v} exceeds planned {self.ctr.get(s, 0)}"
            if self._waited[eng].get(s, -1) >= v:
                continue
            self._waited[eng][s] = v
            wl.append((self.sem(s), v))
        inch = None
        if inc is not None:
            inch = self.sem(inc)
            self.ctr[inc] += inc_n

        def fn(e, wl=wl, build=build, inch=inch, inc_n=inc_n):
            for h, v in wl:
                e.wait_ge(h, v)
            r = build(e)
            if inch is not None:
                r.then_inc(inch, inc_n)
        self.q[eng].append(fn)

    def wait(self, eng, waits):
        self.op(eng, lambda e: None, waits=waits)

    def fence_all(self):
        w = [(s, self.ctr[s]) for s in self.ctr]
        for e in ENGINES:
            self.wait(e, w)

    def constf(self, value):
        if not hasattr(self, "_cstbuf"):
            self._cstbuf = self.sbuf("cst", [1, 16], F32)
            self._csts = {}
        if value not in self._csts:
            i = len(self._csts)
            assert i < 16
            self._csts[value] = i
            ap = self._cstbuf[0:1, i:i + 1]
            self.nc.const_aps.register(F32, value, ap)
            self.op("vector", lambda e, ap=ap, value=value: e.memset(ap, value),
                    inc="init")

    def n(self, semname):
        return self.ctr.get(semname, 0)

    def finish(self):
        with self.nc.Block() as block:
            for e in ENGINES:
                lst = self.q[e]
                if not lst:
                    continue
                dec = getattr(block, e)

                def body(eng, lst=lst):
                    for fn in lst:
                        fn(eng)
                dec(body)
        self.es.close()
        return self.nc


def _r(ap, spec, **kw):
    return ap.rearrange(spec, **kw)



def emit_scales_l0(p, W):
    sgs = p.sbuf("sgs", [128, NJC], F32)
    sus = p.sbuf("sus", [128, NJC], F32)
    sds = p.sbuf("sds", [128, DC], F32)
    p.op("sync", lambda e: e.dma_start(out=sgs[:, :], in_=W["sg"][:, :]), inc="sload", inc_n=16)
    p.op("sync", lambda e: e.dma_start(out=sus[:, :], in_=W["su"][:, :]), inc="sload", inc_n=16)
    p.op("sync", lambda e: e.dma_start(out=sds[:, :], in_=W["sd"][:, :]), inc="sload", inc_n=16)
    p.op("vector", lambda e: e.tensor_scalar(out=sds[:, :], in0=sds[:, :],
                                             scalar1=0.5, scalar2=None, op0=ALU.mult),
         waits=[("sload", p.n("sload"))], inc="init")
    return {"sgs": sgs, "sus": sus, "sds": sds,
            "ready": [("sload", p.n("sload")), ("init", p.n("init"))]}



def emit_quant_body(p, pb, specs):
    es2 = ExitStack()

    def sb(nm, shape, dt):
        return es2.enter_context(p.nc.sbuf_tensor(p.uname(nm), list(shape), dt))
    qw = sb("qw", [128, 2 * 2816], F32)
    qp = sb("qp", [128, 2816], BF16)
    qn = sb("qn", [128, 2816], BF16)
    qo = sb("qo", [128, 2 * 2816], BF16)
    strips = sb("strips", [128, 64], F32)
    onef = sb("onef", [128, 1], F32)
    onefr = sb("onefr", [1, 128], F32)
    gval = sb("gval", [128, 4], F32)
    p.op("vector", lambda e: e.memset(onef[:, :], 1.0))
    p.op("vector", lambda e: e.memset(onefr[:, :], 1.0), inc="qrow")
    base_qbc = p.n("qbc")

    def tiles_of(rows, cols):
        w = 2816 if cols == HID else 2048
        return [(rc * 128, cc * w, w) for rc in range(rows // 128)
                for cc in range(cols // w)]

    for mi, (src, dst, rows, cols) in enumerate(specs):
        tl = tiles_of(rows, cols)
        nt = len(tl)
        base_abs = p.n("qabs")
        for i, (r0, c0, w) in enumerate(tl):
            sl = i % 2
            qeng = "scalar" if sl == 0 else "sync"
            qsem = "qlA" if sl == 0 else "qlS"
            p.op(qeng, lambda e, src=src, r0=r0, c0=c0, w=w, sl=sl: e.dma_start(
                out=qw[:, sl * 2816:sl * 2816 + w], in_=src[r0:r0 + 128, c0:c0 + w]),
                waits=[("qabs", base_abs + i - 1), ("qbc", base_qbc + mi),
                       ("qdv", p.n("qdv"))],
                inc=qsem, inc_n=16)
            p.op("scalar", lambda e, w=w, sl=sl, i=i: e.activation(
                out=qw[:, sl * 2816:sl * 2816 + w], in_=qw[:, sl * 2816:sl * 2816 + w],
                func=AF.Abs, bias=0.0, scale=1.0, accum_out=strips[:, i:i + 1]),
                waits=[(qsem, p.n(qsem))], inc="qabs")
        p.op("vector", lambda e, nt=nt: e.tensor_reduce(
            out=strips[:, 0:1], in_=strips[:, 0:nt], axis=AX.X, op=ALU.add),
            waits=[("qabs", base_abs + nt)], inc="qrow")
        p.op("tensor", lambda e: e.matmul(pb[7][0:1, 0:1], strips[:, 0:1], onef[:, :],
             start=True, stop=True), waits=[("qrow", p.n("qrow"))], inc="qmm")
        nelem = float(rows * cols)
        p.op("vector", lambda e, nelem=nelem: e.tensor_scalar(
            out=gval[0:1, 0:1], in0=pb[7][0:1, 0:1],
            scalar1=0.5 / nelem, scalar2=0.5e-8, op0=ALU.mult, op1=ALU.add),
            waits=[("qmm", p.n("qmm"))], inc="qrow")
        p.op("tensor", lambda e: e.matmul(pb[6][:, 0:1], onefr[:, :], gval[0:1, 0:1],
             start=True, stop=True), waits=[("qrow", p.n("qrow"))], inc="qmm")
        p.op("vector", lambda e: e.tensor_copy(out=gval[:, 0:1], in_=pb[6][:, 0:1]),
             waits=[("qmm", p.n("qmm"))])
        p.op("vector", lambda e: e.tensor_scalar(
            out=gval[:, 1:2], in0=pb[6][:, 0:1], scalar1=-1.0, scalar2=None,
            op0=ALU.mult), inc="qbc")
        base_dv = p.n("qdv")
        for i, (r0, c0, w) in enumerate(tl):
            sl = i % 2
            qeng = "scalar" if sl == 0 else "sync"
            qsem = "qlA" if sl == 0 else "qlS"
            p.op(qeng, lambda e, src=src, r0=r0, c0=c0, w=w, sl=sl: e.dma_start(
                out=qw[:, sl * 2816:sl * 2816 + w], in_=src[r0:r0 + 128, c0:c0 + w]),
                waits=[("qbc", base_qbc + mi + 1), ("qdv", base_dv + 3 * (i - 1))],
                inc=qsem, inc_n=16)
            ld = p.n(qsem)
            p.op("vector", lambda e, w=w, sl=sl: e.tensor_scalar(
                out=qp[:, 0:w], in0=qw[:, sl * 2816:sl * 2816 + w],
                scalar1=gval[:, 0:1], scalar2=None, op0=ALU.is_ge),
                waits=[(qsem, ld)], inc="qdv")
            p.op("vector", lambda e, w=w, sl=sl: e.tensor_scalar(
                out=qn[:, 0:w], in0=qw[:, sl * 2816:sl * 2816 + w],
                scalar1=gval[:, 1:2], scalar2=None, op0=ALU.is_le), inc="qdv")
            p.op("vector", lambda e, w=w, sl=sl: e.tensor_tensor(
                out=qo[:, sl * 2816:sl * 2816 + w], in0=qp[:, 0:w], in1=qn[:, 0:w],
                op=ALU.subtract),
                waits=[(f"qst{sl}", p.n(f"qst{sl}"))], inc="qdv")
            p.op("gpsimd", lambda e, dst=dst, r0=r0, c0=c0, w=w, sl=sl: e.dma_start(
                out=dst[r0:r0 + 128, c0:c0 + w], in_=qo[:, sl * 2816:sl * 2816 + w]),
                waits=[("qdv", p.n("qdv"))], inc=f"qst{sl}", inc_n=16)
    p.wait("sync", [("qst0", p.n("qst0")), ("qst1", p.n("qst1"))])
    es2.close()




def emit_quant_chunks(p, sb, bank, specs):
    QWW = 1408
    qw = sb("qw", [128, 2 * QWW], F32)
    qp = sb("qp", [128, QWW], BF16)
    qn = sb("qn", [128, QWW], BF16)
    qo = sb("qo", [128, 2 * QWW], BF16)
    strips = sb("strips", [128, 192], F32)
    onef = sb("onef", [128, 1], F32)
    onefrq = sb("onefrq", [1, 128], F32)
    gval = sb("gval", [128, 4], F32)
    chunks = []

    def setup():
        p.op("vector", lambda e: e.memset(onef[:, :], 1.0))
        p.op("vector", lambda e: e.memset(onefrq[:, :], 1.0), inc="qrow")
    chunks.append(setup)
    base_qbc = [0]

    def tiles_of(rows, cols):
        w = QWW if cols % QWW == 0 else 1024
        return [(rc * 128, cc * w, w) for rc in range(rows // 128)
                for cc in range(cols // w)]

    for mi, (srcw, dst, rows, cols) in enumerate(specs):
        tl = tiles_of(rows, cols)
        nt = len(tl)
        st0 = [0]

        def mk_passA(i, r0, c0, w, sl, srcw=None, mi=mi, st0=st0):
            def f(srcw=srcw, i=i, r0=r0, c0=c0, w=w, sl=sl, mi=mi):
                if i == 0:
                    st0[0] = p.n("qabs")
                qeng = "scalar" if sl == 0 else "sync"
                qsem = "qlA" if sl == 0 else "qlS"
                p.op(qeng, lambda e, srcw=srcw, r0=r0, c0=c0, w=w, sl=sl: e.dma_start(
                    out=qw[:, sl * QWW:sl * QWW + w], in_=srcw[r0:r0 + 128, c0:c0 + w]),
                    waits=[("qabs", st0[0] + i - 1), ("qbc", base_qbc[0] + mi),
                           ("qdv", p.n("qdv"))],
                    inc=qsem, inc_n=16)
                p.op("scalar", lambda e, w=w, sl=sl, i=i: e.activation(
                    out=qw[:, sl * QWW:sl * QWW + w], in_=qw[:, sl * QWW:sl * QWW + w],
                    func=AF.Abs, bias=0.0, scale=1.0, accum_out=strips[:, i:i + 1]),
                    waits=[(qsem, p.n(qsem))], inc="qabs")
            return f
        for i, (r0, c0, w) in enumerate(tl):
            chunks.append(mk_passA(i, r0, c0, w, i % 2, srcw=srcw))

        def gamma(nt=nt, rows=rows, cols=cols, st0=st0):
            p.op("vector", lambda e, nt=nt: e.tensor_reduce(
                out=strips[:, 0:1], in_=strips[:, 0:nt], axis=AX.X, op=ALU.add),
                waits=[("qabs", st0[0] + nt)], inc="qrow")
            p.op("tensor", lambda e: e.matmul(bank[0:1, 0:1], strips[:, 0:1], onef[:, :],
                 start=True, stop=True), waits=[("qrow", p.n("qrow"))], inc="qmm")
            nelem = float(rows * cols)
            p.op("vector", lambda e, nelem=nelem: e.tensor_scalar(
                out=gval[0:1, 0:1], in0=bank[0:1, 0:1],
                scalar1=0.5 / nelem, scalar2=0.5e-8, op0=ALU.mult, op1=ALU.add),
                waits=[("qmm", p.n("qmm"))], inc="qrow")
            p.op("tensor", lambda e: e.matmul(bank[:, 0:1], onefrq[:, :], gval[0:1, 0:1],
                 start=True, stop=True), waits=[("qrow", p.n("qrow"))], inc="qmm")
            p.op("vector", lambda e: e.tensor_copy(out=gval[:, 0:1], in_=bank[:, 0:1]),
                 waits=[("qmm", p.n("qmm"))])
            p.op("vector", lambda e: e.tensor_scalar(
                out=gval[:, 1:2], in0=bank[:, 0:1], scalar1=-1.0, scalar2=None,
                op0=ALU.mult), inc="qbc")
        chunks.append(gamma)
        dv0 = [0]

        def mk_passB(i, r0, c0, w, sl, srcw=None, dst=None, mi=mi, dv0=dv0):
            def f(srcw=srcw, dst=dst, i=i, r0=r0, c0=c0, w=w, sl=sl, mi=mi):
                if i == 0:
                    dv0[0] = p.n("qdv")
                qeng = "scalar" if sl == 0 else "sync"
                qsem = "qlA" if sl == 0 else "qlS"
                p.op(qeng, lambda e, srcw=srcw, r0=r0, c0=c0, w=w, sl=sl: e.dma_start(
                    out=qw[:, sl * QWW:sl * QWW + w], in_=srcw[r0:r0 + 128, c0:c0 + w]),
                    waits=[("qbc", base_qbc[0] + mi + 1), ("qdv", dv0[0] + 3 * (i - 1))],
                    inc=qsem, inc_n=16)
                ld = p.n(qsem)
                p.op("vector", lambda e, w=w, sl=sl: e.tensor_scalar(
                    out=qp[:, 0:w], in0=qw[:, sl * QWW:sl * QWW + w],
                    scalar1=gval[:, 0:1], scalar2=None, op0=ALU.is_ge),
                    waits=[(qsem, ld)], inc="qdv")
                p.op("vector", lambda e, w=w, sl=sl: e.tensor_scalar(
                    out=qn[:, 0:w], in0=qw[:, sl * QWW:sl * QWW + w],
                    scalar1=gval[:, 1:2], scalar2=None, op0=ALU.is_le), inc="qdv")
                p.op("vector", lambda e, w=w, sl=sl: e.tensor_tensor(
                    out=qo[:, sl * QWW:sl * QWW + w], in0=qp[:, 0:w], in1=qn[:, 0:w],
                    op=ALU.subtract),
                    waits=[(f"qst{sl}", p.n(f"qst{sl}"))], inc="qdv")
                p.op("gpsimd", lambda e, dst=dst, r0=r0, c0=c0, w=w, sl=sl: e.dma_start(
                    out=dst[r0:r0 + 128, c0:c0 + w], in_=qo[:, sl * QWW:sl * QWW + w]),
                    waits=[("qdv", p.n("qdv"))], inc=f"qst{sl}", inc_n=16)
            return f
        for i, (r0, c0, w) in enumerate(tl):
            chunks.append(mk_passB(i, r0, c0, w, i % 2, srcw=srcw, dst=dst))
    return chunks



def emit_ffn_body(p, layer, xT, yT, W, pb, S=None):
    es = ExitStack()

    def sb(nm, shape, dt):
        return es.enter_context(p.nc.sbuf_tensor(p.uname(nm), list(shape), dt))
    xf = sb("xf", [128, 16 * 512], F32)
    xs = sb("xs", [128, 2 * 512], BF16)
    xtmp = sb("xtmp", [128, 512], F32)
    xq = xtmp[:, :].bitcast(BF16)
    xl = sb("xl", [128, DC * 1024], BF16)
    hh = sb("hh", [128, NJC * 1024], BF16)
    wgb = sb("wgb", [128, 2 * DC * 256], BF16)
    wub = sb("wub", [128, 2 * DC * 256], BF16)
    wdb = sb("wdb", [128, 2 * 4 * 512], BF16)
    wdbf = wdb[:, :].bitcast(F32)
    gsb = sb("gsb", [128, 2 * 512], BF16)
    usb = sb("usb", [128, 2 * 512], BF16)
    xsf = xs[:, :].bitcast(F32)
    yv = xf[:, 8 * 512:10 * 512]
    rows = sb("rows", [1, 2 * 512], F32)
    brow = sb("brow", [1, 2 * 512], BF16)
    onec = sb("onec", [128, 1], BF16)
    oner = sb("oner", [1, 128], BF16)

    if layer == 0:
        wsrc_g, wsrc_u, wsrc_d = W["gwT"], W["uwT"], W["dwT"]
        wq_eng = "gpsimd"
        sgs, sus, sds = S["sgs"], S["sus"], S["sds"]
        s_ready = S["ready"]
    else:
        wsrc_g, wsrc_u, wsrc_d = W["w1q"], W["w2q"], W["w3q"]
        wq_eng = "sync"
        s_ready = []

    p.op("vector", lambda e: e.memset(onec[:, :], 1.0))
    p.op("vector", lambda e: e.memset(oner[:, :], 1.0), inc="init")

    b_dn = p.n("rel_dn")
    b_gmm = p.n("gu_mm")
    b_dmm = p.n("dn_mm")
    b_hok = p.n("h_ok")
    ln_ts = [0, 0]

    for tb in range(2):
        t0 = tb * 1024
        for tsub in range(2):
            ts4 = tb * 2 + tsub
            tc0 = ts4 * 512
            base_mm = p.n("ln_mm")
            base_row = p.n("ln_row")
            base_act = p.n("ln_act")
            base_bc = p.n("ln_bc")
            p.op("sync", lambda e, tc0=tc0: e.dma_start(
                out=_r(xf[:, :], "p (c t) -> p c t", t=512),
                in_=_r(xT[:, tc0:tc0 + 512], "(c p) t -> p c t", p=128)),
                waits=[("ln_ap", p.n("ln_ap")), ("rel_dn", b_dn + 32 * tb),
                       ("sst0", p.n("sst0")), ("sst1", p.n("sst1"))],
                inc="xload", inc_n=16)
            xld = p.n("xload")
            for dc in range(DC):
                sl = dc % 2
                p.op("vector", lambda e, dc=dc, sl=sl: e.tensor_copy(
                    out=xq[:, sl * 512:(sl + 1) * 512], in_=xf[:, dc * 512:(dc + 1) * 512]),
                    waits=[("xload", xld), ("ln_mm", base_mm + dc - 1)], inc="ln_xq")
                p.op("scalar", lambda e, dc=dc, sl=sl: e.activation(
                    out=xs[:, sl * 512:(sl + 1) * 512], in_=xf[:, dc * 512:(dc + 1) * 512],
                    func=AF.Square, bias=0.0, scale=1.0),
                    waits=[("xload", xld), ("ln_mm", base_mm + dc - 1)], inc="ln_xs")
                w0 = [("ln_xq", p.n("ln_xq")), ("ln_xs", p.n("ln_xs"))]
                if dc == 0:
                    w0 += [("ln_row", base_row), ("rel_dn", b_dn + 32 * tb),
                           ("init", p.n("init"))]
                p.op("tensor", lambda e, sl=sl, dc=dc: e.matmul(
                    pb[0][0:1, :], onec[:, :], xq[:, sl * 512:(sl + 1) * 512],
                    start=(dc == 0), stop=(dc == DC - 1)), waits=w0)
                p.op("tensor", lambda e, sl=sl, dc=dc: e.matmul(
                    pb[1][0:1, :], onec[:, :], xs[:, sl * 512:(sl + 1) * 512],
                    start=(dc == 0), stop=(dc == DC - 1)), inc="ln_mm")
            m_, msq = rows[0:1, 0:512], rows[0:1, 512:1024]
            var, lnv = rows[0:1, 0:512], rows[0:1, 512:1024]
            bm, br = brow[0:1, 0:512], brow[0:1, 512:1024]
            p.op("vector", lambda e: e.tensor_scalar(out=m_, in0=pb[0][0:1, :],
                 scalar1=INVD, scalar2=None, op0=ALU.mult),
                 waits=[("ln_mm", base_mm + DC), ("ln_act", p.n("ln_act"))])
            p.op("vector", lambda e: e.tensor_copy(out=bm, in_=m_), inc="ln_row")
            p.op("vector", lambda e: e.tensor_tensor(out=msq, in0=m_, in1=m_, op=ALU.mult))
            p.op("vector", lambda e: e.scalar_tensor_tensor(out=var, in0=pb[1][0:1, :],
                 scalar=INVD, in1=msq, op0=ALU.mult, op1=ALU.subtract), inc="ln_row")
            p.op("scalar", lambda e: e.activation(out=lnv, in_=var, func=AF.Ln,
                 bias=EPS_LN, scale=1.0),
                 waits=[("ln_row", base_row + 2), ("init", 1)])
            p.op("scalar", lambda e: e.activation(out=br, in_=lnv, func=AF.Exp,
                 bias=0.0, scale=-0.5), inc="ln_act")
            p.op("tensor", lambda e: e.matmul(pb[2][:, :], oner[:, :], bm, start=True, stop=True),
                 waits=[("ln_row", base_row + 2), ("ln_ap", p.n("ln_ap"))], inc="ln_bc")
            p.op("tensor", lambda e: e.matmul(pb[3][:, :], oner[:, :], br, start=True, stop=True),
                 waits=[("ln_act", base_act + 1)], inc="ln_bc")
            mb_sb = xsf[:, 0:512]
            rb_sb = wdbf[:, 0:512]
            pslot = [xtmp[:, :], wdbf[:, 512:1024]]
            p.op("vector", lambda e: e.tensor_copy(out=mb_sb, in_=pb[2][:, :]),
                 waits=[("ln_bc", base_bc + 2)])
            p.op("vector", lambda e: e.tensor_copy(out=rb_sb, in_=pb[3][:, :]),
                 inc="ln_ev")
            base_pl = p.n("ln_pl")
            evv = p.n("ln_ev")
            for dc in range(DC):
                p.op("gpsimd", lambda e, dc=dc, sl=pslot[dc % 2]: e.tensor_tensor(
                    out=sl, in0=xf[:, dc * 512:(dc + 1) * 512], in1=mb_sb,
                    op=ALU.subtract),
                    waits=[("ln_ev", evv), ("ln_ap", p.n("ln_ap") - 1),
                           ("gu_mm", b_gmm + 44 * tb)], inc="ln_pl")
                p.op("vector", lambda e, dc=dc, sl=pslot[dc % 2], off=dc * 1024 + tsub * 512: e.tensor_tensor(
                    out=xl[:, off:off + 512], in0=sl, in1=rb_sb,
                    op=ALU.mult), waits=[("ln_pl", base_pl + dc + 1)], inc="ln_ap")
            ln_ts[ts4 % 2] = p.n("ln_ap")
            p.wait("tensor", [("ln_ap", p.n("ln_ap"))])

        base_gu = p.n("gu_mm")
        base_gug = p.n("gu_g")
        base_rg = p.n("rel_g")
        base_ru = p.n("rel_u")
        base_wf = p.n("wfix")
        base_h = p.n("h_ok")
        ln_done = p.n("ln_ap")
        for jp in range(NJC // 2):
          buf = jp % 2
          wsem = f"wld{buf}"
          woff = buf * DC * 256
          c0 = jp * 256
          p.op(wq_eng, lambda e, c0=c0, woff=woff: e.dma_start(
              out=_r(wgb[:, woff:woff + DC * 256], "p (c j) -> p c j", j=256),
              in_=_r(wsrc_g[:, c0:c0 + 256], "(c p) j -> p c j", p=128)),
              waits=[("gu_mm", max(base_gu + 2 * jp - 2, base_gu))], inc=wsem, inc_n=16)
          p.op(wq_eng, lambda e, c0=c0, woff=woff: e.dma_start(
              out=_r(wub[:, woff:woff + DC * 256], "p (c j) -> p c j", j=256),
              in_=_r(wsrc_u[:, c0:c0 + 256], "(c p) j -> p c j", p=128)),
              inc=wsem, inc_n=16)
          wld = p.n(wsem)
          if layer == 0:
              p.op("vector", lambda e, woff=woff: e.tensor_scalar(
                  out=wgb[:, woff:woff + DC * 256], in0=wgb[:, woff:woff + DC * 256],
                  scalar1=-128.0, scalar2=None, op0=ALU.add),
                  waits=[(wsem, wld)], inc="wfix")
              p.op("vector", lambda e, woff=woff: e.tensor_scalar(
                  out=wub[:, woff:woff + DC * 256], in0=wub[:, woff:woff + DC * 256],
                  scalar1=-128.0, scalar2=None, op0=ALU.add), inc="wfix")
          for jcl in range(2):
            jc = 2 * jp + jcl
            jb = jc % 2
            gb = [pb[0 + 4 * jb], pb[1 + 4 * jb]]
            ub = [pb[2 + 4 * jb], pb[3 + 4 * jb]]
            w0 = [("rel_g", base_rg + 2 * (jc - 1)),
                  ("rel_u", base_ru + 2 * (jc - 1))]
            if layer == 0:
                w0.append(("wfix", base_wf + 2 * (jp + 1)))
            else:
                w0.append((wsem, wld))
            if jc == 0:
                w0.append(("rel_dn", b_dn + 32 * tb))
            for which, banks, wbuf in (("g", gb, wgb), ("u", ub, wub)):
                for tsub in range(2):
                    w0t = w0 + [("ln_ap", ln_ts[tsub])]
                    for dc in range(DC):
                        p.op("tensor", lambda e, banks=banks, tsub=tsub, dc=dc, wbuf=wbuf, woff=woff, jcl=jcl: e.matmul(
                            banks[tsub][:, :],
                            wbuf[:, woff + dc * 256 + jcl * 128:woff + dc * 256 + jcl * 128 + 128],
                            xl[:, dc * 1024 + tsub * 512:dc * 1024 + tsub * 512 + 512],
                            start=(dc == 0), stop=(dc == DC - 1)),
                            waits=(w0t if dc == 0 else ()),
                            inc=("gu_g" if (which == "g" and dc == DC - 1) else
                                 ("gu_mm" if (which == "u" and tsub == 1 and dc == DC - 1) else None)))
            for tsub in range(2):
                gslot = gsb[:, tsub * 512:(tsub + 1) * 512]
                uslot = usb[:, tsub * 512:(tsub + 1) * 512]
                if layer == 0:
                    p.op("scalar", lambda e, gslot=gslot, b=gb[tsub], jc=jc: e.activation(
                        out=gslot, in_=b[:, :], func=AF.Silu, bias=0.0,
                        scale=sgs[:, jc:jc + 1]),
                        waits=[("gu_g", base_gug + 2 * jc + tsub + 1),
                               ("h_ok", base_h + 2 * jc + tsub - 1)] + s_ready,
                        inc="rel_g")
                    p.op("vector", lambda e, uslot=uslot, b=ub[tsub], jc=jc: e.tensor_scalar(
                        out=uslot, in0=b[:, :], scalar1=sus[:, jc:jc + 1], scalar2=None,
                        op0=ALU.mult), waits=[("gu_mm", base_gu + jc + 1)], inc="rel_u")
                else:
                    p.op("scalar", lambda e, gslot=gslot, b=gb[tsub]: e.activation(
                        out=gslot, in_=b[:, :], func=AF.Sigmoid, bias=0.0, scale=1.0),
                        waits=[("gu_g", base_gug + 2 * jc + tsub + 1),
                               ("h_ok", base_h + 2 * jc + tsub - 1)], inc="rel_g")
                    p.op("vector", lambda e, uslot=uslot, b=ub[tsub]: e.tensor_copy(
                        out=uslot, in_=b[:, :]),
                        waits=[("gu_mm", base_gu + jc + 1)], inc="rel_u")
                p.op("vector", lambda e, gslot=gslot, uslot=uslot, off=jc * 1024 + tsub * 512: e.tensor_tensor(
                    out=hh[:, off:off + 512], in0=gslot, in1=uslot, op=ALU.mult),
                    waits=[("rel_g", base_rg + 2 * jc + tsub + 1),
                           ("dn_mm", b_dmm + 176 * tb)], inc="h_ok")

        for ocg in range(4):
            base_dn = p.n("dn_mm")
            for i in range(8):
                oc = ocg * 4 + i // 2
                tss = i % 2
                p.op("sync", lambda e, oc=oc, tss=tss, i=i, t0=t0: e.dma_start(
                    out=xf[:, i * 512:(i + 1) * 512],
                    in_=xT[oc * 128:(oc + 1) * 128, t0 + tss * 512:t0 + tss * 512 + 512]),
                    waits=[("ln_ap", ln_done), ("rel_dn", b_dn + 8 * (4 * tb + ocg))],
                    inc="xload", inc_n=16)
            xld = p.n("xload")
            for jg in range(NJC // 4):
              buf = jg % 2
              wsem = f"wdl{buf}"
              c0 = ocg * 512
              p.op(wq_eng, lambda e, jg=jg, c0=c0, buf=buf: e.dma_start(
                  out=_r(wdb[:, buf * 2048:(buf + 1) * 2048], "p (j c) -> p j c", c=512),
                  in_=_r(wsrc_d[jg * 512:(jg + 1) * 512, c0:c0 + 512], "(j p) c -> p j c", p=128)),
                  waits=[("dn_mm", max(base_dn + 4 * jg - 4, base_dn)),
                         ("ln_ap", ln_done)], inc=wsem, inc_n=16)
              wld = p.n(wsem)
              if layer == 0:
                  p.op("vector", lambda e, buf=buf: e.tensor_scalar(
                      out=wdb[:, buf * 2048:(buf + 1) * 2048],
                      in0=wdb[:, buf * 2048:(buf + 1) * 2048],
                      scalar1=-128.0, scalar2=None, op0=ALU.add),
                      waits=[(wsem, wld)], inc="wdfix")
                  wfx = p.n("wdfix")
              for jcl in range(4):
                jc = 4 * jg + jcl
                if layer == 0:
                    w0 = [("wdfix", wfx)]
                else:
                    w0 = [(wsem, wld)]
                if jc == 0:
                    w0 = w0 + [("rel_dn", b_dn + 8 * (4 * tb + ocg)),
                               ("h_ok", b_hok + 88 * (tb + 1))]
                for oc4 in range(4):
                    for tss in range(2):
                        p.op("tensor", lambda e, oc4=oc4, tss=tss, jc=jc, buf=buf, jcl=jcl: e.matmul(
                            pb[oc4 * 2 + tss][:, :],
                            wdb[:, buf * 2048 + jcl * 512 + oc4 * 128:buf * 2048 + jcl * 512 + oc4 * 128 + 128],
                            hh[:, jc * 1024 + tss * 512:jc * 1024 + tss * 512 + 512],
                            start=(jc == 0), stop=(jc == NJC - 1)),
                            waits=(w0 if (oc4 == 0 and tss == 0) else ()),
                            inc=("dn_mm" if (oc4 == 3 and tss == 1) else None))
            mm_dn = p.n("dn_mm")
            for i in range(8):
                oc = ocg * 4 + i // 2
                tss = i % 2
                slot = i % 2
                sa = (lambda oc=oc: sds[:, oc:oc + 1])() if layer == 0 else 0.5
                ssem = f"sst{slot}"
                p.op("vector", lambda e, i=i, slot=slot, sa=sa: e.scalar_tensor_tensor(
                    out=yv[:, slot * 512:(slot + 1) * 512], in0=pb[i][:, :],
                    scalar=sa, in1=xf[:, i * 512:(i + 1) * 512],
                    op0=ALU.mult, op1=ALU.add),
                    waits=[("dn_mm", mm_dn), ("xload", xld), (ssem, p.n(ssem))],
                    inc="rel_dn")
                p.op("scalar", lambda e, oc=oc, tss=tss, slot=slot, t0=t0: e.dma_start(
                    out=yT[oc * 128:(oc + 1) * 128, t0 + tss * 512:t0 + tss * 512 + 512],
                    in_=yv[:, slot * 512:(slot + 1) * 512]),
                    waits=[("rel_dn", p.n("rel_dn"))], inc=ssem, inc_n=16)
    es.close()



def emit_attn_body(p, layer, xT_own, yT, W, pbq, fused, cc_tag="", quant_specs=None):
    pb, pbquad = pbq
    nh = NH
    hk = 2048 if layer == 0 else 512
    hkc = hk // 128
    rope = (layer == 0)
    alibi = (layer == 1)
    groups = [[h] for h in range(nh)] if layer == 0 else \
             [[g, g + 4, g + 8, g + 12] for g in range(4)]

    wq, wo = W["wq"], W["wo"]
    maskpat = W["maskpat"]
    ksrc = W["ksrc"]
    vsrc = W["vsrc"]

    if fused:
        kvKin = p.dram(f"kvKin{cc_tag}", [hk, T], BF16)
        kvVin = p.dram(f"kvVin{cc_tag}", [16, hkc, 128, 128], BF16)
        kvKout = p.dram(f"kvKout{cc_tag}", [2, hk, T], BF16)
        kvVout = p.dram(f"kvVout{cc_tag}", [2, 16, hkc, 128, 128], BF16)
    else:
        kvKout = p.dram(f"kvK{cc_tag}", [2, hk, T], BF16)
        kvVout = p.dram(f"kvV{cc_tag}", [2, 16, hkc, 128, 128], BF16)
    ctxd = p.dram(f"ctxd{cc_tag}", [D, T], BF16)

    esQ = ExitStack()
    QT = esQ.enter_context(p.nc.sbuf_tensor(p.uname("QT"), [128, nh * T], BF16))

    esA = ExitStack()

    def sbA(nm, shape, dt):
        return esA.enter_context(p.nc.sbuf_tensor(p.uname(nm), list(shape), dt))
    xf = sbA("xf", [128, 16 * 512], F32)
    xs = sbA("xs", [128, 2 * 512], BF16)
    xtmp = sbA("xtmp", [128, 512], F32)
    xq = xtmp[:, :].bitcast(BF16)
    xl = sbA("xl", [128, DC * T], BF16)
    wkb = sbA("wkb", [128, DC * 128], BF16)
    wvb = sbA("wvb", [128, DC * 256], BF16)
    ksg = sbA("ksg", [128, 2 * 512], BF16)
    vsg = sbA("vsg", [128, 2 * 512], BF16)
    rows = sbA("rows", [1, 2 * 512], F32)
    brow = sbA("brow", [1, 2 * 512], BF16)
    onec = sbA("onecA", [128, 1], BF16)
    oner = sbA("onerA", [1, 128], BF16)
    rta = sbA("rta", [64, 512], F32)
    rtb = sbA("rtb", [64, 512], F32)
    if rope:
        csb = sbA("csb", [64, 2 * T], BF16)

    p.op("vector", lambda e: e.memset(onec[:, :], 1.0))
    p.op("vector", lambda e: e.memset(oner[:, :], 1.0), inc="init")

    def ln_tsub(src, ts4):
        tc0 = ts4 * 512
        base_mm = p.n("ln_mm")
        base_row = p.n("ln_row")
        base_act = p.n("ln_act")
        base_bc = p.n("ln_bc")
        p.op("sync", lambda e, src=src, tc0=tc0: e.dma_start(
            out=_r(xf[:, :], "p (c t) -> p c t", t=512),
            in_=_r(src[:, tc0:tc0 + 512], "(c p) t -> p c t", p=128)),
            waits=[("ln_ap", p.n("ln_ap"))], inc="xload", inc_n=16)
        xld = p.n("xload")
        for dc in range(DC):
            sl = dc % 2
            p.op("vector", lambda e, dc=dc, sl=sl: e.tensor_copy(
                out=xq[:, sl * 512:(sl + 1) * 512], in_=xf[:, dc * 512:(dc + 1) * 512]),
                waits=[("xload", xld), ("ln_mm", base_mm + dc - 1)], inc="ln_xq")
            p.op("scalar", lambda e, dc=dc, sl=sl: e.activation(
                out=xs[:, sl * 512:(sl + 1) * 512], in_=xf[:, dc * 512:(dc + 1) * 512],
                func=AF.Square, bias=0.0, scale=1.0),
                waits=[("xload", xld), ("ln_mm", base_mm + dc - 1)], inc="ln_xs")
            w0 = [("ln_xq", p.n("ln_xq")), ("ln_xs", p.n("ln_xs"))]
            if dc == 0:
                w0 += [("ln_row", base_row), ("init", p.n("init"))]
            p.op("tensor", lambda e, sl=sl, dc=dc: e.matmul(
                pb[0][0:1, :], onec[:, :], xq[:, sl * 512:(sl + 1) * 512],
                start=(dc == 0), stop=(dc == DC - 1)), waits=w0)
            p.op("tensor", lambda e, sl=sl, dc=dc: e.matmul(
                pb[1][0:1, :], onec[:, :], xs[:, sl * 512:(sl + 1) * 512],
                start=(dc == 0), stop=(dc == DC - 1)), inc="ln_mm")
        m_, msq = rows[0:1, 0:512], rows[0:1, 512:1024]
        var, lnv = rows[0:1, 0:512], rows[0:1, 512:1024]
        bm, br = brow[0:1, 0:512], brow[0:1, 512:1024]
        p.op("vector", lambda e: e.tensor_scalar(out=m_, in0=pb[0][0:1, :],
             scalar1=INVD, scalar2=None, op0=ALU.mult),
             waits=[("ln_mm", base_mm + DC), ("ln_act", p.n("ln_act"))])
        p.op("vector", lambda e: e.tensor_copy(out=bm, in_=m_), inc="ln_row")
        p.op("vector", lambda e: e.tensor_tensor(out=msq, in0=m_, in1=m_, op=ALU.mult))
        p.op("vector", lambda e: e.scalar_tensor_tensor(out=var, in0=pb[1][0:1, :],
             scalar=INVD, in1=msq, op0=ALU.mult, op1=ALU.subtract), inc="ln_row")
        p.op("scalar", lambda e: e.activation(out=lnv, in_=var, func=AF.Ln,
             bias=EPS_LN, scale=1.0),
             waits=[("ln_row", base_row + 2), ("init", 1)])
        p.op("scalar", lambda e: e.activation(out=br, in_=lnv, func=AF.Exp,
             bias=0.0, scale=-0.5), inc="ln_act")
        p.op("tensor", lambda e: e.matmul(pb[2][:, :], oner[:, :], bm, start=True, stop=True),
             waits=[("ln_row", base_row + 2), ("ln_ap", p.n("ln_ap"))], inc="ln_bc")
        p.op("tensor", lambda e: e.matmul(pb[3][:, :], oner[:, :], br, start=True, stop=True),
             waits=[("ln_act", base_act + 1)], inc="ln_bc")
        for dc in range(DC):
            p.op("vector", lambda e, dc=dc: e.tensor_tensor(
                out=xtmp[:, :], in0=xf[:, dc * 512:(dc + 1) * 512], in1=pb[2][:, :],
                op=ALU.subtract),
                waits=[("ln_bc", base_bc + 2), ("ap_mm", p.n("ap_mm"))])
            p.op("vector", lambda e, dc=dc, off=dc * T + ts4 * 512: e.tensor_tensor(
                out=xl[:, off:off + 512], in0=xtmp[:, :], in1=pb[3][:, :],
                op=ALU.mult), inc="ln_ap")
        p.wait("tensor", [("ln_ap", p.n("ln_ap"))])

    def rope_evac(bank, csoff, dst_ap_fn, extra_waits, sem_inc):
        C = (lambda: csb[0:64, csoff:csoff + 512])()
        S_ = (lambda: csb[0:64, T + csoff:T + csoff + 512])()
        p.op("vector", lambda e, bank=bank, C=C: e.tensor_tensor(
            out=rta[:, :], in0=bank[0:64, :], in1=C, op=ALU.mult), waits=extra_waits)
        p.op("vector", lambda e, bank=bank, S_=S_: e.tensor_tensor(
            out=rtb[:, :], in0=bank[64:128, :], in1=S_, op=ALU.mult))
        p.op("vector", lambda e: e.tensor_tensor(
            out=dst_ap_fn(0), in0=rta[:, :], in1=rtb[:, :], op=ALU.subtract))
        p.op("vector", lambda e, bank=bank, C=C: e.tensor_tensor(
            out=rta[:, :], in0=bank[64:128, :], in1=C, op=ALU.mult))
        p.op("vector", lambda e, bank=bank, S_=S_: e.tensor_tensor(
            out=rtb[:, :], in0=bank[0:64, :], in1=S_, op=ALU.mult))
        p.op("vector", lambda e: e.tensor_tensor(
            out=dst_ap_fn(1), in0=rta[:, :], in1=rtb[:, :], op=ALU.add), inc=sem_inc)

    def kv_proj(kdst, vdst):
        ln_done = p.n("ln_ap")
        for kc in range(hkc):
            p.op("gpsimd", lambda e, kc=kc: e.dma_start(
                out=_r(wkb[:, :], "p (c j) -> p c j", j=128),
                in_=_r(ksrc(kc), "(c p) j -> p c j", p=128)),
                waits=[("ap_mm", p.n("ap_mm"))], inc="wpld", inc_n=16)
            wld = p.n("wpld")
            for ts4 in range(NS):
                ei = kc * NS + ts4
                bank = pb[4 + ei % 2]
                for dc in range(DC):
                    w0 = []
                    if dc == 0:
                        w0 = [("wpld", wld), ("ln_ap", ln_done),
                              ("ap_ev", p.n("ap_ev") - 1)]
                    p.op("tensor", lambda e, bank=bank, dc=dc, ts4=ts4: e.matmul(
                        bank[:, :], wkb[:, dc * 128:(dc + 1) * 128],
                        xl[:, dc * T + ts4 * 512:dc * T + ts4 * 512 + 512],
                        start=(dc == 0), stop=(dc == DC - 1)),
                        waits=w0, inc=("ap_mm" if dc == DC - 1 else None))
                kpar = ei % 2
                ksl = (lambda kpar=kpar: ksg[:, kpar * 512:(kpar + 1) * 512])()
                ew = [("ap_mm", p.n("ap_mm")), (f"kst{kpar}", p.n(f"kst{kpar}"))]
                if rope:
                    ew.append(("csld", p.n("csld")))
                    rope_evac(bank, ts4 * 512,
                              lambda half, ksl=ksl: ksl[half * 64:(half + 1) * 64, :],
                              ew, "ap_ev")
                else:
                    p.op("vector", lambda e, bank=bank, ksl=ksl: e.tensor_copy(
                        out=ksl, in_=bank[:, :]), waits=ew, inc="ap_ev")
                p.op("scalar", lambda e, kc=kc, ts4=ts4, ksl=ksl, kdst=kdst, kpar=kpar: e.dma_start(
                    out=kdst[kc * 128:(kc + 1) * 128, ts4 * 512:ts4 * 512 + 512],
                    in_=ksl),
                    waits=[("ap_ev", p.n("ap_ev"))], inc=f"kst{kpar}", inc_n=16)
        for vf in range(hk // 256):
            p.op("gpsimd", lambda e, vf=vf: e.dma_start(
                out=_r(wvb[:, :], "p (c j) -> p c j", j=256),
                in_=_r(vsrc(vf), "(c p) j -> p c j", p=128)),
                waits=[("ap_mm", p.n("ap_mm"))], inc="wpld", inc_n=16)
            wld = p.n("wpld")
            for tc in range(16):
                bank = pb[6 + tc % 2]
                for dc in range(DC):
                    w0 = []
                    if dc == 0:
                        w0 = [("wpld", wld), ("ln_ap", ln_done),
                              ("ap_ev2", p.n("ap_ev2") - 1)]
                    p.op("tensor", lambda e, bank=bank, dc=dc, tc=tc: e.matmul(
                        bank[:, 0:256], xl[:, dc * T + tc * 128:dc * T + tc * 128 + 128],
                        wvb[:, dc * 256:(dc + 1) * 256],
                        start=(dc == 0), stop=(dc == DC - 1)),
                        waits=w0, inc=("ap_mm" if dc == DC - 1 else None))
                vpar = tc % 2
                vsl = (lambda vpar=vpar: vsg[:, vpar * 512:vpar * 512 + 256])()
                p.op("vector", lambda e, bank=bank, vsl=vsl: e.tensor_copy(
                    out=vsl, in_=bank[:, 0:256]),
                    waits=[("ap_mm", p.n("ap_mm")), (f"vst{vpar}", p.n(f"vst{vpar}"))],
                    inc="ap_ev2")
                p.op("scalar", lambda e, tc=tc, vf=vf, vsl=vsl, vdst=vdst, vpar=vpar: e.dma_start(
                    out=_r(vdst[tc, vf * 2:(vf + 1) * 2, :, :], "v k h -> k v h"),
                    in_=_r(vsl, "k (v h) -> k v h", h=128)),
                    waits=[("ap_ev2", p.n("ap_ev2"))], inc=f"vst{vpar}", inc_n=16)

    def q_proj():
        ln_done = p.n("ln_ap")
        for h in range(nh):
            p.op("gpsimd", lambda e, h=h: e.dma_start(
                out=_r(wkb[:, :], "p (c j) -> p c j", j=128),
                in_=_r(wq[:, h * 128:(h + 1) * 128], "(c p) j -> p c j", p=128)),
                waits=[("ap_mm", p.n("ap_mm"))], inc="wpld", inc_n=16)
            wld = p.n("wpld")
            for ts4 in range(NS):
                bank = pb[4 + ts4 % 2]
                for dc in range(DC):
                    w0 = []
                    if dc == 0:
                        w0 = [("wpld", wld), ("ln_ap", ln_done),
                              ("ap_ev", p.n("ap_ev") - 1)]
                    p.op("tensor", lambda e, bank=bank, dc=dc, ts4=ts4: e.matmul(
                        bank[:, :], wkb[:, dc * 128:(dc + 1) * 128],
                        xl[:, dc * T + ts4 * 512:dc * T + ts4 * 512 + 512],
                        start=(dc == 0), stop=(dc == DC - 1)),
                        waits=w0, inc=("ap_mm" if dc == DC - 1 else None))
                qoff = h * T + ts4 * 512
                ew = [("ap_mm", p.n("ap_mm"))]
                if rope:
                    ew.append(("csld", p.n("csld")))
                    rope_evac(bank, ts4 * 512,
                              lambda half, qoff=qoff: QT[half * 64:(half + 1) * 64,
                                                         qoff:qoff + 512],
                              ew, "ap_ev")
                else:
                    p.op("vector", lambda e, bank=bank, qoff=qoff: e.tensor_copy(
                        out=QT[:, qoff:qoff + 512], in_=bank[:, :]),
                        waits=ew, inc="ap_ev")

    def load_cs(r):
        if not rope:
            return
        cosd, sind = W["cs"][r]
        p.op("gpsimd", lambda e, cosd=cosd: e.dma_start(out=csb[:, 0:T], in_=cosd[:, :]),
             waits=[("ap_ev", p.n("ap_ev")), ("ap_ev2", p.n("ap_ev2"))],
             inc="csld", inc_n=16)
        p.op("gpsimd", lambda e, sind=sind: e.dma_start(out=csb[:, T:2 * T], in_=sind[:, :]),
             inc="csld", inc_n=16)

    if fused:
        load_cs(2)
        for ts4 in range(NS):
            ln_tsub(xT_own, ts4)
        kv_proj(kvKin, kvVin)
        cw = [(f"kst{i}", p.n(f"kst{i}")) for i in range(2)] + \
             [(f"vst{i}", p.n(f"vst{i}")) for i in range(2)]
        p.op("gpsimd", lambda e: e.collective_compute(
            "AllGather", ALU.bypass,
            replica_groups=[[0, 1], [2, 3], [4, 5], [6, 7]],
            ins=[kvKin.ap().opt()],
            outs=[kvKout.ap().rearrange("r a b -> (r a) b").opt()]),
            waits=cw, inc="ccg")
        p.op("gpsimd", lambda e: e.collective_compute(
            "AllGather", ALU.bypass,
            replica_groups=[[0, 1], [2, 3], [4, 5], [6, 7]],
            ins=[kvVin.ap().rearrange("a b c d -> (a b c) d").opt()],
            outs=[kvVout.ap().rearrange("r a b c d -> (r a b c) d").opt()]),
            inc="ccg")
        ccv = p.n("ccg")
        q_proj()
    else:
        srcs = [W["xT_p0"], W["xT_p1"], xT_own]
        for r in range(2):
            load_cs(r)
            for ts4 in range(NS):
                ln_tsub(srcs[r], ts4)
            kv_proj(kvKout[r], kvVout[r])
        load_cs(2)
        for ts4 in range(NS):
            ln_tsub(srcs[2], ts4)
        q_proj()
        ccv = 0

    p.fence_all()
    esA.close()

    esB = ExitStack()

    def sbB(nm, shape, dt):
        return esB.enter_context(p.nc.sbuf_tensor(p.uname(nm), list(shape), dt))
    pat = sbB("pat", [128, 2 * MPS * QB], BF16)
    ksb = sbB("ksb", [128, 2 * 2 * T], BF16)
    vsb = sbB("vsb", [128, 2 * 2 * 32 * 128], BF16)
    esb = sbB("esb", [128, 2 * 2 * QB], BF16)
    onecB = sbB("onecB", [128, 1], BF16)
    onefr = sbB("onefr", [1, 128], F32)
    rrow = sbB("rrow", [1, 512], F32)
    lnrow = sbB("lnrow", [1, 512], F32)
    ctx_sb = sbB("ctx_sb", [128, 2 * 512], F32)
    den_sb = sbB("den_sb", [1, 2 * 512], F32)
    bcs = sbB("bcs", [128, 512], F32)
    csg = sbB("csg", [128, 2 * 512], BF16)
    if alibi:
        ba = sbB("ba", [6, 2 * T], BF16)
        bb = sbB("bb", [6, T], BF16)
    qchunks = []
    if quant_specs is not None:
        qchunks = emit_quant_chunks(p, sbB, pb[7], quant_specs)
    qci = [0]

    def dispense_quant(k):
        while qci[0] < min(len(qchunks), k):
            qchunks[qci[0]]()
            qci[0] += 1

    p.op("vector", lambda e: e.memset(onecB[:, :], 1.0))
    p.op("vector", lambda e: e.memset(onefr[:, :], 1.0), inc="binit")
    b_binit = p.n("binit")
    p.op("sync", lambda e: e.dma_start(
        out=_r(pat[:, :], "p (r d q) -> p r d q", d=MPS, q=QB),
        in_=_r(maskpat[:, :, :, :], "r d p q -> p r d q")),
        waits=[("ccg", ccv)], inc="pld", inc_n=16)
    b_pld = p.n("pld")

    kv_iters = [(gi, s) for gi in range(len(groups)) for s in range(NS)]
    pend_tail = [None]
    sc_after, mm_after = [], []
    kld_at, vld_at = {}, {}

    def emit_kv_load(j):
        gi, s = kv_iters[j]
        nm = MPS * (s + 1)
        par = j % 2
        kw = [("ccg", ccv)]
        if j >= 2:
            kw += [("at_sc", sc_after[j - 2]), ("at_mm", mm_after[j - 2])]
        for r01 in range(2):
            p.op("sync", lambda e, r01=r01, gi=gi, nm=nm, par=par: e.dma_start(
                out=ksb[:, (par * 2 + r01) * T:(par * 2 + r01) * T + nm * 128],
                in_=kvKout[r01, gi * 128:(gi + 1) * 128, 0:nm * 128]),
                waits=kw, inc=f"kld{par}", inc_n=16)
            p.op("sync", lambda e, r01=r01, gi=gi, nm=nm, par=par: e.dma_start(
                out=_r(vsb[:, (par * 2 + r01) * 4096:(par * 2 + r01) * 4096 + nm * 128],
                       "k (m h) -> k m h", h=128),
                in_=_r(kvVout[r01, 0:nm, gi, :, :], "m k h -> k m h")),
                waits=kw, inc=f"vld{par}", inc_n=16)
        kld_at[j] = p.n(f"kld{par}")
        vld_at[j] = p.n(f"vld{par}")

    emit_kv_load(0)
    for j, (gi, s) in enumerate(kv_iters):
        if j + 1 < len(kv_iters):
            emit_kv_load(j + 1)
        hs = groups[gi]
        nm = MPS * (s + 1)
        par = j % 2
        kldv, vldv = kld_at[j], vld_at[j]
        ksem, vsem = f"kld{par}", f"vld{par}"
        koff = par * 2 * T
        voff = par * 2 * 4096
        for h in hs:
            if alibi:
                p.op("gpsimd", lambda e, h=h: e.dma_start(
                    out=_r(ba[:, :], "b (r c) -> b r c", r=2),
                    in_=_r(W["baT"][h, :, :, :], "r b c -> b r c")),
                    waits=[("at_sc", p.n("at_sc"))], inc="bld", inc_n=16)
                p.op("gpsimd", lambda e, h=h: e.dma_start(
                    out=bb[:, :], in_=W["bbT"][h, :, :]),
                    inc="bld", inc_n=16)
            bldv = p.n("bld") if alibi else 0
            n_i = 2 * nm
            npk = n_i // 2
            tiles = [(r01, m) for r01 in range(2) for m in range(nm)]
            sc_pair, mn_pair = {}, {}

            def emit_scores_pair(pk):
                for t_i in range(2):
                    r01, m = tiles[2 * pk + t_i]
                    i_g = 2 * pk + t_i
                    si = i_g % 4
                    bank = pb[si]
                    w0 = [(ksem, kldv), ("at_ex", p.n("at_ex"))]
                    if alibi:
                        w0.append(("bld", bldv))
                    p.op("tensor", lambda e, bank=bank, r01=r01, m=m, h=h, s=s, koff=koff: e.matmul(
                        bank[:, :], ksb[:, koff + r01 * T + m * 128:koff + r01 * T + m * 128 + 128],
                        QT[:, h * T + s * 512:h * T + s * 512 + 512],
                        start=True, stop=(not alibi)),
                        waits=w0, inc=(None if alibi else "at_sc"))
                    if alibi:
                        p.op("tensor", lambda e, bank=bank, r01=r01, m=m, s=s: e.matmul(
                            bank[:, :], ba[:, r01 * T + m * 128:r01 * T + m * 128 + 128],
                            bb[:, s * 512:s * 512 + 512],
                            start=False, stop=True), inc="at_sc")
                    if m >= MPS * s:
                        d = m - MPS * s
                        p.op("vector", lambda e, bank=bank, r01=r01, d=d: e.tensor_tensor(
                            out=bank[:, :], in0=bank[:, :],
                            in1=pat[:, (r01 * MPS + d) * QB:(r01 * MPS + d + 1) * QB],
                            op=ALU.min),
                            waits=[("at_sc", p.n("at_sc")), ("pld", b_pld)], inc="at_mn")
                sc_pair[pk] = p.n("at_sc")
                mn_pair[pk] = p.n("at_mn")

            emit_scores_pair(0)
            if pend_tail[0] is not None:
                pend_tail[0]()
                pend_tail[0] = None
            for pk in range(npk):
                if pk + 1 < npk:
                    emit_scores_pair(pk + 1)
                ew = [("at_sc", sc_pair[pk]), ("at_mn", mn_pair[pk]),
                      ("at_mm", p.n("at_mm") - 2)]
                q0 = (2 * pk % 4) * 512
                eoff = (pk % 2) * 1024
                p.op("scalar", lambda e, q0=q0, eoff=eoff: e.activation(
                    out=esb[:, eoff:eoff + 1024], in_=pbquad[:, q0:q0 + 1024],
                    func=AF.Exp, bias=0.0, scale=RSQHD),
                    waits=ew, inc="at_ex")
                for t_i in range(2):
                    r01, m = tiles[2 * pk + t_i]
                    i_g = 2 * pk + t_i
                    cw2 = [("at_ex", p.n("at_ex")), (vsem, vldv)]
                    if i_g == 0:
                        cw2.append(("at_cx", p.n("at_cx")))
                    p.op("tensor", lambda e, r01=r01, m=m, st=(i_g == 0), sp=(i_g == n_i - 1),
                         eo=eoff + t_i * 512, voff=voff: e.matmul(
                        pb[4][:, :], vsb[:, voff + r01 * 4096 + m * 128:voff + r01 * 4096 + m * 128 + 128],
                        esb[:, eo:eo + 512],
                        start=st, stop=sp), waits=cw2)
                    ow = [("binit", b_binit)]
                    if i_g == 0:
                        ow.append(("at_rc", p.n("at_rc")))
                    p.op("tensor", lambda e, st=(i_g == 0), sp=(i_g == n_i - 1),
                         eo=eoff + t_i * 512: e.matmul(
                        pb[5][0:1, :], onecB[:, :], esb[:, eo:eo + 512],
                        start=st, stop=sp), waits=ow, inc="at_mm")
            hpar = p.n("at_cx") % 2
            p.op("vector", lambda e, hpar=hpar: e.tensor_copy(
                out=ctx_sb[:, hpar * 512:(hpar + 1) * 512], in_=pb[4][:, :]),
                waits=[("at_mm", p.n("at_mm")), ("at_nm", p.n("at_nm") - 1)],
                inc="at_cx")
            p.op("vector", lambda e, hpar=hpar: e.tensor_copy(
                out=den_sb[0:1, hpar * 512:(hpar + 1) * 512], in_=pb[5][0:1, :]),
                waits=[("at_rp", p.n("at_rp") - 1)], inc="at_rc")
            p.op("scalar", lambda e, hpar=hpar: e.activation(
                out=lnrow[:, :], in_=den_sb[0:1, hpar * 512:(hpar + 1) * 512],
                func=AF.Ln, bias=0.0, scale=1.0),
                waits=[("at_rc", p.n("at_rc")), ("at_bcm", p.n("at_bcm"))])
            p.op("scalar", lambda e: e.activation(
                out=rrow[:, :], in_=lnrow[:, :], func=AF.Exp, bias=0.0, scale=-1.0),
                inc="at_rp")

            def tail_post(h=h, s=s, hpar=hpar, rp=p.n("at_rp")):
                p.op("tensor", lambda e: e.matmul(pb[6][:, :], onefr[:, :], rrow[:, :],
                     start=True, stop=True),
                     waits=[("at_rp", rp), ("at_bc", p.n("at_bc"))], inc="at_bcm")
                p.op("vector", lambda e: e.tensor_copy(out=bcs[:, :], in_=pb[6][:, :]),
                     waits=[("at_bcm", p.n("at_bcm"))], inc="at_bc")
                cpar = p.n("at_nm") % 2
                p.op("vector", lambda e, cpar=cpar, hpar=hpar: e.tensor_tensor(
                    out=csg[:, cpar * 512:(cpar + 1) * 512],
                    in0=ctx_sb[:, hpar * 512:(hpar + 1) * 512],
                    in1=bcs[:, :], op=ALU.mult),
                    waits=[(f"cst{cpar}", p.n(f"cst{cpar}"))], inc="at_nm")
                p.op("gpsimd", lambda e, h=h, s=s, cpar=cpar: e.dma_start(
                    out=ctxd[h * 128:(h + 1) * 128, s * 512:s * 512 + 512],
                    in_=csg[:, cpar * 512:(cpar + 1) * 512]),
                    waits=[("at_nm", p.n("at_nm"))], inc=f"cst{cpar}", inc_n=16)
            pend_tail[0] = tail_post
            if qchunks:
                hsi = p.n("at_cx")
                dispense_quant(1 + (len(qchunks) * hsi) // (len(kv_iters) * len(hs)))
        sc_after.append(p.n("at_sc"))
        mm_after.append(p.n("at_mm"))
    if pend_tail[0] is not None:
        pend_tail[0]()
        pend_tail[0] = None
    dispense_quant(len(qchunks))

    p.fence_all()
    esB.close()
    esQ.close()

    esC = ExitStack()

    def sbC(nm, shape, dt):
        return esC.enter_context(p.nc.sbuf_tensor(p.uname(nm), list(shape), dt))
    cbuf = sbC("cbuf", [128, DC * T], BF16)
    wob = sbC("wob", [128, 2 * DC * 128], BF16)
    yvC = sbC("yvC", [128, 2 * 512], F32)
    xres = sbC("xres", [128, 2 * 512], F32)
    if layer == 1:
        boS = sbC("boS", [128, DC], F32)
        p.op("sync", lambda e: e.dma_start(out=boS[:, :], in_=W["boR"][:, :]),
             inc="cld", inc_n=16)
    p.op("sync", lambda e: e.dma_start(
        out=_r(cbuf[:, :], "p (c t) -> p c t", t=T),
        in_=_r(ctxd[:, :], "(c p) t -> p c t", p=128)), inc="cld", inc_n=16)
    cldv = p.n("cld")
    ec0 = p.n("oc_ev")
    ec = 0
    for oc in range(DC):
        opar = oc % 2
        wsemo = f"wold{opar}"
        woffo = opar * DC * 128
        p.op("gpsimd", lambda e, oc=oc, woffo=woffo: e.dma_start(
            out=_r(wob[:, woffo:woffo + DC * 128], "p (c j) -> p c j", j=128),
            in_=_r(wo[:, oc * 128:(oc + 1) * 128], "(c p) j -> p c j", p=128)),
            waits=[("oc_mm", max(0, p.n("oc_mm") - 4))], inc=wsemo, inc_n=16)
        wldv = p.n(wsemo)
        for ts4 in range(NS):
            bank = pb[ec % 4]
            for hc in range(DC):
                w0 = []
                if hc == 0:
                    w0 = [(wsemo, wldv), ("cld", cldv),
                          ("oc_ev", max(0, ec0 + ec - 3))]
                p.op("tensor", lambda e, bank=bank, hc=hc, ts4=ts4, woffo=woffo: e.matmul(
                    bank[:, :], wob[:, woffo + hc * 128:woffo + hc * 128 + 128],
                    cbuf[:, hc * T + ts4 * 512:hc * T + ts4 * 512 + 512],
                    start=(hc == 0), stop=(hc == DC - 1)),
                    waits=w0, inc=("oc_mm" if hc == DC - 1 else None))
            par = ec % 2
            p.op("sync", lambda e, oc=oc, ts4=ts4, par=par: e.dma_start(
                out=xres[:, par * 512:(par + 1) * 512],
                in_=xT_own[oc * 128:(oc + 1) * 128, ts4 * 512:ts4 * 512 + 512]),
                waits=[("oc_ev", max(0, ec0 + ec - 1))], inc=f"xrl{par}", inc_n=16)
            sa = (lambda oc=oc: boS[:, oc:oc + 1])() if layer == 1 else 0.0
            p.op("vector", lambda e, bank=bank, par=par, sa=sa: e.scalar_tensor_tensor(
                out=yvC[:, par * 512:(par + 1) * 512], in0=bank[:, :], scalar=sa,
                in1=xres[:, par * 512:(par + 1) * 512], op0=ALU.add, op1=ALU.add),
                waits=[("oc_mm", p.n("oc_mm")), (f"xrl{par}", p.n(f"xrl{par}")),
                       (f"yst{par}", p.n(f"yst{par}")), ("cld", cldv)],
                inc="oc_ev")
            p.op("scalar", lambda e, oc=oc, ts4=ts4, par=par: e.dma_start(
                out=yT[oc * 128:(oc + 1) * 128, ts4 * 512:ts4 * 512 + 512],
                in_=yvC[:, par * 512:(par + 1) * 512]),
                waits=[("oc_ev", p.n("oc_ev"))], inc=f"yst{par}", inc_n=16)
            ec += 1
    esC.close()



def attn_weight_handles(p, layer, pref=""):
    W = {}
    if layer == 0:
        W["wq"] = p.dram(pref + "wq", [D, D], F32, kind="ExternalInput")
        wk = p.dram(pref + "wk", [D, D], F32, kind="ExternalInput")
        wv = p.dram(pref + "wv", [D, D], F32, kind="ExternalInput")
        W["wo"] = p.dram(pref + "wo", [D, D], F32, kind="ExternalInput")
        W["ksrc"] = lambda kc: wk[:, kc * 128:(kc + 1) * 128]
        W["vsrc"] = lambda vf: wv[:, vf * 256:(vf + 1) * 256]
        W["cs"] = [(p.dram(pref + "cosk0", [64, T], F32, kind="ExternalInput"),
                    p.dram(pref + "sink0", [64, T], F32, kind="ExternalInput")),
                   (p.dram(pref + "cosk1", [64, T], F32, kind="ExternalInput"),
                    p.dram(pref + "sink1", [64, T], F32, kind="ExternalInput")),
                   (p.dram(pref + "cosq", [64, T], F32, kind="ExternalInput"),
                    p.dram(pref + "sinq", [64, T], F32, kind="ExternalInput"))]
    else:
        W["wq"] = p.dram(pref + "wq", [D, D], F32, kind="ExternalInput")
        wkv = p.dram(pref + "wkv", [D, 1024], F32, kind="ExternalInput")
        W["wo"] = p.dram(pref + "wo", [D, D], F32, kind="ExternalInput")
        W["boR"] = p.dram(pref + "boR", [128, DC], F32, kind="ExternalInput")
        W["baT"] = p.dram(pref + "baT", [NH, 2, 6, T], BF16, kind="ExternalInput")
        W["bbT"] = p.dram(pref + "bbT", [NH, 6, T], BF16, kind="ExternalInput")
        W["ksrc"] = lambda kc: wkv[:, kc * 128:(kc + 1) * 128]
        W["vsrc"] = lambda vf: wkv[:, 512 + vf * 256:512 + (vf + 1) * 256]
    W["maskpat"] = p.dram(pref + "maskpat", [2, MPS, 128, QB], BF16, kind="ExternalInput")
    return W


def ffn_weight_handles(p, layer, pref=""):
    if layer == 0:
        return {
            "gwT": p.dram(pref + "gwT", [D, HID], I32, kind="ExternalInput"),
            "uwT": p.dram(pref + "uwT", [D, HID], I32, kind="ExternalInput"),
            "dwT": p.dram(pref + "dwT", [HID, D], I32, kind="ExternalInput"),
            "sg": p.dram(pref + "sg", [128, NJC], F32, kind="ExternalInput"),
            "su": p.dram(pref + "su", [128, NJC], F32, kind="ExternalInput"),
            "sd": p.dram(pref + "sd", [128, DC], F32, kind="ExternalInput"),
        }
    return {
        "w1T": p.dram(pref + "w1T", [D, HID], F32, kind="ExternalInput"),
        "w2T": p.dram(pref + "w2T", [D, HID], F32, kind="ExternalInput"),
        "w3T": p.dram(pref + "w3T", [HID, D], F32, kind="ExternalInput"),
        "w1q": p.dram(pref + "w1q", [D, HID], BF16),
        "w2q": p.dram(pref + "w2q", [D, HID], BF16),
        "w3q": p.dram(pref + "w3q", [HID, D], BF16),
    }


def build_ffn(layer):
    p = K(f"ffn{layer}")
    xT = p.dram("xT", [D, T], F32, kind="ExternalInput")
    yT = p.dram("yT", [D, T], F32, kind="ExternalOutput")
    W = ffn_weight_handles(p, layer)
    pb, pbquad = p.psum_pool()
    p.constf(EPS_LN)
    S = emit_scales_l0(p, W) if layer == 0 else None
    if layer == 1:
        emit_quant_body(p, pb, [(W["w1T"], W["w1q"], D, HID),
                                (W["w2T"], W["w2q"], D, HID),
                                (W["w3T"], W["w3q"], HID, D)])
        p.fence_all()
    emit_ffn_body(p, layer, xT, yT, W, pb, S)
    dma_sems = [s for s in p.ctr if s.startswith(("sst", "wld", "wdl", "xload",
                                                  "qld", "qst", "sload"))]
    p.wait("gpsimd", [(s, p.n(s)) for s in dma_sems])
    return p


def build_attn(layer):
    p = K(f"att{layer}")
    xT_own = p.dram("xT_own", [D, T], F32, kind="ExternalInput")
    yT = p.dram("yT", [D, T], F32, kind="ExternalOutput")
    W = attn_weight_handles(p, layer)
    W["xT_p0"] = p.dram("xT_p0", [D, T], F32, kind="ExternalInput")
    W["xT_p1"] = p.dram("xT_p1", [D, T], F32, kind="ExternalInput")
    pb, pbquad = p.psum_pool()
    p.constf(EPS_LN)
    emit_attn_body(p, layer, xT_own, yT, W, (pb, pbquad), fused=False)
    dma_sems = [s for s in p.ctr if s.startswith(("yst", "xrl", "wold", "cld"))]
    p.wait("gpsimd", [(s, p.n(s)) for s in dma_sems])
    return p


def build_mega():
    p = K("mega")
    xT = p.dram("xT", [D, T], F32, kind="ExternalInput")
    yT = p.dram("yT", [D, T], F32, kind="ExternalOutput")
    Wf0 = ffn_weight_handles(p, 0, "f0_")
    Wf1 = ffn_weight_handles(p, 1, "f1_")
    Wa0 = attn_weight_handles(p, 0, "a0_")
    Wa1 = attn_weight_handles(p, 1, "a1_")
    xA = p.dram("xA", [D, T], F32)
    xB = p.dram("xB", [D, T], F32)
    xC = p.dram("xC", [D, T], F32)
    xD = p.dram("xD", [D, T], F32)
    xE = p.dram("xE", [D, T], F32)
    pb, pbquad = p.psum_pool()
    p.constf(EPS_LN)
    S0 = emit_scales_l0(p, Wf0)
    emit_ffn_body(p, 0, xT, xA, Wf0, pb, S0)
    p.fence_all()
    emit_attn_body(p, 0, xA, xB, Wa0, (pb, pbquad), fused=True, cc_tag="a0",
                   quant_specs=[(Wf1["w1T"], Wf1["w1q"], D, HID),
                                (Wf1["w2T"], Wf1["w2q"], D, HID),
                                (Wf1["w3T"], Wf1["w3q"], HID, D)])
    p.fence_all()
    emit_ffn_body(p, 0, xB, xC, Wf0, pb, S0)
    p.fence_all()
    emit_ffn_body(p, 1, xC, xD, Wf1, pb)
    p.fence_all()
    emit_attn_body(p, 1, xD, xE, Wa1, (pb, pbquad), fused=True, cc_tag="a1")
    p.fence_all()
    emit_ffn_body(p, 1, xE, yT, Wf1, pb)
    p.fence_all()
    return p



def shard_x(x):
    out = []
    for c in range(N_CORES):
        b, par = c // 2, c % 2
        out.append(np.ascontiguousarray(x[b, par::2, :].T))
    return out


def unshard_x(shards):
    x = np.empty((4, 4096, 2048), np.float32)
    for c in range(N_CORES):
        b, par = c // 2, c % 2
        x[b, par::2, :] = shards[c].T
    return x


def ffn_inputs(layer, inp, pref=""):
    if layer == 0:
        return {
            pref + "gwT": np.ascontiguousarray(np.asarray(inp["L0_ffn_wq_gate"]).T),
            pref + "uwT": np.ascontiguousarray(np.asarray(inp["L0_ffn_wq_up"]).T),
            pref + "dwT": np.ascontiguousarray(np.asarray(inp["L0_ffn_wq_down"]).T),
            pref + "sg": np.ascontiguousarray(np.asarray(inp["L0_ffn_s_gate"]).reshape(NJC, 128).T),
            pref + "su": np.ascontiguousarray(np.asarray(inp["L0_ffn_s_up"]).reshape(NJC, 128).T),
            pref + "sd": np.ascontiguousarray(np.asarray(inp["L0_ffn_s_down"]).reshape(DC, 128).T),
        }
    return {
        pref + "w1T": np.ascontiguousarray(np.asarray(inp["L1_ffn_w1"]).T),
        pref + "w2T": np.ascontiguousarray(np.asarray(inp["L1_ffn_w2"]).T),
        pref + "w3T": np.ascontiguousarray(np.asarray(inp["L1_ffn_w3"]).T),
    }


def _split3(a):
    a = a.astype(np.float32)
    a0 = a.astype(BF).astype(np.float32)
    a1 = (a - a0).astype(BF).astype(np.float32)
    a2 = (a - a0 - a1).astype(BF)
    return np.stack([a0.astype(BF), a1.astype(BF), a2], 0)


def _alibi_slopes(n):
    start = 2.0 ** (-2.0 ** (-(np.log2(n) - 3.0)))
    return np.array([start ** (i + 1) for i in range(n)], np.float64)


def attn_inputs(layer, inp, par, pref=""):
    out = {}
    if layer == 0:
        out[pref + "wq"] = np.ascontiguousarray(np.asarray(inp["L0_attn_Wq"]).T)
        out[pref + "wk"] = np.ascontiguousarray(np.asarray(inp["L0_attn_Wk"]).T)
        out[pref + "wv"] = np.ascontiguousarray(np.asarray(inp["L0_attn_Wv"]).T)
        out[pref + "wo"] = np.ascontiguousarray(np.asarray(inp["L0_attn_Wo"]).T)
        inv_freq = 1.0 / 10000.0 ** (np.arange(0, 128, 2, dtype=np.float32) / 128.0)
        tpos = np.arange(4096, dtype=np.float32)
        fr = np.outer(tpos, inv_freq)
        cos = np.cos(fr).astype(np.float32)
        sin = np.sin(fr).astype(np.float32)
        out[pref + "cosk0"] = np.ascontiguousarray(cos[0::2].T)
        out[pref + "sink0"] = np.ascontiguousarray(sin[0::2].T)
        out[pref + "cosk1"] = np.ascontiguousarray(cos[1::2].T)
        out[pref + "sink1"] = np.ascontiguousarray(sin[1::2].T)
        out[pref + "cosq"] = np.ascontiguousarray(cos[par::2].T)
        out[pref + "sinq"] = np.ascontiguousarray(sin[par::2].T)
    else:
        out[pref + "wq"] = np.ascontiguousarray(np.asarray(inp["L1_attn_Wq"]))
        out[pref + "wkv"] = np.ascontiguousarray(np.asarray(inp["L1_attn_Wkv"]))
        out[pref + "wo"] = np.ascontiguousarray(np.asarray(inp["L1_attn_Wo"]))
        out[pref + "boR"] = np.ascontiguousarray(
            np.asarray(inp["L1_attn_bo"]).reshape(DC, 128).T.astype(np.float32))
        sl = _alibi_slopes(NH)
        S128 = np.sqrt(128.0)
        l = np.arange(T, dtype=np.float64)
        baT = np.zeros((NH, 2, 6, T), BF)
        bbT = np.zeros((NH, 6, T), BF)
        for h in range(NH):
            for r in range(2):
                A = S128 * sl[h] * (2 * l + r)
                baT[h, r, 0:3] = _split3(A)
                baT[h, r, 3:6] = np.ones((3, T), BF)
            Bv = -S128 * sl[h] * (2 * l + par)
            bbT[h, 0:3] = np.ones((3, T), BF)
            bbT[h, 3:6] = _split3(Bv)
        out[pref + "baT"] = baT
        out[pref + "bbT"] = bbT
    kk = np.arange(128)[:, None]
    qq = np.arange(QB)[None, :]
    mp = np.zeros((2, MPS, 128, QB), BF)
    for r in range(2):
        eps = -1 if (par == 0 and r == 1) else 0
        for d in range(MPS):
            valid = kk <= (qq - 128 * d + eps)
            mp[r, d] = np.where(valid, np.float32(1e30), np.float32(-1e30)).astype(BF)
    out[pref + "maskpat"] = mp
    return out


_prog_cache = {}


def get_prog(kind, layer=0):
    key = (kind, layer)
    if key not in _prog_cache:
        if kind == "ffn":
            _prog_cache[key] = build_ffn(layer).finish()
        elif kind == "att":
            _prog_cache[key] = build_attn(layer).finish()
        elif kind == "mega":
            _prog_cache[key] = build_mega().finish()
        else:
            raise KeyError(kind)
    return _prog_cache[key]


def run_ffn(layer, x_shards, inp):
    nc = get_prog("ffn", layer)
    wmap = ffn_inputs(layer, inp)
    in_maps = [dict(wmap, xT=xs) for xs in x_shards]
    res = run_bass_kernel_spmd(nc, in_maps, core_ids=list(range(N_CORES)))
    return [np.asarray(res.results[c]["yT"]) for c in range(N_CORES)]


def run_attn(layer, x_shards, inp):
    nc = get_prog("att", layer)
    wmaps = [attn_inputs(layer, inp, 0), attn_inputs(layer, inp, 1)]
    in_maps = []
    for c in range(N_CORES):
        b, par = c // 2, c % 2
        in_maps.append(dict(wmaps[par], xT_own=x_shards[c],
                            xT_p0=x_shards[2 * b], xT_p1=x_shards[2 * b + 1]))
    res = run_bass_kernel_spmd(nc, in_maps, core_ids=list(range(N_CORES)))
    return [np.asarray(res.results[c]["yT"]) for c in range(N_CORES)]


def kernel(**inputs):
    inputs = {k: np.asarray(v) for k, v in inputs.items()}
    nc = get_prog("mega")
    base = {}
    base.update(ffn_inputs(0, inputs, "f0_"))
    base.update(ffn_inputs(1, inputs, "f1_"))
    wa = [dict(attn_inputs(0, inputs, par, "a0_"),
               **attn_inputs(1, inputs, par, "a1_")) for par in range(2)]
    x_shards = shard_x(inputs["L0_attn_hidden_states"])
    in_maps = []
    for c in range(N_CORES):
        par = c % 2
        m = dict(base)
        m.update(wa[par])
        m["xT"] = x_shards[c]
        in_maps.append(m)
    res = run_bass_kernel_spmd(nc, in_maps, core_ids=list(range(N_CORES)))
    return unshard_x([np.asarray(res.results[c]["yT"]) for c in range(N_CORES)])


def kernel_unfused(**inputs):
    inputs = {k: np.asarray(v) for k, v in inputs.items()}
    x = shard_x(inputs["L0_attn_hidden_states"])
    x = run_ffn(0, x, inputs)
    x = run_attn(0, x, inputs)
    x = run_ffn(0, x, inputs)
    x = run_ffn(1, x, inputs)
    x = run_attn(1, x, inputs)
    x = run_ffn(1, x, inputs)
    return unshard_x(x)


# version 09
# speedup vs baseline: 1.2038x; 1.0322x over previous
import numpy as np
import ml_dtypes
from contextlib import ExitStack
import concourse.bass as bass
import concourse.mybir as mybir
from concourse.bass_utils import run_bass_kernel_spmd

F32 = mybir.dt.float32
BF16 = mybir.dt.bfloat16
I32 = mybir.dt.int32
AF = mybir.ActivationFunctionType
ALU = mybir.AluOpType
AX = mybir.AxisListType
BF = ml_dtypes.bfloat16

N_CORES = 8
D = 2048
T = 2048
HID = 5632
DC = D // 128
NH = 16
NJC = HID // 128
EPS_LN = 1e-5
INVD = 1.0 / D
NS = 4
QB = 512
KC = 128
MPS = QB // KC
RSQHD = float(1.0 / np.sqrt(128.0))

ENGINES = ("tensor", "scalar", "vector", "gpsimd", "sync")


class K:
    def __init__(self, name):
        self.nc = bass.Bass("TRN2", target_bir_lowering=False, debug=False)
        self.name = name
        self.q = {e: [] for e in ENGINES}
        self.ctr = {}
        self._semh = {}
        self._waited = {e: {} for e in ENGINES}
        self.es = ExitStack()
        self._uid = 0

    def sem(self, name):
        if name not in self._semh:
            self._semh[name] = self.es.enter_context(self.nc.semaphore(name))
            self.ctr[name] = 0
        return self._semh[name]

    def uname(self, nm):
        self._uid += 1
        return f"{nm}_{self._uid}"

    def sbuf(self, name, shape, dt):
        return self.es.enter_context(self.nc.sbuf_tensor(self.uname(name), list(shape), dt))

    def psum(self, name, shape, dt=F32):
        return self.es.enter_context(self.nc.psum_tensor(self.uname(name), list(shape), dt))

    def psum_pool(self):
        quad = self.psum("pbq", [128, 2048])
        pb = [quad[:, i * 512:(i + 1) * 512] for i in range(4)]
        pb += [self.psum("pb", [128, 512]) for _ in range(4)]
        return pb, quad

    def dram(self, name, shape, dt, kind="Internal"):
        return self.nc.dram_tensor(name, list(shape), dt, kind=kind)

    def op(self, eng, build, waits=(), inc=None, inc_n=1):
        wl = []
        for s, v in waits:
            if v <= 0:
                continue
            assert v <= self.ctr.get(s, 0), \
                f"wait {s}>={v} exceeds planned {self.ctr.get(s, 0)}"
            if self._waited[eng].get(s, -1) >= v:
                continue
            self._waited[eng][s] = v
            wl.append((self.sem(s), v))
        inch = None
        if inc is not None:
            inch = self.sem(inc)
            self.ctr[inc] += inc_n

        def fn(e, wl=wl, build=build, inch=inch, inc_n=inc_n):
            for h, v in wl:
                e.wait_ge(h, v)
            r = build(e)
            if inch is not None:
                r.then_inc(inch, inc_n)
        self.q[eng].append(fn)

    def wait(self, eng, waits):
        self.op(eng, lambda e: None, waits=waits)

    def fence_all(self):
        w = [(s, self.ctr[s]) for s in self.ctr]
        for e in ENGINES:
            self.wait(e, w)

    def constf(self, value):
        if not hasattr(self, "_cstbuf"):
            self._cstbuf = self.sbuf("cst", [1, 16], F32)
            self._csts = {}
        if value not in self._csts:
            i = len(self._csts)
            assert i < 16
            self._csts[value] = i
            ap = self._cstbuf[0:1, i:i + 1]
            self.nc.const_aps.register(F32, value, ap)
            self.op("vector", lambda e, ap=ap, value=value: e.memset(ap, value),
                    inc="init")

    def n(self, semname):
        return self.ctr.get(semname, 0)

    def finish(self):
        with self.nc.Block() as block:
            for e in ENGINES:
                lst = self.q[e]
                if not lst:
                    continue
                dec = getattr(block, e)

                def body(eng, lst=lst):
                    for fn in lst:
                        fn(eng)
                dec(body)
        self.es.close()
        return self.nc


def _r(ap, spec, **kw):
    return ap.rearrange(spec, **kw)



def emit_scales_l0(p, W):
    sgs = p.sbuf("sgs", [128, NJC], F32)
    sus = p.sbuf("sus", [128, NJC], F32)
    sds = p.sbuf("sds", [128, DC], F32)
    p.op("sync", lambda e: e.dma_start(out=sgs[:, :], in_=W["sg"][:, :]), inc="sload", inc_n=16)
    p.op("sync", lambda e: e.dma_start(out=sus[:, :], in_=W["su"][:, :]), inc="sload", inc_n=16)
    p.op("sync", lambda e: e.dma_start(out=sds[:, :], in_=W["sd"][:, :]), inc="sload", inc_n=16)
    p.op("vector", lambda e: e.tensor_scalar(out=sds[:, :], in0=sds[:, :],
                                             scalar1=0.5, scalar2=None, op0=ALU.mult),
         waits=[("sload", p.n("sload"))], inc="init")
    return {"sgs": sgs, "sus": sus, "sds": sds,
            "ready": [("sload", p.n("sload")), ("init", p.n("init"))]}



def emit_quant_body(p, pb, specs):
    es2 = ExitStack()

    def sb(nm, shape, dt):
        return es2.enter_context(p.nc.sbuf_tensor(p.uname(nm), list(shape), dt))
    qw = sb("qw", [128, 4 * 2816], F32)
    qp = sb("qp", [128, 2 * 2816], BF16)
    qn = sb("qn", [128, 2 * 2816], BF16)
    qo = sb("qo", [128, 2 * 2816], BF16)
    strips = sb("strips", [128, 64], F32)
    onef = sb("onef", [128, 1], F32)
    onefr = sb("onefr", [1, 128], F32)
    gval = sb("gval", [128, 4], F32)
    p.op("vector", lambda e: e.memset(onef[:, :], 1.0))
    p.op("vector", lambda e: e.memset(onefr[:, :], 1.0), inc="qrow")
    base_qbc = p.n("qbc")

    def tiles_of(rows, cols):
        w = 2816 if cols == HID else 2048
        return [(rc * 128, cc * w, w) for rc in range(rows // 128)
                for cc in range(cols // w)]

    for mi, (src, dst, rows, cols) in enumerate(specs):
        tl = tiles_of(rows, cols)
        nt = len(tl)
        base_abs = p.n("qabs")
        for i, (r0, c0, w) in enumerate(tl):
            sl = i % 4
            qeng = "scalar" if sl % 2 == 0 else "sync"
            qsem = f"ql{sl}"
            p.op(qeng, lambda e, src=src, r0=r0, c0=c0, w=w, sl=sl: e.dma_start(
                out=qw[:, sl * 2816:sl * 2816 + w], in_=src[r0:r0 + 128, c0:c0 + w]),
                waits=[("qabs", base_abs + i - 3), ("qbc", base_qbc + mi),
                       ("qdv", p.n("qdv"))],
                inc=qsem, inc_n=16)
            p.op("scalar", lambda e, w=w, sl=sl, i=i: e.activation(
                out=qw[:, sl * 2816:sl * 2816 + w], in_=qw[:, sl * 2816:sl * 2816 + w],
                func=AF.Abs, bias=0.0, scale=1.0, accum_out=strips[:, i:i + 1]),
                waits=[(qsem, p.n(qsem))], inc="qabs")
        p.op("vector", lambda e, nt=nt: e.tensor_reduce(
            out=strips[:, 0:1], in_=strips[:, 0:nt], axis=AX.X, op=ALU.add),
            waits=[("qabs", base_abs + nt)], inc="qrow")
        p.op("tensor", lambda e: e.matmul(pb[7][0:1, 0:1], strips[:, 0:1], onef[:, :],
             start=True, stop=True), waits=[("qrow", p.n("qrow"))], inc="qmm")
        nelem = float(rows * cols)
        p.op("vector", lambda e, nelem=nelem: e.tensor_scalar(
            out=gval[0:1, 0:1], in0=pb[7][0:1, 0:1],
            scalar1=0.5 / nelem, scalar2=0.5e-8, op0=ALU.mult, op1=ALU.add),
            waits=[("qmm", p.n("qmm"))], inc="qrow")
        p.op("tensor", lambda e: e.matmul(pb[6][:, 0:1], onefr[:, :], gval[0:1, 0:1],
             start=True, stop=True), waits=[("qrow", p.n("qrow"))], inc="qmm")
        p.op("vector", lambda e: e.tensor_copy(out=gval[:, 0:1], in_=pb[6][:, 0:1]),
             waits=[("qmm", p.n("qmm"))])
        p.op("vector", lambda e: e.tensor_scalar(
            out=gval[:, 1:2], in0=pb[6][:, 0:1], scalar1=-1.0, scalar2=None,
            op0=ALU.mult), inc="qbc")
        base_dv = p.n("qdv")
        for i, (r0, c0, w) in enumerate(tl):
            sl = i % 4
            sl2 = i % 2
            qeng = "scalar" if sl % 2 == 0 else "sync"
            qsem = f"ql{sl}"
            p.op(qeng, lambda e, src=src, r0=r0, c0=c0, w=w, sl=sl: e.dma_start(
                out=qw[:, sl * 2816:sl * 2816 + w], in_=src[r0:r0 + 128, c0:c0 + w]),
                waits=[("qbc", base_qbc + mi + 1), ("qdv", base_dv + 3 * (i - 3))],
                inc=qsem, inc_n=16)
            ld = p.n(qsem)
            p.op("vector", lambda e, w=w, sl=sl, sl2=sl2: e.tensor_scalar(
                out=qp[:, sl2 * 2816:sl2 * 2816 + w], in0=qw[:, sl * 2816:sl * 2816 + w],
                scalar1=gval[:, 0:1], scalar2=None, op0=ALU.is_ge),
                waits=[(qsem, ld)], inc="qdv")
            p.op("vector", lambda e, w=w, sl=sl, sl2=sl2: e.tensor_scalar(
                out=qn[:, sl2 * 2816:sl2 * 2816 + w], in0=qw[:, sl * 2816:sl * 2816 + w],
                scalar1=gval[:, 1:2], scalar2=None, op0=ALU.is_le), inc="qdv")
            p.op("vector", lambda e, w=w, sl2=sl2: e.tensor_tensor(
                out=qo[:, sl2 * 2816:sl2 * 2816 + w], in0=qp[:, sl2 * 2816:sl2 * 2816 + w],
                in1=qn[:, sl2 * 2816:sl2 * 2816 + w],
                op=ALU.subtract),
                waits=[(f"qst{sl2}", p.n(f"qst{sl2}"))], inc="qdv")
            p.op("gpsimd", lambda e, dst=dst, r0=r0, c0=c0, w=w, sl2=sl2: e.dma_start(
                out=dst[r0:r0 + 128, c0:c0 + w], in_=qo[:, sl2 * 2816:sl2 * 2816 + w]),
                waits=[("qdv", p.n("qdv"))], inc=f"qst{sl2}", inc_n=16)
    p.wait("sync", [("qst0", p.n("qst0")), ("qst1", p.n("qst1"))])
    es2.close()




def emit_quant_chunks(p, sb, bank, specs):
    QWW = 1408
    qw = sb("qw", [128, 2 * QWW], F32)
    qp = sb("qp", [128, QWW], BF16)
    qn = sb("qn", [128, QWW], BF16)
    qo = sb("qo", [128, 2 * QWW], BF16)
    strips = sb("strips", [128, 192], F32)
    onef = sb("onef", [128, 1], F32)
    onefrq = sb("onefrq", [1, 128], F32)
    gval = sb("gval", [128, 4], F32)
    chunks = []

    def setup():
        p.op("vector", lambda e: e.memset(onef[:, :], 1.0))
        p.op("vector", lambda e: e.memset(onefrq[:, :], 1.0), inc="qrow")
    chunks.append(setup)
    base_qbc = [0]

    def tiles_of(rows, cols):
        w = QWW if cols % QWW == 0 else 1024
        return [(rc * 128, cc * w, w) for rc in range(rows // 128)
                for cc in range(cols // w)]

    for mi, (srcw, dst, rows, cols) in enumerate(specs):
        tl = tiles_of(rows, cols)
        nt = len(tl)
        st0 = [0]

        def mk_passA(i, r0, c0, w, sl, srcw=None, mi=mi, st0=st0):
            def f(srcw=srcw, i=i, r0=r0, c0=c0, w=w, sl=sl, mi=mi):
                if i == 0:
                    st0[0] = p.n("qabs")
                qeng = "scalar" if sl == 0 else "sync"
                qsem = "qlA" if sl == 0 else "qlS"
                p.op(qeng, lambda e, srcw=srcw, r0=r0, c0=c0, w=w, sl=sl: e.dma_start(
                    out=qw[:, sl * QWW:sl * QWW + w], in_=srcw[r0:r0 + 128, c0:c0 + w]),
                    waits=[("qabs", st0[0] + i - 1), ("qbc", base_qbc[0] + mi),
                           ("qdv", p.n("qdv"))],
                    inc=qsem, inc_n=16)
                p.op("scalar", lambda e, w=w, sl=sl, i=i: e.activation(
                    out=qw[:, sl * QWW:sl * QWW + w], in_=qw[:, sl * QWW:sl * QWW + w],
                    func=AF.Abs, bias=0.0, scale=1.0, accum_out=strips[:, i:i + 1]),
                    waits=[(qsem, p.n(qsem))], inc="qabs")
            return f
        for i, (r0, c0, w) in enumerate(tl):
            chunks.append(mk_passA(i, r0, c0, w, i % 2, srcw=srcw))

        def gamma(nt=nt, rows=rows, cols=cols, st0=st0):
            p.op("vector", lambda e, nt=nt: e.tensor_reduce(
                out=strips[:, 0:1], in_=strips[:, 0:nt], axis=AX.X, op=ALU.add),
                waits=[("qabs", st0[0] + nt)], inc="qrow")
            p.op("tensor", lambda e: e.matmul(bank[0:1, 0:1], strips[:, 0:1], onef[:, :],
                 start=True, stop=True), waits=[("qrow", p.n("qrow"))], inc="qmm")
            nelem = float(rows * cols)
            p.op("vector", lambda e, nelem=nelem: e.tensor_scalar(
                out=gval[0:1, 0:1], in0=bank[0:1, 0:1],
                scalar1=0.5 / nelem, scalar2=0.5e-8, op0=ALU.mult, op1=ALU.add),
                waits=[("qmm", p.n("qmm"))], inc="qrow")
            p.op("tensor", lambda e: e.matmul(bank[:, 0:1], onefrq[:, :], gval[0:1, 0:1],
                 start=True, stop=True), waits=[("qrow", p.n("qrow"))], inc="qmm")
            p.op("vector", lambda e: e.tensor_copy(out=gval[:, 0:1], in_=bank[:, 0:1]),
                 waits=[("qmm", p.n("qmm"))])
            p.op("vector", lambda e: e.tensor_scalar(
                out=gval[:, 1:2], in0=bank[:, 0:1], scalar1=-1.0, scalar2=None,
                op0=ALU.mult), inc="qbc")
        chunks.append(gamma)
        dv0 = [0]

        def mk_passB(i, r0, c0, w, sl, srcw=None, dst=None, mi=mi, dv0=dv0):
            def f(srcw=srcw, dst=dst, i=i, r0=r0, c0=c0, w=w, sl=sl, mi=mi):
                if i == 0:
                    dv0[0] = p.n("qdv")
                qeng = "scalar" if sl == 0 else "sync"
                qsem = "qlA" if sl == 0 else "qlS"
                p.op(qeng, lambda e, srcw=srcw, r0=r0, c0=c0, w=w, sl=sl: e.dma_start(
                    out=qw[:, sl * QWW:sl * QWW + w], in_=srcw[r0:r0 + 128, c0:c0 + w]),
                    waits=[("qbc", base_qbc[0] + mi + 1), ("qdv", dv0[0] + 3 * (i - 1))],
                    inc=qsem, inc_n=16)
                ld = p.n(qsem)
                p.op("vector", lambda e, w=w, sl=sl: e.tensor_scalar(
                    out=qp[:, 0:w], in0=qw[:, sl * QWW:sl * QWW + w],
                    scalar1=gval[:, 0:1], scalar2=None, op0=ALU.is_ge),
                    waits=[(qsem, ld)], inc="qdv")
                p.op("vector", lambda e, w=w, sl=sl: e.tensor_scalar(
                    out=qn[:, 0:w], in0=qw[:, sl * QWW:sl * QWW + w],
                    scalar1=gval[:, 1:2], scalar2=None, op0=ALU.is_le), inc="qdv")
                p.op("vector", lambda e, w=w, sl=sl: e.tensor_tensor(
                    out=qo[:, sl * QWW:sl * QWW + w], in0=qp[:, 0:w], in1=qn[:, 0:w],
                    op=ALU.subtract),
                    waits=[(f"qst{sl}", p.n(f"qst{sl}"))], inc="qdv")
                p.op("gpsimd", lambda e, dst=dst, r0=r0, c0=c0, w=w, sl=sl: e.dma_start(
                    out=dst[r0:r0 + 128, c0:c0 + w], in_=qo[:, sl * QWW:sl * QWW + w]),
                    waits=[("qdv", p.n("qdv"))], inc=f"qst{sl}", inc_n=16)
            return f
        for i, (r0, c0, w) in enumerate(tl):
            chunks.append(mk_passB(i, r0, c0, w, i % 2, srcw=srcw, dst=dst))
    return chunks



def emit_ffn_body(p, layer, xT, yT, W, pb, S=None):
    es = ExitStack()

    def sb(nm, shape, dt):
        return es.enter_context(p.nc.sbuf_tensor(p.uname(nm), list(shape), dt))
    xf = sb("xf", [128, 16 * 512], F32)
    xs = sb("xs", [128, 2 * 512], BF16)
    xtmp = sb("xtmp", [128, 512], F32)
    xq = xtmp[:, :].bitcast(BF16)
    xl = sb("xl", [128, DC * 1024], BF16)
    hh = sb("hh", [128, NJC * 1024], BF16)
    wgb = sb("wgb", [128, 2 * DC * 256], BF16)
    wub = sb("wub", [128, 2 * DC * 256], BF16)
    wdb = sb("wdb", [128, 2 * 4 * 512], BF16)
    wdbf = wdb[:, :].bitcast(F32)
    gsb = sb("gsb", [128, 2 * 512], BF16)
    usb = sb("usb", [128, 2 * 512], BF16)
    xsf = xs[:, :].bitcast(F32)
    yv = xf[:, 8 * 512:10 * 512]
    rows = sb("rows", [1, 2 * 512], F32)
    brow = sb("brow", [1, 2 * 512], BF16)
    onec = sb("onec", [128, 1], BF16)
    oner = sb("oner", [1, 128], BF16)

    if layer == 0:
        wsrc_g, wsrc_u, wsrc_d = W["gwT"], W["uwT"], W["dwT"]
        wq_eng = "gpsimd"
        sgs, sus, sds = S["sgs"], S["sus"], S["sds"]
        s_ready = S["ready"]
    else:
        wsrc_g, wsrc_u, wsrc_d = W["w1q"], W["w2q"], W["w3q"]
        wq_eng = "sync"
        s_ready = []

    p.op("vector", lambda e: e.memset(onec[:, :], 1.0))
    p.op("vector", lambda e: e.memset(oner[:, :], 1.0), inc="init")

    b_dn = p.n("rel_dn")
    b_gmm = p.n("gu_mm")
    b_dmm = p.n("dn_mm")
    b_hok = p.n("h_ok")
    ln_ts = [0, 0]

    for tb in range(2):
        t0 = tb * 1024
        for tsub in range(2):
            ts4 = tb * 2 + tsub
            tc0 = ts4 * 512
            base_mm = p.n("ln_mm")
            base_row = p.n("ln_row")
            base_act = p.n("ln_act")
            base_bc = p.n("ln_bc")
            p.op("sync", lambda e, tc0=tc0: e.dma_start(
                out=_r(xf[:, :], "p (c t) -> p c t", t=512),
                in_=_r(xT[:, tc0:tc0 + 512], "(c p) t -> p c t", p=128)),
                waits=[("ln_ap", p.n("ln_ap")), ("rel_dn", b_dn + 32 * tb),
                       ("sst0", p.n("sst0")), ("sst1", p.n("sst1"))],
                inc="xload", inc_n=16)
            xld = p.n("xload")
            for dc in range(DC):
                sl = dc % 2
                p.op("vector", lambda e, dc=dc, sl=sl: e.tensor_copy(
                    out=xq[:, sl * 512:(sl + 1) * 512], in_=xf[:, dc * 512:(dc + 1) * 512]),
                    waits=[("xload", xld), ("ln_mm", base_mm + dc - 1)], inc="ln_xq")
                p.op("scalar", lambda e, dc=dc, sl=sl: e.activation(
                    out=xs[:, sl * 512:(sl + 1) * 512], in_=xf[:, dc * 512:(dc + 1) * 512],
                    func=AF.Square, bias=0.0, scale=1.0),
                    waits=[("xload", xld), ("ln_mm", base_mm + dc - 1)], inc="ln_xs")
                w0 = [("ln_xq", p.n("ln_xq")), ("ln_xs", p.n("ln_xs"))]
                if dc == 0:
                    w0 += [("ln_row", base_row), ("rel_dn", b_dn + 32 * tb),
                           ("init", p.n("init"))]
                p.op("tensor", lambda e, sl=sl, dc=dc: e.matmul(
                    pb[0][0:1, :], onec[:, :], xq[:, sl * 512:(sl + 1) * 512],
                    start=(dc == 0), stop=(dc == DC - 1)), waits=w0)
                p.op("tensor", lambda e, sl=sl, dc=dc: e.matmul(
                    pb[1][0:1, :], onec[:, :], xs[:, sl * 512:(sl + 1) * 512],
                    start=(dc == 0), stop=(dc == DC - 1)), inc="ln_mm")
            m_, msq = rows[0:1, 0:512], rows[0:1, 512:1024]
            var, lnv = rows[0:1, 0:512], rows[0:1, 512:1024]
            bm, br = brow[0:1, 0:512], brow[0:1, 512:1024]
            p.op("vector", lambda e: e.tensor_scalar(out=m_, in0=pb[0][0:1, :],
                 scalar1=INVD, scalar2=None, op0=ALU.mult),
                 waits=[("ln_mm", base_mm + DC), ("ln_act", p.n("ln_act"))])
            p.op("vector", lambda e: e.tensor_copy(out=bm, in_=m_), inc="ln_row")
            p.op("vector", lambda e: e.tensor_tensor(out=msq, in0=m_, in1=m_, op=ALU.mult))
            p.op("vector", lambda e: e.scalar_tensor_tensor(out=var, in0=pb[1][0:1, :],
                 scalar=INVD, in1=msq, op0=ALU.mult, op1=ALU.subtract), inc="ln_row")
            p.op("scalar", lambda e: e.activation(out=lnv, in_=var, func=AF.Ln,
                 bias=EPS_LN, scale=1.0),
                 waits=[("ln_row", base_row + 2), ("init", 1)])
            p.op("scalar", lambda e: e.activation(out=br, in_=lnv, func=AF.Exp,
                 bias=0.0, scale=-0.5), inc="ln_act")
            p.op("tensor", lambda e: e.matmul(pb[2][:, :], oner[:, :], bm, start=True, stop=True),
                 waits=[("ln_row", base_row + 2), ("ln_ap", p.n("ln_ap"))], inc="ln_bc")
            p.op("tensor", lambda e: e.matmul(pb[3][:, :], oner[:, :], br, start=True, stop=True),
                 waits=[("ln_act", base_act + 1)], inc="ln_bc")
            mb_sb = xsf[:, 0:512]
            rb_sb = wdbf[:, 0:512]
            pslot = [xtmp[:, :], wdbf[:, 512:1024]]
            p.op("vector", lambda e: e.tensor_copy(out=mb_sb, in_=pb[2][:, :]),
                 waits=[("ln_bc", base_bc + 2)])
            p.op("vector", lambda e: e.tensor_copy(out=rb_sb, in_=pb[3][:, :]),
                 inc="ln_ev")
            base_pl = p.n("ln_pl")
            evv = p.n("ln_ev")
            for dc in range(DC):
                p.op("gpsimd", lambda e, dc=dc, sl=pslot[dc % 2]: e.tensor_tensor(
                    out=sl, in0=xf[:, dc * 512:(dc + 1) * 512], in1=mb_sb,
                    op=ALU.subtract),
                    waits=[("ln_ev", evv), ("ln_ap", p.n("ln_ap") - 1),
                           ("gu_mm", b_gmm + 44 * tb)], inc="ln_pl")
                p.op("vector", lambda e, dc=dc, sl=pslot[dc % 2], off=dc * 1024 + tsub * 512: e.tensor_tensor(
                    out=xl[:, off:off + 512], in0=sl, in1=rb_sb,
                    op=ALU.mult), waits=[("ln_pl", base_pl + dc + 1)], inc="ln_ap")
            ln_ts[ts4 % 2] = p.n("ln_ap")
            p.wait("tensor", [("ln_ap", p.n("ln_ap"))])

        base_gu = p.n("gu_mm")
        base_gug = p.n("gu_g")
        base_rg = p.n("rel_g")
        base_ru = p.n("rel_u")
        base_wf = p.n("wfix")
        base_h = p.n("h_ok")
        ln_done = p.n("ln_ap")
        for jp in range(NJC // 2):
          buf = jp % 2
          wsem = f"wld{buf}"
          woff = buf * DC * 256
          c0 = jp * 256
          p.op(wq_eng, lambda e, c0=c0, woff=woff: e.dma_start(
              out=_r(wgb[:, woff:woff + DC * 256], "p (c j) -> p c j", j=256),
              in_=_r(wsrc_g[:, c0:c0 + 256], "(c p) j -> p c j", p=128)),
              waits=[("gu_mm", max(base_gu + 2 * jp - 2, base_gu))], inc=wsem, inc_n=16)
          p.op(wq_eng, lambda e, c0=c0, woff=woff: e.dma_start(
              out=_r(wub[:, woff:woff + DC * 256], "p (c j) -> p c j", j=256),
              in_=_r(wsrc_u[:, c0:c0 + 256], "(c p) j -> p c j", p=128)),
              inc=wsem, inc_n=16)
          wld = p.n(wsem)
          if layer == 0:
              p.op("vector", lambda e, woff=woff: e.tensor_scalar(
                  out=wgb[:, woff:woff + DC * 256], in0=wgb[:, woff:woff + DC * 256],
                  scalar1=-128.0, scalar2=None, op0=ALU.add),
                  waits=[(wsem, wld)], inc="wfix")
              p.op("vector", lambda e, woff=woff: e.tensor_scalar(
                  out=wub[:, woff:woff + DC * 256], in0=wub[:, woff:woff + DC * 256],
                  scalar1=-128.0, scalar2=None, op0=ALU.add), inc="wfix")
          for jcl in range(2):
            jc = 2 * jp + jcl
            jb = jc % 2
            gb = [pb[0 + 4 * jb], pb[1 + 4 * jb]]
            ub = [pb[2 + 4 * jb], pb[3 + 4 * jb]]
            w0 = [("rel_g", base_rg + 2 * (jc - 1)),
                  ("rel_u", base_ru + 2 * (jc - 1))]
            if layer == 0:
                w0.append(("wfix", base_wf + 2 * (jp + 1)))
            else:
                w0.append((wsem, wld))
            if jc == 0:
                w0.append(("rel_dn", b_dn + 32 * tb))
            for which, banks, wbuf in (("g", gb, wgb), ("u", ub, wub)):
                for tsub in range(2):
                    w0t = w0 + [("ln_ap", ln_ts[tsub])]
                    for dc in range(DC):
                        p.op("tensor", lambda e, banks=banks, tsub=tsub, dc=dc, wbuf=wbuf, woff=woff, jcl=jcl: e.matmul(
                            banks[tsub][:, :],
                            wbuf[:, woff + dc * 256 + jcl * 128:woff + dc * 256 + jcl * 128 + 128],
                            xl[:, dc * 1024 + tsub * 512:dc * 1024 + tsub * 512 + 512],
                            start=(dc == 0), stop=(dc == DC - 1)),
                            waits=(w0t if dc == 0 else ()),
                            inc=("gu_g" if (which == "g" and dc == DC - 1) else
                                 ("gu_mm" if (which == "u" and tsub == 1 and dc == DC - 1) else None)))
            for tsub in range(2):
                gslot = gsb[:, tsub * 512:(tsub + 1) * 512]
                uslot = usb[:, tsub * 512:(tsub + 1) * 512]
                if layer == 0:
                    p.op("scalar", lambda e, gslot=gslot, b=gb[tsub], jc=jc: e.activation(
                        out=gslot, in_=b[:, :], func=AF.Silu, bias=0.0,
                        scale=sgs[:, jc:jc + 1]),
                        waits=[("gu_g", base_gug + 2 * jc + tsub + 1),
                               ("h_ok", base_h + 2 * jc + tsub - 1)] + s_ready,
                        inc="rel_g")
                    p.op("vector", lambda e, uslot=uslot, b=ub[tsub], jc=jc: e.tensor_scalar(
                        out=uslot, in0=b[:, :], scalar1=sus[:, jc:jc + 1], scalar2=None,
                        op0=ALU.mult), waits=[("gu_mm", base_gu + jc + 1)], inc="rel_u")
                else:
                    p.op("scalar", lambda e, gslot=gslot, b=gb[tsub]: e.activation(
                        out=gslot, in_=b[:, :], func=AF.Sigmoid, bias=0.0, scale=1.0),
                        waits=[("gu_g", base_gug + 2 * jc + tsub + 1),
                               ("h_ok", base_h + 2 * jc + tsub - 1)], inc="rel_g")
                    p.op("vector", lambda e, uslot=uslot, b=ub[tsub]: e.tensor_copy(
                        out=uslot, in_=b[:, :]),
                        waits=[("gu_mm", base_gu + jc + 1)], inc="rel_u")
                p.op("vector", lambda e, gslot=gslot, uslot=uslot, off=jc * 1024 + tsub * 512: e.tensor_tensor(
                    out=hh[:, off:off + 512], in0=gslot, in1=uslot, op=ALU.mult),
                    waits=[("rel_g", base_rg + 2 * jc + tsub + 1),
                           ("dn_mm", b_dmm + 176 * tb)], inc="h_ok")

        for ocg in range(4):
            base_dn = p.n("dn_mm")
            for i in range(8):
                oc = ocg * 4 + i // 2
                tss = i % 2
                p.op("sync", lambda e, oc=oc, tss=tss, i=i, t0=t0: e.dma_start(
                    out=xf[:, i * 512:(i + 1) * 512],
                    in_=xT[oc * 128:(oc + 1) * 128, t0 + tss * 512:t0 + tss * 512 + 512]),
                    waits=[("ln_ap", ln_done), ("rel_dn", b_dn + 8 * (4 * tb + ocg))],
                    inc="xload", inc_n=16)
            xld = p.n("xload")
            for jg in range(NJC // 4):
              buf = jg % 2
              wsem = f"wdl{buf}"
              c0 = ocg * 512
              p.op(wq_eng, lambda e, jg=jg, c0=c0, buf=buf: e.dma_start(
                  out=_r(wdb[:, buf * 2048:(buf + 1) * 2048], "p (j c) -> p j c", c=512),
                  in_=_r(wsrc_d[jg * 512:(jg + 1) * 512, c0:c0 + 512], "(j p) c -> p j c", p=128)),
                  waits=[("dn_mm", max(base_dn + 4 * jg - 4, base_dn)),
                         ("ln_ap", ln_done)], inc=wsem, inc_n=16)
              wld = p.n(wsem)
              if layer == 0:
                  p.op("vector", lambda e, buf=buf: e.tensor_scalar(
                      out=wdb[:, buf * 2048:(buf + 1) * 2048],
                      in0=wdb[:, buf * 2048:(buf + 1) * 2048],
                      scalar1=-128.0, scalar2=None, op0=ALU.add),
                      waits=[(wsem, wld)], inc="wdfix")
                  wfx = p.n("wdfix")
              for jcl in range(4):
                jc = 4 * jg + jcl
                if layer == 0:
                    w0 = [("wdfix", wfx)]
                else:
                    w0 = [(wsem, wld)]
                if jc == 0:
                    w0 = w0 + [("rel_dn", b_dn + 8 * (4 * tb + ocg)),
                               ("h_ok", b_hok + 88 * (tb + 1))]
                for oc4 in range(4):
                    for tss in range(2):
                        p.op("tensor", lambda e, oc4=oc4, tss=tss, jc=jc, buf=buf, jcl=jcl: e.matmul(
                            pb[oc4 * 2 + tss][:, :],
                            wdb[:, buf * 2048 + jcl * 512 + oc4 * 128:buf * 2048 + jcl * 512 + oc4 * 128 + 128],
                            hh[:, jc * 1024 + tss * 512:jc * 1024 + tss * 512 + 512],
                            start=(jc == 0), stop=(jc == NJC - 1)),
                            waits=(w0 if (oc4 == 0 and tss == 0) else ()),
                            inc=("dn_mm" if (oc4 == 3 and tss == 1) else None))
            mm_dn = p.n("dn_mm")
            for i in range(8):
                oc = ocg * 4 + i // 2
                tss = i % 2
                slot = i % 2
                sa = (lambda oc=oc: sds[:, oc:oc + 1])() if layer == 0 else 0.5
                ssem = f"sst{slot}"
                p.op("vector", lambda e, i=i, slot=slot, sa=sa: e.scalar_tensor_tensor(
                    out=yv[:, slot * 512:(slot + 1) * 512], in0=pb[i][:, :],
                    scalar=sa, in1=xf[:, i * 512:(i + 1) * 512],
                    op0=ALU.mult, op1=ALU.add),
                    waits=[("dn_mm", mm_dn), ("xload", xld), (ssem, p.n(ssem))],
                    inc="rel_dn")
                p.op("scalar", lambda e, oc=oc, tss=tss, slot=slot, t0=t0: e.dma_start(
                    out=yT[oc * 128:(oc + 1) * 128, t0 + tss * 512:t0 + tss * 512 + 512],
                    in_=yv[:, slot * 512:(slot + 1) * 512]),
                    waits=[("rel_dn", p.n("rel_dn"))], inc=ssem, inc_n=16)
    es.close()



def emit_attn_body(p, layer, xT_own, yT, W, pbq, fused, cc_tag="", quant_specs=None):
    pb, pbquad = pbq
    nh = NH
    hk = 2048 if layer == 0 else 512
    hkc = hk // 128
    rope = (layer == 0)
    alibi = (layer == 1)
    groups = [[h] for h in range(nh)] if layer == 0 else \
             [[g, g + 4, g + 8, g + 12] for g in range(4)]

    wq, wo = W["wq"], W["wo"]
    maskpat = W["maskpat"]
    ksrc = W["ksrc"]
    vsrc = W["vsrc"]

    if fused:
        kvKin = p.dram(f"kvKin{cc_tag}", [hk, T], BF16)
        kvVin = p.dram(f"kvVin{cc_tag}", [16, hkc, 128, 128], BF16)
        kvKout = p.dram(f"kvKout{cc_tag}", [2, hk, T], BF16)
        kvVout = p.dram(f"kvVout{cc_tag}", [2, 16, hkc, 128, 128], BF16)
    else:
        kvKout = p.dram(f"kvK{cc_tag}", [2, hk, T], BF16)
        kvVout = p.dram(f"kvV{cc_tag}", [2, 16, hkc, 128, 128], BF16)
    ctxd = p.dram(f"ctxd{cc_tag}", [D, T], BF16)

    esQ = ExitStack()
    QT = esQ.enter_context(p.nc.sbuf_tensor(p.uname("QT"), [128, nh * T], BF16))

    esA = ExitStack()

    def sbA(nm, shape, dt):
        return esA.enter_context(p.nc.sbuf_tensor(p.uname(nm), list(shape), dt))
    xf = sbA("xf", [128, 16 * 512], F32)
    xs = sbA("xs", [128, 2 * 512], BF16)
    xtmp = sbA("xtmp", [128, 512], F32)
    xq = xtmp[:, :].bitcast(BF16)
    xl = sbA("xl", [128, DC * T], BF16)
    wkb = sbA("wkb", [128, 2 * DC * 128], BF16)
    wvb = sbA("wvb", [128, DC * 256], BF16)
    ksg = sbA("ksg", [128, 2 * 512], BF16)
    vsg = sbA("vsg", [128, 2 * 512], BF16)
    rows = sbA("rows", [1, 2 * 512], F32)
    brow = sbA("brow", [1, 2 * 512], BF16)
    onec = sbA("onecA", [128, 1], BF16)
    oner = sbA("onerA", [1, 128], BF16)
    rta = sbA("rta", [64, 512], F32)
    rtb = sbA("rtb", [64, 512], F32)
    if rope:
        csb = sbA("csb", [64, 2 * T], BF16)

    p.op("vector", lambda e: e.memset(onec[:, :], 1.0))
    p.op("vector", lambda e: e.memset(oner[:, :], 1.0), inc="init")

    def ln_tsub(src, ts4):
        tc0 = ts4 * 512
        base_mm = p.n("ln_mm")
        base_row = p.n("ln_row")
        base_act = p.n("ln_act")
        base_bc = p.n("ln_bc")
        p.op("sync", lambda e, src=src, tc0=tc0: e.dma_start(
            out=_r(xf[:, :], "p (c t) -> p c t", t=512),
            in_=_r(src[:, tc0:tc0 + 512], "(c p) t -> p c t", p=128)),
            waits=[("ln_ap", p.n("ln_ap"))], inc="xload", inc_n=16)
        xld = p.n("xload")
        for dc in range(DC):
            sl = dc % 2
            p.op("vector", lambda e, dc=dc, sl=sl: e.tensor_copy(
                out=xq[:, sl * 512:(sl + 1) * 512], in_=xf[:, dc * 512:(dc + 1) * 512]),
                waits=[("xload", xld), ("ln_mm", base_mm + dc - 1)], inc="ln_xq")
            p.op("scalar", lambda e, dc=dc, sl=sl: e.activation(
                out=xs[:, sl * 512:(sl + 1) * 512], in_=xf[:, dc * 512:(dc + 1) * 512],
                func=AF.Square, bias=0.0, scale=1.0),
                waits=[("xload", xld), ("ln_mm", base_mm + dc - 1)], inc="ln_xs")
            w0 = [("ln_xq", p.n("ln_xq")), ("ln_xs", p.n("ln_xs"))]
            if dc == 0:
                w0 += [("ln_row", base_row), ("init", p.n("init"))]
            p.op("tensor", lambda e, sl=sl, dc=dc: e.matmul(
                pb[0][0:1, :], onec[:, :], xq[:, sl * 512:(sl + 1) * 512],
                start=(dc == 0), stop=(dc == DC - 1)), waits=w0)
            p.op("tensor", lambda e, sl=sl, dc=dc: e.matmul(
                pb[1][0:1, :], onec[:, :], xs[:, sl * 512:(sl + 1) * 512],
                start=(dc == 0), stop=(dc == DC - 1)), inc="ln_mm")
        m_, msq = rows[0:1, 0:512], rows[0:1, 512:1024]
        var, lnv = rows[0:1, 0:512], rows[0:1, 512:1024]
        bm, br = brow[0:1, 0:512], brow[0:1, 512:1024]
        p.op("vector", lambda e: e.tensor_scalar(out=m_, in0=pb[0][0:1, :],
             scalar1=INVD, scalar2=None, op0=ALU.mult),
             waits=[("ln_mm", base_mm + DC), ("ln_act", p.n("ln_act"))])
        p.op("vector", lambda e: e.tensor_copy(out=bm, in_=m_), inc="ln_row")
        p.op("vector", lambda e: e.tensor_tensor(out=msq, in0=m_, in1=m_, op=ALU.mult))
        p.op("vector", lambda e: e.scalar_tensor_tensor(out=var, in0=pb[1][0:1, :],
             scalar=INVD, in1=msq, op0=ALU.mult, op1=ALU.subtract), inc="ln_row")
        p.op("scalar", lambda e: e.activation(out=lnv, in_=var, func=AF.Ln,
             bias=EPS_LN, scale=1.0),
             waits=[("ln_row", base_row + 2), ("init", 1)])
        p.op("scalar", lambda e: e.activation(out=br, in_=lnv, func=AF.Exp,
             bias=0.0, scale=-0.5), inc="ln_act")
        p.op("tensor", lambda e: e.matmul(pb[2][:, :], oner[:, :], bm, start=True, stop=True),
             waits=[("ln_row", base_row + 2), ("ln_ap", p.n("ln_ap"))], inc="ln_bc")
        p.op("tensor", lambda e: e.matmul(pb[3][:, :], oner[:, :], br, start=True, stop=True),
             waits=[("ln_act", base_act + 1)], inc="ln_bc")
        for dc in range(DC):
            p.op("vector", lambda e, dc=dc: e.tensor_tensor(
                out=xtmp[:, :], in0=xf[:, dc * 512:(dc + 1) * 512], in1=pb[2][:, :],
                op=ALU.subtract),
                waits=[("ln_bc", base_bc + 2), ("ap_mm", p.n("ap_mm"))])
            p.op("vector", lambda e, dc=dc, off=dc * T + ts4 * 512: e.tensor_tensor(
                out=xl[:, off:off + 512], in0=xtmp[:, :], in1=pb[3][:, :],
                op=ALU.mult), inc="ln_ap")
        p.wait("tensor", [("ln_ap", p.n("ln_ap"))])

    def rope_evac(bank, csoff, dst_ap_fn, extra_waits, sem_inc):
        C = (lambda: csb[0:64, csoff:csoff + 512])()
        S_ = (lambda: csb[0:64, T + csoff:T + csoff + 512])()
        p.op("vector", lambda e, bank=bank, C=C: e.tensor_tensor(
            out=rta[:, :], in0=bank[0:64, :], in1=C, op=ALU.mult), waits=extra_waits)
        p.op("vector", lambda e, bank=bank, S_=S_: e.tensor_tensor(
            out=rtb[:, :], in0=bank[64:128, :], in1=S_, op=ALU.mult))
        p.op("vector", lambda e: e.tensor_tensor(
            out=dst_ap_fn(0), in0=rta[:, :], in1=rtb[:, :], op=ALU.subtract))
        p.op("vector", lambda e, bank=bank, C=C: e.tensor_tensor(
            out=rta[:, :], in0=bank[64:128, :], in1=C, op=ALU.mult))
        p.op("vector", lambda e, bank=bank, S_=S_: e.tensor_tensor(
            out=rtb[:, :], in0=bank[0:64, :], in1=S_, op=ALU.mult))
        p.op("vector", lambda e: e.tensor_tensor(
            out=dst_ap_fn(1), in0=rta[:, :], in1=rtb[:, :], op=ALU.add), inc=sem_inc)

    def kv_proj(kdst, vdst):
        ln_done = p.n("ln_ap")
        mm_hist = []
        for kc in range(hkc):
            kpar2 = kc % 2
            woffk = kpar2 * DC * 128
            p.op("gpsimd", lambda e, kc=kc, woffk=woffk: e.dma_start(
                out=_r(wkb[:, woffk:woffk + DC * 128], "p (c j) -> p c j", j=128),
                in_=_r(ksrc(kc), "(c p) j -> p c j", p=128)),
                waits=[("ap_mm", mm_hist[kc - 2] if kc >= 2 else 0)],
                inc=f"wpld{kpar2}", inc_n=16)
            wld = p.n(f"wpld{kpar2}")
            for ts4 in range(NS):
                ei = kc * NS + ts4
                bank = pb[4 + ei % 2]
                for dc in range(DC):
                    w0 = []
                    if dc == 0:
                        w0 = [(f"wpld{kpar2}", wld), ("ln_ap", ln_done),
                              ("ap_ev", p.n("ap_ev") - 1)]
                    p.op("tensor", lambda e, bank=bank, dc=dc, ts4=ts4, woffk=woffk: e.matmul(
                        bank[:, :], wkb[:, woffk + dc * 128:woffk + dc * 128 + 128],
                        xl[:, dc * T + ts4 * 512:dc * T + ts4 * 512 + 512],
                        start=(dc == 0), stop=(dc == DC - 1)),
                        waits=w0, inc=("ap_mm" if dc == DC - 1 else None))
                kpar = ei % 2
                ksl = (lambda kpar=kpar: ksg[:, kpar * 512:(kpar + 1) * 512])()
                ew = [("ap_mm", p.n("ap_mm")), (f"kst{kpar}", p.n(f"kst{kpar}"))]
                if rope:
                    ew.append(("csld", p.n("csld")))
                    rope_evac(bank, ts4 * 512,
                              lambda half, ksl=ksl: ksl[half * 64:(half + 1) * 64, :],
                              ew, "ap_ev")
                else:
                    p.op("vector", lambda e, bank=bank, ksl=ksl: e.tensor_copy(
                        out=ksl, in_=bank[:, :]), waits=ew, inc="ap_ev")
                p.op("scalar", lambda e, kc=kc, ts4=ts4, ksl=ksl, kdst=kdst, kpar=kpar: e.dma_start(
                    out=kdst[kc * 128:(kc + 1) * 128, ts4 * 512:ts4 * 512 + 512],
                    in_=ksl),
                    waits=[("ap_ev", p.n("ap_ev"))], inc=f"kst{kpar}", inc_n=16)
            mm_hist.append(p.n("ap_mm"))
        for vf in range(hk // 256):
            p.op("gpsimd", lambda e, vf=vf: e.dma_start(
                out=_r(wvb[:, :], "p (c j) -> p c j", j=256),
                in_=_r(vsrc(vf), "(c p) j -> p c j", p=128)),
                waits=[("ap_mm", p.n("ap_mm"))], inc="wpld", inc_n=16)
            wld = p.n("wpld")
            for tc in range(16):
                bank = pb[6 + tc % 2]
                for dc in range(DC):
                    w0 = []
                    if dc == 0:
                        w0 = [("wpld", wld), ("ln_ap", ln_done),
                              ("ap_ev2", p.n("ap_ev2") - 1)]
                    p.op("tensor", lambda e, bank=bank, dc=dc, tc=tc: e.matmul(
                        bank[:, 0:256], xl[:, dc * T + tc * 128:dc * T + tc * 128 + 128],
                        wvb[:, dc * 256:(dc + 1) * 256],
                        start=(dc == 0), stop=(dc == DC - 1)),
                        waits=w0, inc=("ap_mm" if dc == DC - 1 else None))
                vpar = tc % 2
                vsl = (lambda vpar=vpar: vsg[:, vpar * 512:vpar * 512 + 256])()
                p.op("vector", lambda e, bank=bank, vsl=vsl: e.tensor_copy(
                    out=vsl, in_=bank[:, 0:256]),
                    waits=[("ap_mm", p.n("ap_mm")), (f"vst{vpar}", p.n(f"vst{vpar}"))],
                    inc="ap_ev2")
                p.op("scalar", lambda e, tc=tc, vf=vf, vsl=vsl, vdst=vdst, vpar=vpar: e.dma_start(
                    out=_r(vdst[tc, vf * 2:(vf + 1) * 2, :, :], "v k h -> k v h"),
                    in_=_r(vsl, "k (v h) -> k v h", h=128)),
                    waits=[("ap_ev2", p.n("ap_ev2"))], inc=f"vst{vpar}", inc_n=16)

    def q_proj():
        ln_done = p.n("ln_ap")
        mm_hist = []
        kv_end = p.n("ap_mm")
        for h in range(nh):
            kpar2 = h % 2
            woffk = kpar2 * DC * 128
            p.op("gpsimd", lambda e, h=h, woffk=woffk: e.dma_start(
                out=_r(wkb[:, woffk:woffk + DC * 128], "p (c j) -> p c j", j=128),
                in_=_r(wq[:, h * 128:(h + 1) * 128], "(c p) j -> p c j", p=128)),
                waits=[("ap_mm", mm_hist[h - 2] if h >= 2 else kv_end)],
                inc=f"wpld{kpar2}", inc_n=16)
            wld = p.n(f"wpld{kpar2}")
            for ts4 in range(NS):
                bank = pb[4 + ts4 % 2]
                for dc in range(DC):
                    w0 = []
                    if dc == 0:
                        w0 = [(f"wpld{kpar2}", wld), ("ln_ap", ln_done),
                              ("ap_ev", p.n("ap_ev") - 1)]
                    p.op("tensor", lambda e, bank=bank, dc=dc, ts4=ts4, woffk=woffk: e.matmul(
                        bank[:, :], wkb[:, woffk + dc * 128:woffk + dc * 128 + 128],
                        xl[:, dc * T + ts4 * 512:dc * T + ts4 * 512 + 512],
                        start=(dc == 0), stop=(dc == DC - 1)),
                        waits=w0, inc=("ap_mm" if dc == DC - 1 else None))
                qoff = h * T + ts4 * 512
                ew = [("ap_mm", p.n("ap_mm"))]
                if rope:
                    ew.append(("csld", p.n("csld")))
                    rope_evac(bank, ts4 * 512,
                              lambda half, qoff=qoff: QT[half * 64:(half + 1) * 64,
                                                         qoff:qoff + 512],
                              ew, "ap_ev")
                else:
                    p.op("vector", lambda e, bank=bank, qoff=qoff: e.tensor_copy(
                        out=QT[:, qoff:qoff + 512], in_=bank[:, :]),
                        waits=ew, inc="ap_ev")
            mm_hist.append(p.n("ap_mm"))

    def load_cs(r):
        if not rope:
            return
        cosd, sind = W["cs"][r]
        p.op("gpsimd", lambda e, cosd=cosd: e.dma_start(out=csb[:, 0:T], in_=cosd[:, :]),
             waits=[("ap_ev", p.n("ap_ev")), ("ap_ev2", p.n("ap_ev2"))],
             inc="csld", inc_n=16)
        p.op("gpsimd", lambda e, sind=sind: e.dma_start(out=csb[:, T:2 * T], in_=sind[:, :]),
             inc="csld", inc_n=16)

    if fused:
        load_cs(2)
        for ts4 in range(NS):
            ln_tsub(xT_own, ts4)
        kv_proj(kvKin, kvVin)
        cw = [(f"kst{i}", p.n(f"kst{i}")) for i in range(2)] + \
             [(f"vst{i}", p.n(f"vst{i}")) for i in range(2)]
        p.op("gpsimd", lambda e: e.collective_compute(
            "AllGather", ALU.bypass,
            replica_groups=[[0, 1], [2, 3], [4, 5], [6, 7]],
            ins=[kvKin.ap().opt()],
            outs=[kvKout.ap().rearrange("r a b -> (r a) b").opt()]),
            waits=cw, inc="ccg")
        p.op("gpsimd", lambda e: e.collective_compute(
            "AllGather", ALU.bypass,
            replica_groups=[[0, 1], [2, 3], [4, 5], [6, 7]],
            ins=[kvVin.ap().rearrange("a b c d -> (a b c) d").opt()],
            outs=[kvVout.ap().rearrange("r a b c d -> (r a b c) d").opt()]),
            inc="ccg")
        ccv = p.n("ccg")
        q_proj()
    else:
        srcs = [W["xT_p0"], W["xT_p1"], xT_own]
        for r in range(2):
            load_cs(r)
            for ts4 in range(NS):
                ln_tsub(srcs[r], ts4)
            kv_proj(kvKout[r], kvVout[r])
        load_cs(2)
        for ts4 in range(NS):
            ln_tsub(srcs[2], ts4)
        q_proj()
        ccv = 0

    p.fence_all()
    esA.close()

    esB = ExitStack()

    def sbB(nm, shape, dt):
        return esB.enter_context(p.nc.sbuf_tensor(p.uname(nm), list(shape), dt))
    pat = sbB("pat", [128, 2 * MPS * QB], BF16)
    ksb = sbB("ksb", [128, 2 * 2 * T], BF16)
    vsb = sbB("vsb", [128, 2 * 2 * 32 * 128], BF16)
    esb = sbB("esb", [128, 2 * 2 * QB], BF16)
    onecB = sbB("onecB", [128, 1], BF16)
    onefr = sbB("onefr", [2, 128], BF16)
    rrow = sbB("rrow", [1, 512], F32)
    rh_ = sbB("rh", [1, 512], BF16)
    rl_ = sbB("rl", [1, 512], BF16)
    lnrow = sbB("lnrow", [1, 512], F32)
    ctx_sb = sbB("ctx_sb", [128, 2 * 512], F32)
    den_sb = sbB("den_sb", [1, 2 * 512], F32)
    bcs = sbB("bcs", [128, 512], F32)
    csg = sbB("csg", [128, 2 * 512], BF16)
    if alibi:
        ba = sbB("ba", [6, 2 * T], BF16)
        bb = sbB("bb", [6, T], BF16)
    qchunks = []
    if quant_specs is not None:
        qchunks = emit_quant_chunks(p, sbB, pb[7], quant_specs)
    qci = [0]

    def dispense_quant(k):
        while qci[0] < min(len(qchunks), k):
            qchunks[qci[0]]()
            qci[0] += 1

    p.op("vector", lambda e: e.memset(onecB[:, :], 1.0))
    p.op("vector", lambda e: e.memset(onefr[:, :], 1.0), inc="binit")
    b_binit = p.n("binit")
    p.op("sync", lambda e: e.dma_start(
        out=_r(pat[:, :], "p (r d q) -> p r d q", d=MPS, q=QB),
        in_=_r(maskpat[:, :, :, :], "r d p q -> p r d q")),
        waits=[("ccg", ccv)], inc="pld", inc_n=16)
    b_pld = p.n("pld")

    kv_iters = [(gi, s) for gi in range(len(groups)) for s in range(NS)]
    pend_tail = [None]
    sc_after, mm_after = [], []
    kld_at, vld_at = {}, {}

    def emit_kv_load(j):
        gi, s = kv_iters[j]
        nm = MPS * (s + 1)
        par = j % 2
        kw = [("ccg", ccv)]
        if j >= 2:
            kw += [("at_sc", sc_after[j - 2]), ("at_mm", mm_after[j - 2])]
        for r01 in range(2):
            p.op("sync", lambda e, r01=r01, gi=gi, nm=nm, par=par: e.dma_start(
                out=ksb[:, (par * 2 + r01) * T:(par * 2 + r01) * T + nm * 128],
                in_=kvKout[r01, gi * 128:(gi + 1) * 128, 0:nm * 128]),
                waits=kw, inc=f"kld{par}", inc_n=16)
            p.op("sync", lambda e, r01=r01, gi=gi, nm=nm, par=par: e.dma_start(
                out=_r(vsb[:, (par * 2 + r01) * 4096:(par * 2 + r01) * 4096 + nm * 128],
                       "k (m h) -> k m h", h=128),
                in_=_r(kvVout[r01, 0:nm, gi, :, :], "m k h -> k m h")),
                waits=kw, inc=f"vld{par}", inc_n=16)
        kld_at[j] = p.n(f"kld{par}")
        vld_at[j] = p.n(f"vld{par}")

    emit_kv_load(0)
    for j, (gi, s) in enumerate(kv_iters):
        if j + 1 < len(kv_iters):
            emit_kv_load(j + 1)
        hs = groups[gi]
        nm = MPS * (s + 1)
        par = j % 2
        kldv, vldv = kld_at[j], vld_at[j]
        ksem, vsem = f"kld{par}", f"vld{par}"
        koff = par * 2 * T
        voff = par * 2 * 4096
        for h in hs:
            if alibi:
                p.op("gpsimd", lambda e, h=h: e.dma_start(
                    out=_r(ba[:, :], "b (r c) -> b r c", r=2),
                    in_=_r(W["baT"][h, :, :, :], "r b c -> b r c")),
                    waits=[("at_sc", p.n("at_sc"))], inc="bld", inc_n=16)
                p.op("gpsimd", lambda e, h=h: e.dma_start(
                    out=bb[:, :], in_=W["bbT"][h, :, :]),
                    inc="bld", inc_n=16)
            bldv = p.n("bld") if alibi else 0
            n_i = 2 * nm
            npk = n_i // 2
            tiles = [(r01, m) for r01 in range(2) for m in range(nm)]
            sc_pair, mn_pair = {}, {}

            def emit_scores_pair(pk):
                for t_i in range(2):
                    r01, m = tiles[2 * pk + t_i]
                    i_g = 2 * pk + t_i
                    si = i_g % 4
                    bank = pb[si]
                    w0 = [(ksem, kldv), ("at_ex", p.n("at_ex"))]
                    if alibi:
                        w0.append(("bld", bldv))
                    p.op("tensor", lambda e, bank=bank, r01=r01, m=m, h=h, s=s, koff=koff: e.matmul(
                        bank[:, :], ksb[:, koff + r01 * T + m * 128:koff + r01 * T + m * 128 + 128],
                        QT[:, h * T + s * 512:h * T + s * 512 + 512],
                        start=True, stop=(not alibi)),
                        waits=w0, inc=(None if alibi else "at_sc"))
                    if alibi:
                        p.op("tensor", lambda e, bank=bank, r01=r01, m=m, s=s: e.matmul(
                            bank[:, :], ba[:, r01 * T + m * 128:r01 * T + m * 128 + 128],
                            bb[:, s * 512:s * 512 + 512],
                            start=False, stop=True), inc="at_sc")
                    if m >= MPS * s:
                        d = m - MPS * s
                        p.op("vector", lambda e, bank=bank, r01=r01, d=d: e.tensor_tensor(
                            out=bank[:, :], in0=bank[:, :],
                            in1=pat[:, (r01 * MPS + d) * QB:(r01 * MPS + d + 1) * QB],
                            op=ALU.min),
                            waits=[("at_sc", p.n("at_sc")), ("pld", b_pld)], inc="at_mn")
                sc_pair[pk] = p.n("at_sc")
                mn_pair[pk] = p.n("at_mn")

            emit_scores_pair(0)
            if pend_tail[0] is not None:
                pend_tail[0]()
                pend_tail[0] = None
            for pk in range(npk):
                if pk + 1 < npk:
                    emit_scores_pair(pk + 1)
                ew = [("at_sc", sc_pair[pk]), ("at_mn", mn_pair[pk]),
                      ("at_mm", p.n("at_mm") - 2)]
                q0 = (2 * pk % 4) * 512
                eoff = (pk % 2) * 1024
                p.op("scalar", lambda e, q0=q0, eoff=eoff: e.activation(
                    out=esb[:, eoff:eoff + 1024], in_=pbquad[:, q0:q0 + 1024],
                    func=AF.Exp, bias=0.0, scale=RSQHD),
                    waits=ew, inc="at_ex")
                for t_i in range(2):
                    r01, m = tiles[2 * pk + t_i]
                    i_g = 2 * pk + t_i
                    cw2 = [("at_ex", p.n("at_ex")), (vsem, vldv)]
                    if i_g == 0:
                        cw2.append(("at_cx", p.n("at_cx")))
                    p.op("tensor", lambda e, r01=r01, m=m, st=(i_g == 0), sp=(i_g == n_i - 1),
                         eo=eoff + t_i * 512, voff=voff: e.matmul(
                        pb[4][:, :], vsb[:, voff + r01 * 4096 + m * 128:voff + r01 * 4096 + m * 128 + 128],
                        esb[:, eo:eo + 512],
                        start=st, stop=sp), waits=cw2)
                    ow = [("binit", b_binit)]
                    if i_g == 0:
                        ow.append(("at_rc", p.n("at_rc")))
                    p.op("tensor", lambda e, st=(i_g == 0), sp=(i_g == n_i - 1),
                         eo=eoff + t_i * 512: e.matmul(
                        pb[5][0:1, :], onecB[:, :], esb[:, eo:eo + 512],
                        start=st, stop=sp), waits=ow, inc="at_mm")
            hpar = p.n("at_cx") % 2
            p.op("vector", lambda e, hpar=hpar: e.tensor_copy(
                out=ctx_sb[:, hpar * 512:(hpar + 1) * 512], in_=pb[4][:, :]),
                waits=[("at_mm", p.n("at_mm")), ("at_nm", p.n("at_nm") - 1)],
                inc="at_cx")
            p.op("vector", lambda e, hpar=hpar: e.tensor_copy(
                out=den_sb[0:1, hpar * 512:(hpar + 1) * 512], in_=pb[5][0:1, :]),
                waits=[("at_rp", p.n("at_rp") - 1)], inc="at_rc")
            p.op("scalar", lambda e, hpar=hpar: e.activation(
                out=lnrow[:, :], in_=den_sb[0:1, hpar * 512:(hpar + 1) * 512],
                func=AF.Ln, bias=0.0, scale=1.0),
                waits=[("at_rc", p.n("at_rc")), ("at_bcm", p.n("at_bcm"))])
            p.op("scalar", lambda e: e.activation(
                out=rrow[:, :], in_=lnrow[:, :], func=AF.Exp, bias=0.0, scale=-1.0),
                inc="at_rp")

            def tail_post(h=h, s=s, hpar=hpar, rp=p.n("at_rp")):
                p.op("vector", lambda e: e.tensor_copy(out=rh_[:, :], in_=rrow[:, :]),
                     waits=[("at_rp", rp), ("at_bcm", p.n("at_bcm"))])
                p.op("vector", lambda e: e.tensor_tensor(out=rl_[:, :], in0=rrow[:, :],
                     in1=rh_[:, :], op=ALU.subtract), inc="at_hl")
                p.op("tensor", lambda e: e.matmul(pb[6][:, :], onefr[0:1, :], rh_[:, :],
                     start=True, stop=False),
                     waits=[("at_hl", p.n("at_hl")), ("at_bc", p.n("at_bc"))])
                p.op("tensor", lambda e: e.matmul(pb[6][:, :], onefr[0:1, :], rl_[:, :],
                     start=False, stop=True), inc="at_bcm")
                p.op("vector", lambda e: e.tensor_copy(out=bcs[:, :], in_=pb[6][:, :]),
                     waits=[("at_bcm", p.n("at_bcm"))], inc="at_bc")
                cpar = p.n("at_nm") % 2
                p.op("vector", lambda e, cpar=cpar, hpar=hpar: e.tensor_tensor(
                    out=csg[:, cpar * 512:(cpar + 1) * 512],
                    in0=ctx_sb[:, hpar * 512:(hpar + 1) * 512],
                    in1=bcs[:, :], op=ALU.mult),
                    waits=[(f"cst{cpar}", p.n(f"cst{cpar}"))], inc="at_nm")
                p.op("gpsimd", lambda e, h=h, s=s, cpar=cpar: e.dma_start(
                    out=ctxd[h * 128:(h + 1) * 128, s * 512:s * 512 + 512],
                    in_=csg[:, cpar * 512:(cpar + 1) * 512]),
                    waits=[("at_nm", p.n("at_nm"))], inc=f"cst{cpar}", inc_n=16)
            pend_tail[0] = tail_post
            if qchunks:
                hsi = p.n("at_cx")
                dispense_quant(1 + (len(qchunks) * hsi) // (len(kv_iters) * len(hs)))
        sc_after.append(p.n("at_sc"))
        mm_after.append(p.n("at_mm"))
    if pend_tail[0] is not None:
        pend_tail[0]()
        pend_tail[0] = None
    dispense_quant(len(qchunks))

    p.fence_all()
    esB.close()
    esQ.close()

    esC = ExitStack()

    def sbC(nm, shape, dt):
        return esC.enter_context(p.nc.sbuf_tensor(p.uname(nm), list(shape), dt))
    cbuf = sbC("cbuf", [128, DC * T], BF16)
    wob = sbC("wob", [128, 2 * DC * 128], BF16)
    yvC = sbC("yvC", [128, 2 * 512], F32)
    xres = sbC("xres", [128, 2 * 512], F32)
    if layer == 1:
        boS = sbC("boS", [128, DC], F32)
        p.op("sync", lambda e: e.dma_start(out=boS[:, :], in_=W["boR"][:, :]),
             inc="cld", inc_n=16)
    p.op("sync", lambda e: e.dma_start(
        out=_r(cbuf[:, :], "p (c t) -> p c t", t=T),
        in_=_r(ctxd[:, :], "(c p) t -> p c t", p=128)), inc="cld", inc_n=16)
    cldv = p.n("cld")
    ec0 = p.n("oc_ev")
    ec = 0
    for oc in range(DC):
        opar = oc % 2
        wsemo = f"wold{opar}"
        woffo = opar * DC * 128
        p.op("gpsimd", lambda e, oc=oc, woffo=woffo: e.dma_start(
            out=_r(wob[:, woffo:woffo + DC * 128], "p (c j) -> p c j", j=128),
            in_=_r(wo[:, oc * 128:(oc + 1) * 128], "(c p) j -> p c j", p=128)),
            waits=[("oc_mm", max(0, p.n("oc_mm") - 4))], inc=wsemo, inc_n=16)
        wldv = p.n(wsemo)
        for ts4 in range(NS):
            bank = pb[ec % 4]
            for hc in range(DC):
                w0 = []
                if hc == 0:
                    w0 = [(wsemo, wldv), ("cld", cldv),
                          ("oc_ev", max(0, ec0 + ec - 3))]
                p.op("tensor", lambda e, bank=bank, hc=hc, ts4=ts4, woffo=woffo: e.matmul(
                    bank[:, :], wob[:, woffo + hc * 128:woffo + hc * 128 + 128],
                    cbuf[:, hc * T + ts4 * 512:hc * T + ts4 * 512 + 512],
                    start=(hc == 0), stop=(hc == DC - 1)),
                    waits=w0, inc=("oc_mm" if hc == DC - 1 else None))
            par = ec % 2
            p.op("sync", lambda e, oc=oc, ts4=ts4, par=par: e.dma_start(
                out=xres[:, par * 512:(par + 1) * 512],
                in_=xT_own[oc * 128:(oc + 1) * 128, ts4 * 512:ts4 * 512 + 512]),
                waits=[("oc_ev", max(0, ec0 + ec - 1))], inc=f"xrl{par}", inc_n=16)
            sa = (lambda oc=oc: boS[:, oc:oc + 1])() if layer == 1 else 0.0
            p.op("vector", lambda e, bank=bank, par=par, sa=sa: e.scalar_tensor_tensor(
                out=yvC[:, par * 512:(par + 1) * 512], in0=bank[:, :], scalar=sa,
                in1=xres[:, par * 512:(par + 1) * 512], op0=ALU.add, op1=ALU.add),
                waits=[("oc_mm", p.n("oc_mm")), (f"xrl{par}", p.n(f"xrl{par}")),
                       (f"yst{par}", p.n(f"yst{par}")), ("cld", cldv)],
                inc="oc_ev")
            p.op("scalar", lambda e, oc=oc, ts4=ts4, par=par: e.dma_start(
                out=yT[oc * 128:(oc + 1) * 128, ts4 * 512:ts4 * 512 + 512],
                in_=yvC[:, par * 512:(par + 1) * 512]),
                waits=[("oc_ev", p.n("oc_ev"))], inc=f"yst{par}", inc_n=16)
            ec += 1
    esC.close()



def attn_weight_handles(p, layer, pref=""):
    W = {}
    if layer == 0:
        W["wq"] = p.dram(pref + "wq", [D, D], F32, kind="ExternalInput")
        wk = p.dram(pref + "wk", [D, D], F32, kind="ExternalInput")
        wv = p.dram(pref + "wv", [D, D], F32, kind="ExternalInput")
        W["wo"] = p.dram(pref + "wo", [D, D], F32, kind="ExternalInput")
        W["ksrc"] = lambda kc: wk[:, kc * 128:(kc + 1) * 128]
        W["vsrc"] = lambda vf: wv[:, vf * 256:(vf + 1) * 256]
        W["cs"] = [(p.dram(pref + "cosk0", [64, T], F32, kind="ExternalInput"),
                    p.dram(pref + "sink0", [64, T], F32, kind="ExternalInput")),
                   (p.dram(pref + "cosk1", [64, T], F32, kind="ExternalInput"),
                    p.dram(pref + "sink1", [64, T], F32, kind="ExternalInput")),
                   (p.dram(pref + "cosq", [64, T], F32, kind="ExternalInput"),
                    p.dram(pref + "sinq", [64, T], F32, kind="ExternalInput"))]
    else:
        W["wq"] = p.dram(pref + "wq", [D, D], F32, kind="ExternalInput")
        wkv = p.dram(pref + "wkv", [D, 1024], F32, kind="ExternalInput")
        W["wo"] = p.dram(pref + "wo", [D, D], F32, kind="ExternalInput")
        W["boR"] = p.dram(pref + "boR", [128, DC], F32, kind="ExternalInput")
        W["baT"] = p.dram(pref + "baT", [NH, 2, 6, T], BF16, kind="ExternalInput")
        W["bbT"] = p.dram(pref + "bbT", [NH, 6, T], BF16, kind="ExternalInput")
        W["ksrc"] = lambda kc: wkv[:, kc * 128:(kc + 1) * 128]
        W["vsrc"] = lambda vf: wkv[:, 512 + vf * 256:512 + (vf + 1) * 256]
    W["maskpat"] = p.dram(pref + "maskpat", [2, MPS, 128, QB], BF16, kind="ExternalInput")
    return W


def ffn_weight_handles(p, layer, pref=""):
    if layer == 0:
        return {
            "gwT": p.dram(pref + "gwT", [D, HID], I32, kind="ExternalInput"),
            "uwT": p.dram(pref + "uwT", [D, HID], I32, kind="ExternalInput"),
            "dwT": p.dram(pref + "dwT", [HID, D], I32, kind="ExternalInput"),
            "sg": p.dram(pref + "sg", [128, NJC], F32, kind="ExternalInput"),
            "su": p.dram(pref + "su", [128, NJC], F32, kind="ExternalInput"),
            "sd": p.dram(pref + "sd", [128, DC], F32, kind="ExternalInput"),
        }
    return {
        "w1T": p.dram(pref + "w1T", [D, HID], F32, kind="ExternalInput"),
        "w2T": p.dram(pref + "w2T", [D, HID], F32, kind="ExternalInput"),
        "w3T": p.dram(pref + "w3T", [HID, D], F32, kind="ExternalInput"),
        "w1q": p.dram(pref + "w1q", [D, HID], BF16),
        "w2q": p.dram(pref + "w2q", [D, HID], BF16),
        "w3q": p.dram(pref + "w3q", [HID, D], BF16),
    }


def build_ffn(layer):
    p = K(f"ffn{layer}")
    xT = p.dram("xT", [D, T], F32, kind="ExternalInput")
    yT = p.dram("yT", [D, T], F32, kind="ExternalOutput")
    W = ffn_weight_handles(p, layer)
    pb, pbquad = p.psum_pool()
    p.constf(EPS_LN)
    S = emit_scales_l0(p, W) if layer == 0 else None
    if layer == 1:
        emit_quant_body(p, pb, [(W["w1T"], W["w1q"], D, HID),
                                (W["w2T"], W["w2q"], D, HID),
                                (W["w3T"], W["w3q"], HID, D)])
        p.fence_all()
    emit_ffn_body(p, layer, xT, yT, W, pb, S)
    dma_sems = [s for s in p.ctr if s.startswith(("sst", "wld", "wdl", "xload",
                                                  "qld", "qst", "sload"))]
    p.wait("gpsimd", [(s, p.n(s)) for s in dma_sems])
    return p


def build_attn(layer):
    p = K(f"att{layer}")
    xT_own = p.dram("xT_own", [D, T], F32, kind="ExternalInput")
    yT = p.dram("yT", [D, T], F32, kind="ExternalOutput")
    W = attn_weight_handles(p, layer)
    W["xT_p0"] = p.dram("xT_p0", [D, T], F32, kind="ExternalInput")
    W["xT_p1"] = p.dram("xT_p1", [D, T], F32, kind="ExternalInput")
    pb, pbquad = p.psum_pool()
    p.constf(EPS_LN)
    emit_attn_body(p, layer, xT_own, yT, W, (pb, pbquad), fused=False)
    dma_sems = [s for s in p.ctr if s.startswith(("yst", "xrl", "wold", "cld"))]
    p.wait("gpsimd", [(s, p.n(s)) for s in dma_sems])
    return p


def build_mega():
    p = K("mega")
    xT = p.dram("xT", [D, T], F32, kind="ExternalInput")
    yT = p.dram("yT", [D, T], F32, kind="ExternalOutput")
    Wf0 = ffn_weight_handles(p, 0, "f0_")
    Wf1 = ffn_weight_handles(p, 1, "f1_")
    Wa0 = attn_weight_handles(p, 0, "a0_")
    Wa1 = attn_weight_handles(p, 1, "a1_")
    xA = p.dram("xA", [D, T], F32)
    xB = p.dram("xB", [D, T], F32)
    xC = p.dram("xC", [D, T], F32)
    xD = p.dram("xD", [D, T], F32)
    xE = p.dram("xE", [D, T], F32)
    pb, pbquad = p.psum_pool()
    p.constf(EPS_LN)
    S0 = emit_scales_l0(p, Wf0)
    emit_quant_body(p, pb, [(Wf1["w1T"], Wf1["w1q"], D, HID),
                            (Wf1["w2T"], Wf1["w2q"], D, HID),
                            (Wf1["w3T"], Wf1["w3q"], HID, D)])
    p.fence_all()
    emit_ffn_body(p, 0, xT, xA, Wf0, pb, S0)
    p.fence_all()
    emit_attn_body(p, 0, xA, xB, Wa0, (pb, pbquad), fused=True, cc_tag="a0")
    p.fence_all()
    emit_ffn_body(p, 0, xB, xC, Wf0, pb, S0)
    p.fence_all()
    emit_ffn_body(p, 1, xC, xD, Wf1, pb)
    p.fence_all()
    emit_attn_body(p, 1, xD, xE, Wa1, (pb, pbquad), fused=True, cc_tag="a1")
    p.fence_all()
    emit_ffn_body(p, 1, xE, yT, Wf1, pb)
    p.fence_all()
    return p



def shard_x(x):
    out = []
    for c in range(N_CORES):
        b, par = c // 2, c % 2
        out.append(np.ascontiguousarray(x[b, par::2, :].T))
    return out


def unshard_x(shards):
    x = np.empty((4, 4096, 2048), np.float32)
    for c in range(N_CORES):
        b, par = c // 2, c % 2
        x[b, par::2, :] = shards[c].T
    return x


def ffn_inputs(layer, inp, pref=""):
    if layer == 0:
        return {
            pref + "gwT": np.ascontiguousarray(np.asarray(inp["L0_ffn_wq_gate"]).T),
            pref + "uwT": np.ascontiguousarray(np.asarray(inp["L0_ffn_wq_up"]).T),
            pref + "dwT": np.ascontiguousarray(np.asarray(inp["L0_ffn_wq_down"]).T),
            pref + "sg": np.ascontiguousarray(np.asarray(inp["L0_ffn_s_gate"]).reshape(NJC, 128).T),
            pref + "su": np.ascontiguousarray(np.asarray(inp["L0_ffn_s_up"]).reshape(NJC, 128).T),
            pref + "sd": np.ascontiguousarray(np.asarray(inp["L0_ffn_s_down"]).reshape(DC, 128).T),
        }
    return {
        pref + "w1T": np.ascontiguousarray(np.asarray(inp["L1_ffn_w1"]).T),
        pref + "w2T": np.ascontiguousarray(np.asarray(inp["L1_ffn_w2"]).T),
        pref + "w3T": np.ascontiguousarray(np.asarray(inp["L1_ffn_w3"]).T),
    }


def _split3(a):
    a = a.astype(np.float32)
    a0 = a.astype(BF).astype(np.float32)
    a1 = (a - a0).astype(BF).astype(np.float32)
    a2 = (a - a0 - a1).astype(BF)
    return np.stack([a0.astype(BF), a1.astype(BF), a2], 0)


def _alibi_slopes(n):
    start = 2.0 ** (-2.0 ** (-(np.log2(n) - 3.0)))
    return np.array([start ** (i + 1) for i in range(n)], np.float64)


def attn_inputs(layer, inp, par, pref=""):
    out = {}
    if layer == 0:
        out[pref + "wq"] = np.ascontiguousarray(np.asarray(inp["L0_attn_Wq"]).T)
        out[pref + "wk"] = np.ascontiguousarray(np.asarray(inp["L0_attn_Wk"]).T)
        out[pref + "wv"] = np.ascontiguousarray(np.asarray(inp["L0_attn_Wv"]).T)
        out[pref + "wo"] = np.ascontiguousarray(np.asarray(inp["L0_attn_Wo"]).T)
        inv_freq = 1.0 / 10000.0 ** (np.arange(0, 128, 2, dtype=np.float32) / 128.0)
        tpos = np.arange(4096, dtype=np.float32)
        fr = np.outer(tpos, inv_freq)
        cos = np.cos(fr).astype(np.float32)
        sin = np.sin(fr).astype(np.float32)
        out[pref + "cosk0"] = np.ascontiguousarray(cos[0::2].T)
        out[pref + "sink0"] = np.ascontiguousarray(sin[0::2].T)
        out[pref + "cosk1"] = np.ascontiguousarray(cos[1::2].T)
        out[pref + "sink1"] = np.ascontiguousarray(sin[1::2].T)
        out[pref + "cosq"] = np.ascontiguousarray(cos[par::2].T)
        out[pref + "sinq"] = np.ascontiguousarray(sin[par::2].T)
    else:
        out[pref + "wq"] = np.ascontiguousarray(np.asarray(inp["L1_attn_Wq"]))
        out[pref + "wkv"] = np.ascontiguousarray(np.asarray(inp["L1_attn_Wkv"]))
        out[pref + "wo"] = np.ascontiguousarray(np.asarray(inp["L1_attn_Wo"]))
        out[pref + "boR"] = np.ascontiguousarray(
            np.asarray(inp["L1_attn_bo"]).reshape(DC, 128).T.astype(np.float32))
        sl = _alibi_slopes(NH)
        S128 = np.sqrt(128.0)
        l = np.arange(T, dtype=np.float64)
        baT = np.zeros((NH, 2, 6, T), BF)
        bbT = np.zeros((NH, 6, T), BF)
        for h in range(NH):
            for r in range(2):
                A = S128 * sl[h] * (2 * l + r)
                baT[h, r, 0:3] = _split3(A)
                baT[h, r, 3:6] = np.ones((3, T), BF)
            Bv = -S128 * sl[h] * (2 * l + par)
            bbT[h, 0:3] = np.ones((3, T), BF)
            bbT[h, 3:6] = _split3(Bv)
        out[pref + "baT"] = baT
        out[pref + "bbT"] = bbT
    kk = np.arange(128)[:, None]
    qq = np.arange(QB)[None, :]
    mp = np.zeros((2, MPS, 128, QB), BF)
    for r in range(2):
        eps = -1 if (par == 0 and r == 1) else 0
        for d in range(MPS):
            valid = kk <= (qq - 128 * d + eps)
            mp[r, d] = np.where(valid, np.float32(1e30), np.float32(-1e30)).astype(BF)
    out[pref + "maskpat"] = mp
    return out


_prog_cache = {}


def get_prog(kind, layer=0):
    key = (kind, layer)
    if key not in _prog_cache:
        if kind == "ffn":
            _prog_cache[key] = build_ffn(layer).finish()
        elif kind == "att":
            _prog_cache[key] = build_attn(layer).finish()
        elif kind == "mega":
            _prog_cache[key] = build_mega().finish()
        else:
            raise KeyError(kind)
    return _prog_cache[key]


def run_ffn(layer, x_shards, inp):
    nc = get_prog("ffn", layer)
    wmap = ffn_inputs(layer, inp)
    in_maps = [dict(wmap, xT=xs) for xs in x_shards]
    res = run_bass_kernel_spmd(nc, in_maps, core_ids=list(range(N_CORES)))
    return [np.asarray(res.results[c]["yT"]) for c in range(N_CORES)]


def run_attn(layer, x_shards, inp):
    nc = get_prog("att", layer)
    wmaps = [attn_inputs(layer, inp, 0), attn_inputs(layer, inp, 1)]
    in_maps = []
    for c in range(N_CORES):
        b, par = c // 2, c % 2
        in_maps.append(dict(wmaps[par], xT_own=x_shards[c],
                            xT_p0=x_shards[2 * b], xT_p1=x_shards[2 * b + 1]))
    res = run_bass_kernel_spmd(nc, in_maps, core_ids=list(range(N_CORES)))
    return [np.asarray(res.results[c]["yT"]) for c in range(N_CORES)]


def kernel(**inputs):
    inputs = {k: np.asarray(v) for k, v in inputs.items()}
    nc = get_prog("mega")
    base = {}
    base.update(ffn_inputs(0, inputs, "f0_"))
    base.update(ffn_inputs(1, inputs, "f1_"))
    wa = [dict(attn_inputs(0, inputs, par, "a0_"),
               **attn_inputs(1, inputs, par, "a1_")) for par in range(2)]
    x_shards = shard_x(inputs["L0_attn_hidden_states"])
    in_maps = []
    for c in range(N_CORES):
        par = c % 2
        m = dict(base)
        m.update(wa[par])
        m["xT"] = x_shards[c]
        in_maps.append(m)
    res = run_bass_kernel_spmd(nc, in_maps, core_ids=list(range(N_CORES)))
    return unshard_x([np.asarray(res.results[c]["yT"]) for c in range(N_CORES)])


def kernel_unfused(**inputs):
    inputs = {k: np.asarray(v) for k, v in inputs.items()}
    x = shard_x(inputs["L0_attn_hidden_states"])
    x = run_ffn(0, x, inputs)
    x = run_attn(0, x, inputs)
    x = run_ffn(0, x, inputs)
    x = run_ffn(1, x, inputs)
    x = run_attn(1, x, inputs)
    x = run_ffn(1, x, inputs)
    return unshard_x(x)


# version 10
# speedup vs baseline: 1.2103x; 1.0054x over previous
import numpy as np
import ml_dtypes
from contextlib import ExitStack
import concourse.bass as bass
import concourse.mybir as mybir
from concourse.bass_utils import run_bass_kernel_spmd

F32 = mybir.dt.float32
BF16 = mybir.dt.bfloat16
I32 = mybir.dt.int32
AF = mybir.ActivationFunctionType
ALU = mybir.AluOpType
AX = mybir.AxisListType
BF = ml_dtypes.bfloat16

N_CORES = 8
D = 2048
T = 2048
HID = 5632
DC = D // 128
NH = 16
NJC = HID // 128
EPS_LN = 1e-5
INVD = 1.0 / D
NS = 4
QB = 512
KC = 128
MPS = QB // KC
RSQHD = float(1.0 / np.sqrt(128.0))

ENGINES = ("tensor", "scalar", "vector", "gpsimd", "sync")


class K:
    def __init__(self, name):
        self.nc = bass.Bass("TRN2", target_bir_lowering=False, debug=False)
        self.name = name
        self.q = {e: [] for e in ENGINES}
        self.ctr = {}
        self._semh = {}
        self._waited = {e: {} for e in ENGINES}
        self.es = ExitStack()
        self._uid = 0

    def sem(self, name):
        if name not in self._semh:
            self._semh[name] = self.es.enter_context(self.nc.semaphore(name))
            self.ctr[name] = 0
        return self._semh[name]

    def uname(self, nm):
        self._uid += 1
        return f"{nm}_{self._uid}"

    def sbuf(self, name, shape, dt):
        return self.es.enter_context(self.nc.sbuf_tensor(self.uname(name), list(shape), dt))

    def psum(self, name, shape, dt=F32):
        return self.es.enter_context(self.nc.psum_tensor(self.uname(name), list(shape), dt))

    def psum_pool(self):
        quad = self.psum("pbq", [128, 2048])
        pb = [quad[:, i * 512:(i + 1) * 512] for i in range(4)]
        pb += [self.psum("pb", [128, 512]) for _ in range(4)]
        return pb, quad

    def dram(self, name, shape, dt, kind="Internal"):
        return self.nc.dram_tensor(name, list(shape), dt, kind=kind)

    def op(self, eng, build, waits=(), inc=None, inc_n=1):
        wl = []
        for s, v in waits:
            if v <= 0:
                continue
            assert v <= self.ctr.get(s, 0), \
                f"wait {s}>={v} exceeds planned {self.ctr.get(s, 0)}"
            if self._waited[eng].get(s, -1) >= v:
                continue
            self._waited[eng][s] = v
            wl.append((self.sem(s), v))
        inch = None
        if inc is not None:
            inch = self.sem(inc)
            self.ctr[inc] += inc_n

        def fn(e, wl=wl, build=build, inch=inch, inc_n=inc_n):
            for h, v in wl:
                e.wait_ge(h, v)
            r = build(e)
            if inch is not None:
                r.then_inc(inch, inc_n)
        self.q[eng].append(fn)

    def wait(self, eng, waits):
        self.op(eng, lambda e: None, waits=waits)

    def fence_all(self):
        w = [(s, self.ctr[s]) for s in self.ctr]
        for e in ENGINES:
            self.wait(e, w)

    def constf(self, value):
        if not hasattr(self, "_cstbuf"):
            self._cstbuf = self.sbuf("cst", [1, 16], F32)
            self._csts = {}
        if value not in self._csts:
            i = len(self._csts)
            assert i < 16
            self._csts[value] = i
            ap = self._cstbuf[0:1, i:i + 1]
            self.nc.const_aps.register(F32, value, ap)
            self.op("vector", lambda e, ap=ap, value=value: e.memset(ap, value),
                    inc="init")

    def n(self, semname):
        return self.ctr.get(semname, 0)

    def finish(self):
        with self.nc.Block() as block:
            for e in ENGINES:
                lst = self.q[e]
                if not lst:
                    continue
                dec = getattr(block, e)

                def body(eng, lst=lst):
                    for fn in lst:
                        fn(eng)
                dec(body)
        self.es.close()
        return self.nc


def _r(ap, spec, **kw):
    return ap.rearrange(spec, **kw)



def emit_scales_l0(p, W):
    sgs = p.sbuf("sgs", [128, NJC], F32)
    sus = p.sbuf("sus", [128, NJC], F32)
    sds = p.sbuf("sds", [128, DC], F32)
    p.op("sync", lambda e: e.dma_start(out=sgs[:, :], in_=W["sg"][:, :]), inc="sload", inc_n=16)
    p.op("sync", lambda e: e.dma_start(out=sus[:, :], in_=W["su"][:, :]), inc="sload", inc_n=16)
    p.op("sync", lambda e: e.dma_start(out=sds[:, :], in_=W["sd"][:, :]), inc="sload", inc_n=16)
    p.op("vector", lambda e: e.tensor_scalar(out=sds[:, :], in0=sds[:, :],
                                             scalar1=0.5, scalar2=None, op0=ALU.mult),
         waits=[("sload", p.n("sload"))], inc="init")
    return {"sgs": sgs, "sus": sus, "sds": sds,
            "ready": [("sload", p.n("sload")), ("init", p.n("init"))]}



def emit_quant_body(p, pb, specs):
    es2 = ExitStack()

    def sb(nm, shape, dt):
        return es2.enter_context(p.nc.sbuf_tensor(p.uname(nm), list(shape), dt))
    qw = sb("qw", [128, 4 * 2816], F32)
    qp = sb("qp", [128, 2 * 2816], BF16)
    qn = sb("qn", [128, 2 * 2816], BF16)
    qo = sb("qo", [128, 2 * 2816], BF16)
    strips = sb("strips", [128, 64], F32)
    onef = sb("onef", [128, 1], F32)
    onefr = sb("onefr", [1, 128], F32)
    gval = sb("gval", [128, 4], F32)
    p.op("vector", lambda e: e.memset(onef[:, :], 1.0))
    p.op("vector", lambda e: e.memset(onefr[:, :], 1.0), inc="qrow")
    base_qbc = p.n("qbc")

    def tiles_of(rows, cols):
        w = 2816 if cols == HID else 2048
        return [(rc * 128, cc * w, w) for rc in range(rows // 128)
                for cc in range(cols // w)]

    for mi, (src, dst, rows, cols) in enumerate(specs):
        tl = tiles_of(rows, cols)
        nt = len(tl)
        base_abs = p.n("qabs")
        for i, (r0, c0, w) in enumerate(tl):
            sl = i % 2
            qeng = "scalar" if sl == 0 else "sync"
            qsem = f"ql{sl}"
            p.op(qeng, lambda e, src=src, r0=r0, c0=c0, w=w, sl=sl: e.dma_start(
                out=qw[:, sl * 2816:sl * 2816 + w], in_=src[r0:r0 + 128, c0:c0 + w]),
                waits=[("qabs", base_abs + i - 1), ("qbc", base_qbc + mi)],
                inc=qsem, inc_n=16)
            p.op("scalar", lambda e, w=w, sl=sl, i=i: e.activation(
                out=qw[:, sl * 2816:sl * 2816 + w], in_=qw[:, sl * 2816:sl * 2816 + w],
                func=AF.Abs, bias=0.0, scale=1.0, accum_out=strips[:, i:i + 1]),
                waits=[(qsem, p.n(qsem))], inc="qabs")
        p.op("vector", lambda e, nt=nt: e.tensor_reduce(
            out=strips[:, 0:1], in_=strips[:, 0:nt], axis=AX.X, op=ALU.add),
            waits=[("qabs", base_abs + nt)], inc="qrow")
        p.op("tensor", lambda e: e.matmul(pb[7][0:1, 0:1], strips[:, 0:1], onef[:, :],
             start=True, stop=True), waits=[("qrow", p.n("qrow"))], inc="qmm")
        nelem = float(rows * cols)
        p.op("vector", lambda e, nelem=nelem: e.tensor_scalar(
            out=gval[0:1, 0:1], in0=pb[7][0:1, 0:1],
            scalar1=0.5 / nelem, scalar2=0.5e-8, op0=ALU.mult, op1=ALU.add),
            waits=[("qmm", p.n("qmm"))], inc="qrow")
        p.op("tensor", lambda e: e.matmul(pb[6][:, 0:1], onefr[:, :], gval[0:1, 0:1],
             start=True, stop=True), waits=[("qrow", p.n("qrow"))], inc="qmm")
        p.op("vector", lambda e: e.tensor_copy(out=gval[:, 0:1], in_=pb[6][:, 0:1]),
             waits=[("qmm", p.n("qmm"))])
        p.op("vector", lambda e: e.tensor_scalar(
            out=gval[:, 1:2], in0=pb[6][:, 0:1], scalar1=-1.0, scalar2=None,
            op0=ALU.mult), inc="qbc")
        base_dv = p.n("qdv")
        for i, (r0, c0, w) in enumerate(tl):
            sl = 2 + i % 2
            sl2 = i % 2
            qeng = "scalar" if sl2 == 0 else "sync"
            qsem = f"ql{sl}"
            p.op(qeng, lambda e, src=src, r0=r0, c0=c0, w=w, sl=sl: e.dma_start(
                out=qw[:, sl * 2816:sl * 2816 + w], in_=src[r0:r0 + 128, c0:c0 + w]),
                waits=[("qbc", base_qbc + mi + 1), ("qdv", base_dv + 3 * (i - 1))],
                inc=qsem, inc_n=16)
            ld = p.n(qsem)
            p.op("vector", lambda e, w=w, sl=sl, sl2=sl2: e.tensor_scalar(
                out=qp[:, sl2 * 2816:sl2 * 2816 + w], in0=qw[:, sl * 2816:sl * 2816 + w],
                scalar1=gval[:, 0:1], scalar2=None, op0=ALU.is_ge),
                waits=[(qsem, ld)], inc="qdv")
            p.op("vector", lambda e, w=w, sl=sl, sl2=sl2: e.tensor_scalar(
                out=qn[:, sl2 * 2816:sl2 * 2816 + w], in0=qw[:, sl * 2816:sl * 2816 + w],
                scalar1=gval[:, 1:2], scalar2=None, op0=ALU.is_le), inc="qdv")
            p.op("vector", lambda e, w=w, sl2=sl2: e.tensor_tensor(
                out=qo[:, sl2 * 2816:sl2 * 2816 + w], in0=qp[:, sl2 * 2816:sl2 * 2816 + w],
                in1=qn[:, sl2 * 2816:sl2 * 2816 + w],
                op=ALU.subtract),
                waits=[(f"qst{sl2}", p.n(f"qst{sl2}"))], inc="qdv")
            p.op("gpsimd", lambda e, dst=dst, r0=r0, c0=c0, w=w, sl2=sl2: e.dma_start(
                out=dst[r0:r0 + 128, c0:c0 + w], in_=qo[:, sl2 * 2816:sl2 * 2816 + w]),
                waits=[("qdv", p.n("qdv"))], inc=f"qst{sl2}", inc_n=16)
    p.wait("sync", [("qst0", p.n("qst0")), ("qst1", p.n("qst1"))])
    es2.close()




def emit_quant_chunks(p, sb, bank, specs):
    QWW = 1408
    qw = sb("qw", [128, 2 * QWW], F32)
    qp = sb("qp", [128, QWW], BF16)
    qn = sb("qn", [128, QWW], BF16)
    qo = sb("qo", [128, 2 * QWW], BF16)
    strips = sb("strips", [128, 192], F32)
    onef = sb("onef", [128, 1], F32)
    onefrq = sb("onefrq", [1, 128], F32)
    gval = sb("gval", [128, 4], F32)
    chunks = []

    def setup():
        p.op("vector", lambda e: e.memset(onef[:, :], 1.0))
        p.op("vector", lambda e: e.memset(onefrq[:, :], 1.0), inc="qrow")
    chunks.append(setup)
    base_qbc = [0]

    def tiles_of(rows, cols):
        w = QWW if cols % QWW == 0 else 1024
        return [(rc * 128, cc * w, w) for rc in range(rows // 128)
                for cc in range(cols // w)]

    for mi, (srcw, dst, rows, cols) in enumerate(specs):
        tl = tiles_of(rows, cols)
        nt = len(tl)
        st0 = [0]

        def mk_passA(i, r0, c0, w, sl, srcw=None, mi=mi, st0=st0):
            def f(srcw=srcw, i=i, r0=r0, c0=c0, w=w, sl=sl, mi=mi):
                if i == 0:
                    st0[0] = p.n("qabs")
                qeng = "scalar" if sl == 0 else "sync"
                qsem = "qlA" if sl == 0 else "qlS"
                p.op(qeng, lambda e, srcw=srcw, r0=r0, c0=c0, w=w, sl=sl: e.dma_start(
                    out=qw[:, sl * QWW:sl * QWW + w], in_=srcw[r0:r0 + 128, c0:c0 + w]),
                    waits=[("qabs", st0[0] + i - 1), ("qbc", base_qbc[0] + mi),
                           ("qdv", p.n("qdv"))],
                    inc=qsem, inc_n=16)
                p.op("scalar", lambda e, w=w, sl=sl, i=i: e.activation(
                    out=qw[:, sl * QWW:sl * QWW + w], in_=qw[:, sl * QWW:sl * QWW + w],
                    func=AF.Abs, bias=0.0, scale=1.0, accum_out=strips[:, i:i + 1]),
                    waits=[(qsem, p.n(qsem))], inc="qabs")
            return f
        for i, (r0, c0, w) in enumerate(tl):
            chunks.append(mk_passA(i, r0, c0, w, i % 2, srcw=srcw))

        def gamma(nt=nt, rows=rows, cols=cols, st0=st0):
            p.op("vector", lambda e, nt=nt: e.tensor_reduce(
                out=strips[:, 0:1], in_=strips[:, 0:nt], axis=AX.X, op=ALU.add),
                waits=[("qabs", st0[0] + nt)], inc="qrow")
            p.op("tensor", lambda e: e.matmul(bank[0:1, 0:1], strips[:, 0:1], onef[:, :],
                 start=True, stop=True), waits=[("qrow", p.n("qrow"))], inc="qmm")
            nelem = float(rows * cols)
            p.op("vector", lambda e, nelem=nelem: e.tensor_scalar(
                out=gval[0:1, 0:1], in0=bank[0:1, 0:1],
                scalar1=0.5 / nelem, scalar2=0.5e-8, op0=ALU.mult, op1=ALU.add),
                waits=[("qmm", p.n("qmm"))], inc="qrow")
            p.op("tensor", lambda e: e.matmul(bank[:, 0:1], onefrq[:, :], gval[0:1, 0:1],
                 start=True, stop=True), waits=[("qrow", p.n("qrow"))], inc="qmm")
            p.op("vector", lambda e: e.tensor_copy(out=gval[:, 0:1], in_=bank[:, 0:1]),
                 waits=[("qmm", p.n("qmm"))])
            p.op("vector", lambda e: e.tensor_scalar(
                out=gval[:, 1:2], in0=bank[:, 0:1], scalar1=-1.0, scalar2=None,
                op0=ALU.mult), inc="qbc")
        chunks.append(gamma)
        dv0 = [0]

        def mk_passB(i, r0, c0, w, sl, srcw=None, dst=None, mi=mi, dv0=dv0):
            def f(srcw=srcw, dst=dst, i=i, r0=r0, c0=c0, w=w, sl=sl, mi=mi):
                if i == 0:
                    dv0[0] = p.n("qdv")
                qeng = "scalar" if sl == 0 else "sync"
                qsem = "qlA" if sl == 0 else "qlS"
                p.op(qeng, lambda e, srcw=srcw, r0=r0, c0=c0, w=w, sl=sl: e.dma_start(
                    out=qw[:, sl * QWW:sl * QWW + w], in_=srcw[r0:r0 + 128, c0:c0 + w]),
                    waits=[("qbc", base_qbc[0] + mi + 1), ("qdv", dv0[0] + 3 * (i - 1))],
                    inc=qsem, inc_n=16)
                ld = p.n(qsem)
                p.op("vector", lambda e, w=w, sl=sl: e.tensor_scalar(
                    out=qp[:, 0:w], in0=qw[:, sl * QWW:sl * QWW + w],
                    scalar1=gval[:, 0:1], scalar2=None, op0=ALU.is_ge),
                    waits=[(qsem, ld)], inc="qdv")
                p.op("vector", lambda e, w=w, sl=sl: e.tensor_scalar(
                    out=qn[:, 0:w], in0=qw[:, sl * QWW:sl * QWW + w],
                    scalar1=gval[:, 1:2], scalar2=None, op0=ALU.is_le), inc="qdv")
                p.op("vector", lambda e, w=w, sl=sl: e.tensor_tensor(
                    out=qo[:, sl * QWW:sl * QWW + w], in0=qp[:, 0:w], in1=qn[:, 0:w],
                    op=ALU.subtract),
                    waits=[(f"qst{sl}", p.n(f"qst{sl}"))], inc="qdv")
                p.op("gpsimd", lambda e, dst=dst, r0=r0, c0=c0, w=w, sl=sl: e.dma_start(
                    out=dst[r0:r0 + 128, c0:c0 + w], in_=qo[:, sl * QWW:sl * QWW + w]),
                    waits=[("qdv", p.n("qdv"))], inc=f"qst{sl}", inc_n=16)
            return f
        for i, (r0, c0, w) in enumerate(tl):
            chunks.append(mk_passB(i, r0, c0, w, i % 2, srcw=srcw, dst=dst))
    return chunks



def emit_ffn_body(p, layer, xT, yT, W, pb, S=None):
    es = ExitStack()

    def sb(nm, shape, dt):
        return es.enter_context(p.nc.sbuf_tensor(p.uname(nm), list(shape), dt))
    xf = sb("xf", [128, 16 * 512], F32)
    xs = sb("xs", [128, 2 * 512], BF16)
    xtmp = sb("xtmp", [128, 512], F32)
    xq = xtmp[:, :].bitcast(BF16)
    xl = sb("xl", [128, DC * 1024], BF16)
    hh = sb("hh", [128, NJC * 1024], BF16)
    wgb = sb("wgb", [128, 2 * DC * 256], BF16)
    wub = sb("wub", [128, 2 * DC * 256], BF16)
    wdb = sb("wdb", [128, 2 * 4 * 512], BF16)
    wdbf = wdb[:, :].bitcast(F32)
    gsb = sb("gsb", [128, 2 * 512], BF16)
    usb = sb("usb", [128, 2 * 512], BF16)
    xsf = xs[:, :].bitcast(F32)
    yv = xf[:, 8 * 512:10 * 512]
    rows = sb("rows", [1, 2 * 512], F32)
    brow = sb("brow", [1, 2 * 512], BF16)
    onec = sb("onec", [128, 1], BF16)
    oner = sb("oner", [1, 128], BF16)

    if layer == 0:
        wsrc_g, wsrc_u, wsrc_d = W["gwT"], W["uwT"], W["dwT"]
        wq_eng = "gpsimd"
        sgs, sus, sds = S["sgs"], S["sus"], S["sds"]
        s_ready = S["ready"]
    else:
        wsrc_g, wsrc_u, wsrc_d = W["w1q"], W["w2q"], W["w3q"]
        wq_eng = "sync"
        s_ready = []

    p.op("vector", lambda e: e.memset(onec[:, :], 1.0))
    p.op("vector", lambda e: e.memset(oner[:, :], 1.0), inc="init")

    b_dn = p.n("rel_dn")
    b_gmm = p.n("gu_mm")
    b_dmm = p.n("dn_mm")
    b_hok = p.n("h_ok")
    ln_ts = [0, 0]

    for tb in range(2):
        t0 = tb * 1024
        for tsub in range(2):
            ts4 = tb * 2 + tsub
            tc0 = ts4 * 512
            base_mm = p.n("ln_mm")
            base_row = p.n("ln_row")
            base_act = p.n("ln_act")
            base_bc = p.n("ln_bc")
            p.op("sync", lambda e, tc0=tc0: e.dma_start(
                out=_r(xf[:, :], "p (c t) -> p c t", t=512),
                in_=_r(xT[:, tc0:tc0 + 512], "(c p) t -> p c t", p=128)),
                waits=[("ln_ap", p.n("ln_ap")), ("rel_dn", b_dn + 32 * tb),
                       ("sst0", p.n("sst0")), ("sst1", p.n("sst1"))],
                inc="xload", inc_n=16)
            xld = p.n("xload")
            for dc in range(DC):
                sl = dc % 2
                p.op("vector", lambda e, dc=dc, sl=sl: e.tensor_copy(
                    out=xq[:, sl * 512:(sl + 1) * 512], in_=xf[:, dc * 512:(dc + 1) * 512]),
                    waits=[("xload", xld), ("ln_mm", base_mm + dc - 1)], inc="ln_xq")
                p.op("scalar", lambda e, dc=dc, sl=sl: e.activation(
                    out=xs[:, sl * 512:(sl + 1) * 512], in_=xf[:, dc * 512:(dc + 1) * 512],
                    func=AF.Square, bias=0.0, scale=1.0),
                    waits=[("xload", xld), ("ln_mm", base_mm + dc - 1)], inc="ln_xs")
                w0 = [("ln_xq", p.n("ln_xq")), ("ln_xs", p.n("ln_xs"))]
                if dc == 0:
                    w0 += [("ln_row", base_row), ("rel_dn", b_dn + 32 * tb),
                           ("init", p.n("init"))]
                p.op("tensor", lambda e, sl=sl, dc=dc: e.matmul(
                    pb[0][0:1, :], onec[:, :], xq[:, sl * 512:(sl + 1) * 512],
                    start=(dc == 0), stop=(dc == DC - 1)), waits=w0)
                p.op("tensor", lambda e, sl=sl, dc=dc: e.matmul(
                    pb[1][0:1, :], onec[:, :], xs[:, sl * 512:(sl + 1) * 512],
                    start=(dc == 0), stop=(dc == DC - 1)), inc="ln_mm")
            m_, msq = rows[0:1, 0:512], rows[0:1, 512:1024]
            var, lnv = rows[0:1, 0:512], rows[0:1, 512:1024]
            bm, br = brow[0:1, 0:512], brow[0:1, 512:1024]
            p.op("vector", lambda e: e.tensor_scalar(out=m_, in0=pb[0][0:1, :],
                 scalar1=INVD, scalar2=None, op0=ALU.mult),
                 waits=[("ln_mm", base_mm + DC), ("ln_act", p.n("ln_act"))])
            p.op("vector", lambda e: e.tensor_copy(out=bm, in_=m_), inc="ln_row")
            p.op("vector", lambda e: e.tensor_tensor(out=msq, in0=m_, in1=m_, op=ALU.mult))
            p.op("vector", lambda e: e.scalar_tensor_tensor(out=var, in0=pb[1][0:1, :],
                 scalar=INVD, in1=msq, op0=ALU.mult, op1=ALU.subtract), inc="ln_row")
            p.op("scalar", lambda e: e.activation(out=lnv, in_=var, func=AF.Ln,
                 bias=EPS_LN, scale=1.0),
                 waits=[("ln_row", base_row + 2), ("init", 1)])
            p.op("scalar", lambda e: e.activation(out=br, in_=lnv, func=AF.Exp,
                 bias=0.0, scale=-0.5), inc="ln_act")
            p.op("tensor", lambda e: e.matmul(pb[2][:, :], oner[:, :], bm, start=True, stop=True),
                 waits=[("ln_row", base_row + 2), ("ln_ap", p.n("ln_ap"))], inc="ln_bc")
            p.op("tensor", lambda e: e.matmul(pb[3][:, :], oner[:, :], br, start=True, stop=True),
                 waits=[("ln_act", base_act + 1)], inc="ln_bc")
            mb_sb = xsf[:, 0:512]
            rb_sb = wdbf[:, 0:512]
            pslot = [xtmp[:, :], wdbf[:, 512:1024]]
            p.op("vector", lambda e: e.tensor_copy(out=mb_sb, in_=pb[2][:, :]),
                 waits=[("ln_bc", base_bc + 2)])
            p.op("vector", lambda e: e.tensor_copy(out=rb_sb, in_=pb[3][:, :]),
                 inc="ln_ev")
            base_pl = p.n("ln_pl")
            evv = p.n("ln_ev")
            for dc in range(DC):
                p.op("gpsimd", lambda e, dc=dc, sl=pslot[dc % 2]: e.tensor_tensor(
                    out=sl, in0=xf[:, dc * 512:(dc + 1) * 512], in1=mb_sb,
                    op=ALU.subtract),
                    waits=[("ln_ev", evv), ("ln_ap", p.n("ln_ap") - 1),
                           ("gu_mm", b_gmm + 44 * tb)], inc="ln_pl")
                p.op("vector", lambda e, dc=dc, sl=pslot[dc % 2], off=dc * 1024 + tsub * 512: e.tensor_tensor(
                    out=xl[:, off:off + 512], in0=sl, in1=rb_sb,
                    op=ALU.mult), waits=[("ln_pl", base_pl + dc + 1)], inc="ln_ap")
            ln_ts[ts4 % 2] = p.n("ln_ap")
            p.wait("tensor", [("ln_ap", p.n("ln_ap"))])

        base_gu = p.n("gu_mm")
        base_gug = p.n("gu_g")
        base_rg = p.n("rel_g")
        base_ru = p.n("rel_u")
        base_wf = p.n("wfix")
        base_h = p.n("h_ok")
        ln_done = p.n("ln_ap")
        for jp in range(NJC // 2):
          buf = jp % 2
          wsem = f"wld{buf}"
          woff = buf * DC * 256
          c0 = jp * 256
          p.op(wq_eng, lambda e, c0=c0, woff=woff: e.dma_start(
              out=_r(wgb[:, woff:woff + DC * 256], "p (c j) -> p c j", j=256),
              in_=_r(wsrc_g[:, c0:c0 + 256], "(c p) j -> p c j", p=128)),
              waits=[("gu_mm", max(base_gu + 2 * jp - 2, base_gu))], inc=wsem, inc_n=16)
          p.op(wq_eng, lambda e, c0=c0, woff=woff: e.dma_start(
              out=_r(wub[:, woff:woff + DC * 256], "p (c j) -> p c j", j=256),
              in_=_r(wsrc_u[:, c0:c0 + 256], "(c p) j -> p c j", p=128)),
              inc=wsem, inc_n=16)
          wld = p.n(wsem)
          if layer == 0:
              p.op("vector", lambda e, woff=woff: e.tensor_scalar(
                  out=wgb[:, woff:woff + DC * 256], in0=wgb[:, woff:woff + DC * 256],
                  scalar1=-128.0, scalar2=None, op0=ALU.add),
                  waits=[(wsem, wld)], inc="wfix")
              p.op("vector", lambda e, woff=woff: e.tensor_scalar(
                  out=wub[:, woff:woff + DC * 256], in0=wub[:, woff:woff + DC * 256],
                  scalar1=-128.0, scalar2=None, op0=ALU.add), inc="wfix")
          for jcl in range(2):
            jc = 2 * jp + jcl
            jb = jc % 2
            gb = [pb[0 + 4 * jb], pb[1 + 4 * jb]]
            ub = [pb[2 + 4 * jb], pb[3 + 4 * jb]]
            w0 = [("rel_g", base_rg + 2 * (jc - 1)),
                  ("rel_u", base_ru + 2 * (jc - 1))]
            if layer == 0:
                w0.append(("wfix", base_wf + 2 * (jp + 1)))
            else:
                w0.append((wsem, wld))
            if jc == 0:
                w0.append(("rel_dn", b_dn + 32 * tb))
            for which, banks, wbuf in (("g", gb, wgb), ("u", ub, wub)):
                for tsub in range(2):
                    w0t = w0 + [("ln_ap", ln_ts[tsub])]
                    for dc in range(DC):
                        p.op("tensor", lambda e, banks=banks, tsub=tsub, dc=dc, wbuf=wbuf, woff=woff, jcl=jcl: e.matmul(
                            banks[tsub][:, :],
                            wbuf[:, woff + dc * 256 + jcl * 128:woff + dc * 256 + jcl * 128 + 128],
                            xl[:, dc * 1024 + tsub * 512:dc * 1024 + tsub * 512 + 512],
                            start=(dc == 0), stop=(dc == DC - 1)),
                            waits=(w0t if dc == 0 else ()),
                            inc=("gu_g" if (which == "g" and dc == DC - 1) else
                                 ("gu_mm" if (which == "u" and tsub == 1 and dc == DC - 1) else None)))
            for tsub in range(2):
                gslot = gsb[:, tsub * 512:(tsub + 1) * 512]
                uslot = usb[:, tsub * 512:(tsub + 1) * 512]
                if layer == 0:
                    p.op("scalar", lambda e, gslot=gslot, b=gb[tsub], jc=jc: e.activation(
                        out=gslot, in_=b[:, :], func=AF.Silu, bias=0.0,
                        scale=sgs[:, jc:jc + 1]),
                        waits=[("gu_g", base_gug + 2 * jc + tsub + 1),
                               ("h_ok", base_h + 2 * jc + tsub - 1)] + s_ready,
                        inc="rel_g")
                    p.op("vector", lambda e, uslot=uslot, b=ub[tsub], jc=jc: e.tensor_scalar(
                        out=uslot, in0=b[:, :], scalar1=sus[:, jc:jc + 1], scalar2=None,
                        op0=ALU.mult), waits=[("gu_mm", base_gu + jc + 1)], inc="rel_u")
                else:
                    p.op("scalar", lambda e, gslot=gslot, b=gb[tsub]: e.activation(
                        out=gslot, in_=b[:, :], func=AF.Sigmoid, bias=0.0, scale=1.0),
                        waits=[("gu_g", base_gug + 2 * jc + tsub + 1),
                               ("h_ok", base_h + 2 * jc + tsub - 1)], inc="rel_g")
                    p.op("vector", lambda e, uslot=uslot, b=ub[tsub]: e.tensor_copy(
                        out=uslot, in_=b[:, :]),
                        waits=[("gu_mm", base_gu + jc + 1)], inc="rel_u")
                p.op("vector", lambda e, gslot=gslot, uslot=uslot, off=jc * 1024 + tsub * 512: e.tensor_tensor(
                    out=hh[:, off:off + 512], in0=gslot, in1=uslot, op=ALU.mult),
                    waits=[("rel_g", base_rg + 2 * jc + tsub + 1),
                           ("dn_mm", b_dmm + 176 * tb)], inc="h_ok")

        for ocg in range(4):
            base_dn = p.n("dn_mm")
            for i in range(8):
                oc = ocg * 4 + i // 2
                tss = i % 2
                p.op("sync", lambda e, oc=oc, tss=tss, i=i, t0=t0: e.dma_start(
                    out=xf[:, i * 512:(i + 1) * 512],
                    in_=xT[oc * 128:(oc + 1) * 128, t0 + tss * 512:t0 + tss * 512 + 512]),
                    waits=[("ln_ap", ln_done), ("rel_dn", b_dn + 8 * (4 * tb + ocg))],
                    inc="xload", inc_n=16)
            xld = p.n("xload")
            for jg in range(NJC // 4):
              buf = jg % 2
              wsem = f"wdl{buf}"
              c0 = ocg * 512
              p.op(wq_eng, lambda e, jg=jg, c0=c0, buf=buf: e.dma_start(
                  out=_r(wdb[:, buf * 2048:(buf + 1) * 2048], "p (j c) -> p j c", c=512),
                  in_=_r(wsrc_d[jg * 512:(jg + 1) * 512, c0:c0 + 512], "(j p) c -> p j c", p=128)),
                  waits=[("dn_mm", max(base_dn + 4 * jg - 4, base_dn)),
                         ("ln_ap", ln_done)], inc=wsem, inc_n=16)
              wld = p.n(wsem)
              if layer == 0:
                  p.op("vector", lambda e, buf=buf: e.tensor_scalar(
                      out=wdb[:, buf * 2048:(buf + 1) * 2048],
                      in0=wdb[:, buf * 2048:(buf + 1) * 2048],
                      scalar1=-128.0, scalar2=None, op0=ALU.add),
                      waits=[(wsem, wld)], inc="wdfix")
                  wfx = p.n("wdfix")
              for jcl in range(4):
                jc = 4 * jg + jcl
                if layer == 0:
                    w0 = [("wdfix", wfx)]
                else:
                    w0 = [(wsem, wld)]
                if jc == 0:
                    w0 = w0 + [("rel_dn", b_dn + 8 * (4 * tb + ocg)),
                               ("h_ok", b_hok + 88 * (tb + 1))]
                for oc4 in range(4):
                    for tss in range(2):
                        p.op("tensor", lambda e, oc4=oc4, tss=tss, jc=jc, buf=buf, jcl=jcl: e.matmul(
                            pb[oc4 * 2 + tss][:, :],
                            wdb[:, buf * 2048 + jcl * 512 + oc4 * 128:buf * 2048 + jcl * 512 + oc4 * 128 + 128],
                            hh[:, jc * 1024 + tss * 512:jc * 1024 + tss * 512 + 512],
                            start=(jc == 0), stop=(jc == NJC - 1)),
                            waits=(w0 if (oc4 == 0 and tss == 0) else ()),
                            inc=("dn_mm" if (oc4 == 3 and tss == 1) else None))
            mm_dn = p.n("dn_mm")
            for i in range(8):
                oc = ocg * 4 + i // 2
                tss = i % 2
                slot = i % 2
                sa = (lambda oc=oc: sds[:, oc:oc + 1])() if layer == 0 else 0.5
                ssem = f"sst{slot}"
                p.op("vector", lambda e, i=i, slot=slot, sa=sa: e.scalar_tensor_tensor(
                    out=yv[:, slot * 512:(slot + 1) * 512], in0=pb[i][:, :],
                    scalar=sa, in1=xf[:, i * 512:(i + 1) * 512],
                    op0=ALU.mult, op1=ALU.add),
                    waits=[("dn_mm", mm_dn), ("xload", xld), (ssem, p.n(ssem))],
                    inc="rel_dn")
                p.op("scalar", lambda e, oc=oc, tss=tss, slot=slot, t0=t0: e.dma_start(
                    out=yT[oc * 128:(oc + 1) * 128, t0 + tss * 512:t0 + tss * 512 + 512],
                    in_=yv[:, slot * 512:(slot + 1) * 512]),
                    waits=[("rel_dn", p.n("rel_dn"))], inc=ssem, inc_n=16)
    es.close()



def emit_attn_body(p, layer, xT_own, yT, W, pbq, fused, cc_tag="", quant_specs=None):
    pb, pbquad = pbq
    nh = NH
    hk = 2048 if layer == 0 else 512
    hkc = hk // 128
    rope = (layer == 0)
    alibi = (layer == 1)
    groups = [[h] for h in range(nh)] if layer == 0 else \
             [[g, g + 4, g + 8, g + 12] for g in range(4)]

    wq, wo = W["wq"], W["wo"]
    maskpat = W["maskpat"]
    ksrc = W["ksrc"]
    vsrc = W["vsrc"]

    if fused:
        kvKin = p.dram(f"kvKin{cc_tag}", [hk, T], BF16)
        kvVin = p.dram(f"kvVin{cc_tag}", [16, hkc, 128, 128], BF16)
        kvKout = p.dram(f"kvKout{cc_tag}", [2, hk, T], BF16)
        kvVout = p.dram(f"kvVout{cc_tag}", [2, 16, hkc, 128, 128], BF16)
    else:
        kvKout = p.dram(f"kvK{cc_tag}", [2, hk, T], BF16)
        kvVout = p.dram(f"kvV{cc_tag}", [2, 16, hkc, 128, 128], BF16)
    ctxd = p.dram(f"ctxd{cc_tag}", [D, T], BF16)

    esQ = ExitStack()
    QT = esQ.enter_context(p.nc.sbuf_tensor(p.uname("QT"), [128, nh * T], BF16))

    esA = ExitStack()

    def sbA(nm, shape, dt):
        return esA.enter_context(p.nc.sbuf_tensor(p.uname(nm), list(shape), dt))
    xf = sbA("xf", [128, 16 * 512], F32)
    xs = sbA("xs", [128, 2 * 512], BF16)
    xtmp = sbA("xtmp", [128, 512], F32)
    xq = xtmp[:, :].bitcast(BF16)
    xl = sbA("xl", [128, DC * T], BF16)
    wkb = sbA("wkb", [128, 2 * DC * 128], BF16)
    wvb = sbA("wvb", [128, DC * 256], BF16)
    ksg = sbA("ksg", [128, 2 * 512], BF16)
    vsg = sbA("vsg", [128, 2 * 512], BF16)
    rows = sbA("rows", [1, 2 * 512], F32)
    brow = sbA("brow", [1, 2 * 512], BF16)
    onec = sbA("onecA", [128, 1], BF16)
    oner = sbA("onerA", [1, 128], BF16)
    rta = sbA("rta", [64, 512], F32)
    rtb = sbA("rtb", [64, 512], F32)
    if rope:
        csb = sbA("csb", [64, 2 * T], BF16)

    p.op("vector", lambda e: e.memset(onec[:, :], 1.0))
    p.op("vector", lambda e: e.memset(oner[:, :], 1.0), inc="init")

    def ln_tsub(src, ts4):
        tc0 = ts4 * 512
        base_mm = p.n("ln_mm")
        base_row = p.n("ln_row")
        base_act = p.n("ln_act")
        base_bc = p.n("ln_bc")
        p.op("sync", lambda e, src=src, tc0=tc0: e.dma_start(
            out=_r(xf[:, :], "p (c t) -> p c t", t=512),
            in_=_r(src[:, tc0:tc0 + 512], "(c p) t -> p c t", p=128)),
            waits=[("ln_ap", p.n("ln_ap"))], inc="xload", inc_n=16)
        xld = p.n("xload")
        for dc in range(DC):
            sl = dc % 2
            p.op("vector", lambda e, dc=dc, sl=sl: e.tensor_copy(
                out=xq[:, sl * 512:(sl + 1) * 512], in_=xf[:, dc * 512:(dc + 1) * 512]),
                waits=[("xload", xld), ("ln_mm", base_mm + dc - 1)], inc="ln_xq")
            p.op("scalar", lambda e, dc=dc, sl=sl: e.activation(
                out=xs[:, sl * 512:(sl + 1) * 512], in_=xf[:, dc * 512:(dc + 1) * 512],
                func=AF.Square, bias=0.0, scale=1.0),
                waits=[("xload", xld), ("ln_mm", base_mm + dc - 1)], inc="ln_xs")
            w0 = [("ln_xq", p.n("ln_xq")), ("ln_xs", p.n("ln_xs"))]
            if dc == 0:
                w0 += [("ln_row", base_row), ("init", p.n("init"))]
            p.op("tensor", lambda e, sl=sl, dc=dc: e.matmul(
                pb[0][0:1, :], onec[:, :], xq[:, sl * 512:(sl + 1) * 512],
                start=(dc == 0), stop=(dc == DC - 1)), waits=w0)
            p.op("tensor", lambda e, sl=sl, dc=dc: e.matmul(
                pb[1][0:1, :], onec[:, :], xs[:, sl * 512:(sl + 1) * 512],
                start=(dc == 0), stop=(dc == DC - 1)), inc="ln_mm")
        m_, msq = rows[0:1, 0:512], rows[0:1, 512:1024]
        var, lnv = rows[0:1, 0:512], rows[0:1, 512:1024]
        bm, br = brow[0:1, 0:512], brow[0:1, 512:1024]
        p.op("vector", lambda e: e.tensor_scalar(out=m_, in0=pb[0][0:1, :],
             scalar1=INVD, scalar2=None, op0=ALU.mult),
             waits=[("ln_mm", base_mm + DC), ("ln_act", p.n("ln_act"))])
        p.op("vector", lambda e: e.tensor_copy(out=bm, in_=m_), inc="ln_row")
        p.op("vector", lambda e: e.tensor_tensor(out=msq, in0=m_, in1=m_, op=ALU.mult))
        p.op("vector", lambda e: e.scalar_tensor_tensor(out=var, in0=pb[1][0:1, :],
             scalar=INVD, in1=msq, op0=ALU.mult, op1=ALU.subtract), inc="ln_row")
        p.op("scalar", lambda e: e.activation(out=lnv, in_=var, func=AF.Ln,
             bias=EPS_LN, scale=1.0),
             waits=[("ln_row", base_row + 2), ("init", 1)])
        p.op("scalar", lambda e: e.activation(out=br, in_=lnv, func=AF.Exp,
             bias=0.0, scale=-0.5), inc="ln_act")
        p.op("tensor", lambda e: e.matmul(pb[2][:, :], oner[:, :], bm, start=True, stop=True),
             waits=[("ln_row", base_row + 2), ("ln_ap", p.n("ln_ap"))], inc="ln_bc")
        p.op("tensor", lambda e: e.matmul(pb[3][:, :], oner[:, :], br, start=True, stop=True),
             waits=[("ln_act", base_act + 1)], inc="ln_bc")
        for dc in range(DC):
            p.op("vector", lambda e, dc=dc: e.tensor_tensor(
                out=xtmp[:, :], in0=xf[:, dc * 512:(dc + 1) * 512], in1=pb[2][:, :],
                op=ALU.subtract),
                waits=[("ln_bc", base_bc + 2), ("ap_mm", p.n("ap_mm"))])
            p.op("vector", lambda e, dc=dc, off=dc * T + ts4 * 512: e.tensor_tensor(
                out=xl[:, off:off + 512], in0=xtmp[:, :], in1=pb[3][:, :],
                op=ALU.mult), inc="ln_ap")
        p.wait("tensor", [("ln_ap", p.n("ln_ap"))])

    def rope_evac(bank, csoff, dst_ap_fn, extra_waits, sem_inc):
        C = (lambda: csb[0:64, csoff:csoff + 512])()
        S_ = (lambda: csb[0:64, T + csoff:T + csoff + 512])()
        p.op("vector", lambda e, bank=bank, C=C: e.tensor_tensor(
            out=rta[:, :], in0=bank[0:64, :], in1=C, op=ALU.mult), waits=extra_waits)
        p.op("vector", lambda e, bank=bank, S_=S_: e.tensor_tensor(
            out=rtb[:, :], in0=bank[64:128, :], in1=S_, op=ALU.mult))
        p.op("vector", lambda e: e.tensor_tensor(
            out=dst_ap_fn(0), in0=rta[:, :], in1=rtb[:, :], op=ALU.subtract))
        p.op("vector", lambda e, bank=bank, C=C: e.tensor_tensor(
            out=rta[:, :], in0=bank[64:128, :], in1=C, op=ALU.mult))
        p.op("vector", lambda e, bank=bank, S_=S_: e.tensor_tensor(
            out=rtb[:, :], in0=bank[0:64, :], in1=S_, op=ALU.mult))
        p.op("vector", lambda e: e.tensor_tensor(
            out=dst_ap_fn(1), in0=rta[:, :], in1=rtb[:, :], op=ALU.add), inc=sem_inc)

    def kv_proj(kdst, vdst):
        ln_done = p.n("ln_ap")
        mm_hist = []
        for kc in range(hkc):
            kpar2 = kc % 2
            woffk = kpar2 * DC * 128
            p.op("gpsimd", lambda e, kc=kc, woffk=woffk: e.dma_start(
                out=_r(wkb[:, woffk:woffk + DC * 128], "p (c j) -> p c j", j=128),
                in_=_r(ksrc(kc), "(c p) j -> p c j", p=128)),
                waits=[("ap_mm", mm_hist[kc - 2] if kc >= 2 else 0)],
                inc=f"wpld{kpar2}", inc_n=16)
            wld = p.n(f"wpld{kpar2}")
            for ts4 in range(NS):
                ei = kc * NS + ts4
                bank = pb[4 + ei % 2]
                for dc in range(DC):
                    w0 = []
                    if dc == 0:
                        w0 = [(f"wpld{kpar2}", wld), ("ln_ap", ln_done),
                              ("ap_ev", p.n("ap_ev") - 1)]
                    p.op("tensor", lambda e, bank=bank, dc=dc, ts4=ts4, woffk=woffk: e.matmul(
                        bank[:, :], wkb[:, woffk + dc * 128:woffk + dc * 128 + 128],
                        xl[:, dc * T + ts4 * 512:dc * T + ts4 * 512 + 512],
                        start=(dc == 0), stop=(dc == DC - 1)),
                        waits=w0, inc=("ap_mm" if dc == DC - 1 else None))
                kpar = ei % 2
                ksl = (lambda kpar=kpar: ksg[:, kpar * 512:(kpar + 1) * 512])()
                ew = [("ap_mm", p.n("ap_mm")), (f"kst{kpar}", p.n(f"kst{kpar}"))]
                if rope:
                    ew.append(("csld", p.n("csld")))
                    rope_evac(bank, ts4 * 512,
                              lambda half, ksl=ksl: ksl[half * 64:(half + 1) * 64, :],
                              ew, "ap_ev")
                else:
                    p.op("vector", lambda e, bank=bank, ksl=ksl: e.tensor_copy(
                        out=ksl, in_=bank[:, :]), waits=ew, inc="ap_ev")
                p.op("scalar", lambda e, kc=kc, ts4=ts4, ksl=ksl, kdst=kdst, kpar=kpar: e.dma_start(
                    out=kdst[kc * 128:(kc + 1) * 128, ts4 * 512:ts4 * 512 + 512],
                    in_=ksl),
                    waits=[("ap_ev", p.n("ap_ev"))], inc=f"kst{kpar}", inc_n=16)
            mm_hist.append(p.n("ap_mm"))
        for vf in range(hk // 256):
            p.op("gpsimd", lambda e, vf=vf: e.dma_start(
                out=_r(wvb[:, :], "p (c j) -> p c j", j=256),
                in_=_r(vsrc(vf), "(c p) j -> p c j", p=128)),
                waits=[("ap_mm", p.n("ap_mm"))], inc="wpld", inc_n=16)
            wld = p.n("wpld")
            for tc in range(16):
                bank = pb[6 + tc % 2]
                for dc in range(DC):
                    w0 = []
                    if dc == 0:
                        w0 = [("wpld", wld), ("ln_ap", ln_done),
                              ("ap_ev2", p.n("ap_ev2") - 1)]
                    p.op("tensor", lambda e, bank=bank, dc=dc, tc=tc: e.matmul(
                        bank[:, 0:256], xl[:, dc * T + tc * 128:dc * T + tc * 128 + 128],
                        wvb[:, dc * 256:(dc + 1) * 256],
                        start=(dc == 0), stop=(dc == DC - 1)),
                        waits=w0, inc=("ap_mm" if dc == DC - 1 else None))
                vpar = tc % 2
                vsl = (lambda vpar=vpar: vsg[:, vpar * 512:vpar * 512 + 256])()
                p.op("vector", lambda e, bank=bank, vsl=vsl: e.tensor_copy(
                    out=vsl, in_=bank[:, 0:256]),
                    waits=[("ap_mm", p.n("ap_mm")), (f"vst{vpar}", p.n(f"vst{vpar}"))],
                    inc="ap_ev2")
                p.op("scalar", lambda e, tc=tc, vf=vf, vsl=vsl, vdst=vdst, vpar=vpar: e.dma_start(
                    out=_r(vdst[tc, vf * 2:(vf + 1) * 2, :, :], "v k h -> k v h"),
                    in_=_r(vsl, "k (v h) -> k v h", h=128)),
                    waits=[("ap_ev2", p.n("ap_ev2"))], inc=f"vst{vpar}", inc_n=16)

    def q_proj():
        ln_done = p.n("ln_ap")
        mm_hist = []
        kv_end = p.n("ap_mm")
        for h in range(nh):
            kpar2 = h % 2
            woffk = kpar2 * DC * 128
            p.op("gpsimd", lambda e, h=h, woffk=woffk: e.dma_start(
                out=_r(wkb[:, woffk:woffk + DC * 128], "p (c j) -> p c j", j=128),
                in_=_r(wq[:, h * 128:(h + 1) * 128], "(c p) j -> p c j", p=128)),
                waits=[("ap_mm", mm_hist[h - 2] if h >= 2 else kv_end)],
                inc=f"wpld{kpar2}", inc_n=16)
            wld = p.n(f"wpld{kpar2}")
            for ts4 in range(NS):
                bank = pb[4 + ts4 % 2]
                for dc in range(DC):
                    w0 = []
                    if dc == 0:
                        w0 = [(f"wpld{kpar2}", wld), ("ln_ap", ln_done),
                              ("ap_ev", p.n("ap_ev") - 1)]
                    p.op("tensor", lambda e, bank=bank, dc=dc, ts4=ts4, woffk=woffk: e.matmul(
                        bank[:, :], wkb[:, woffk + dc * 128:woffk + dc * 128 + 128],
                        xl[:, dc * T + ts4 * 512:dc * T + ts4 * 512 + 512],
                        start=(dc == 0), stop=(dc == DC - 1)),
                        waits=w0, inc=("ap_mm" if dc == DC - 1 else None))
                qoff = h * T + ts4 * 512
                ew = [("ap_mm", p.n("ap_mm"))]
                if rope:
                    ew.append(("csld", p.n("csld")))
                    rope_evac(bank, ts4 * 512,
                              lambda half, qoff=qoff: QT[half * 64:(half + 1) * 64,
                                                         qoff:qoff + 512],
                              ew, "ap_ev")
                else:
                    p.op("vector", lambda e, bank=bank, qoff=qoff: e.tensor_copy(
                        out=QT[:, qoff:qoff + 512], in_=bank[:, :]),
                        waits=ew, inc="ap_ev")
            mm_hist.append(p.n("ap_mm"))

    def load_cs(r):
        if not rope:
            return
        cosd, sind = W["cs"][r]
        p.op("gpsimd", lambda e, cosd=cosd: e.dma_start(out=csb[:, 0:T], in_=cosd[:, :]),
             waits=[("ap_ev", p.n("ap_ev")), ("ap_ev2", p.n("ap_ev2"))],
             inc="csld", inc_n=16)
        p.op("gpsimd", lambda e, sind=sind: e.dma_start(out=csb[:, T:2 * T], in_=sind[:, :]),
             inc="csld", inc_n=16)

    if fused:
        load_cs(2)
        for ts4 in range(NS):
            ln_tsub(xT_own, ts4)
        kv_proj(kvKin, kvVin)
        cw = [(f"kst{i}", p.n(f"kst{i}")) for i in range(2)] + \
             [(f"vst{i}", p.n(f"vst{i}")) for i in range(2)]
        p.op("gpsimd", lambda e: e.collective_compute(
            "AllGather", ALU.bypass,
            replica_groups=[[0, 1], [2, 3], [4, 5], [6, 7]],
            ins=[kvKin.ap().opt()],
            outs=[kvKout.ap().rearrange("r a b -> (r a) b").opt()]),
            waits=cw, inc="ccg")
        p.op("gpsimd", lambda e: e.collective_compute(
            "AllGather", ALU.bypass,
            replica_groups=[[0, 1], [2, 3], [4, 5], [6, 7]],
            ins=[kvVin.ap().rearrange("a b c d -> (a b c) d").opt()],
            outs=[kvVout.ap().rearrange("r a b c d -> (r a b c) d").opt()]),
            inc="ccg")
        ccv = p.n("ccg")
        q_proj()
    else:
        srcs = [W["xT_p0"], W["xT_p1"], xT_own]
        for r in range(2):
            load_cs(r)
            for ts4 in range(NS):
                ln_tsub(srcs[r], ts4)
            kv_proj(kvKout[r], kvVout[r])
        load_cs(2)
        for ts4 in range(NS):
            ln_tsub(srcs[2], ts4)
        q_proj()
        ccv = 0

    p.fence_all()
    esA.close()

    esB = ExitStack()

    def sbB(nm, shape, dt):
        return esB.enter_context(p.nc.sbuf_tensor(p.uname(nm), list(shape), dt))
    pat = sbB("pat", [128, 2 * MPS * QB], BF16)
    ksb = sbB("ksb", [128, 2 * 2 * T], BF16)
    vsb = sbB("vsb", [128, 2 * 2 * 32 * 128], BF16)
    esb = sbB("esb", [128, 2 * 2 * QB], BF16)
    onecB = sbB("onecB", [128, 1], BF16)
    onefr = sbB("onefr", [2, 128], BF16)
    rrow = sbB("rrow", [1, 512], F32)
    rh_ = sbB("rh", [1, 512], BF16)
    rl_ = sbB("rl", [1, 512], BF16)
    lnrow = sbB("lnrow", [1, 512], F32)
    ctx_sb = sbB("ctx_sb", [128, 2 * 512], F32)
    den_sb = sbB("den_sb", [1, 2 * 512], F32)
    bcs = sbB("bcs", [128, 512], F32)
    csg = sbB("csg", [128, 2 * 512], BF16)
    if alibi:
        ba = sbB("ba", [6, 2 * T], BF16)
        bb = sbB("bb", [6, T], BF16)
    qchunks = []
    if quant_specs is not None:
        qchunks = emit_quant_chunks(p, sbB, pb[7], quant_specs)
    qci = [0]

    def dispense_quant(k):
        while qci[0] < min(len(qchunks), k):
            qchunks[qci[0]]()
            qci[0] += 1

    p.op("vector", lambda e: e.memset(onecB[:, :], 1.0))
    p.op("vector", lambda e: e.memset(onefr[:, :], 1.0), inc="binit")
    b_binit = p.n("binit")
    p.op("sync", lambda e: e.dma_start(
        out=_r(pat[:, :], "p (r d q) -> p r d q", d=MPS, q=QB),
        in_=_r(maskpat[:, :, :, :], "r d p q -> p r d q")),
        waits=[("ccg", ccv)], inc="pld", inc_n=16)
    b_pld = p.n("pld")

    kv_iters = [(gi, s) for gi in range(len(groups)) for s in range(NS)]
    pend_tail = [None]
    sc_after, mm_after = [], []
    kld_at, vld_at = {}, {}

    def emit_kv_load(j):
        gi, s = kv_iters[j]
        nm = MPS * (s + 1)
        par = j % 2
        kw = [("ccg", ccv)]
        if j >= 2:
            kw += [("at_sc", sc_after[j - 2]), ("at_mm", mm_after[j - 2])]
        for r01 in range(2):
            p.op("sync", lambda e, r01=r01, gi=gi, nm=nm, par=par: e.dma_start(
                out=ksb[:, (par * 2 + r01) * T:(par * 2 + r01) * T + nm * 128],
                in_=kvKout[r01, gi * 128:(gi + 1) * 128, 0:nm * 128]),
                waits=kw, inc=f"kld{par}", inc_n=16)
            p.op("sync", lambda e, r01=r01, gi=gi, nm=nm, par=par: e.dma_start(
                out=_r(vsb[:, (par * 2 + r01) * 4096:(par * 2 + r01) * 4096 + nm * 128],
                       "k (m h) -> k m h", h=128),
                in_=_r(kvVout[r01, 0:nm, gi, :, :], "m k h -> k m h")),
                waits=kw, inc=f"vld{par}", inc_n=16)
        kld_at[j] = p.n(f"kld{par}")
        vld_at[j] = p.n(f"vld{par}")

    emit_kv_load(0)
    for j, (gi, s) in enumerate(kv_iters):
        if j + 1 < len(kv_iters):
            emit_kv_load(j + 1)
        hs = groups[gi]
        nm = MPS * (s + 1)
        par = j % 2
        kldv, vldv = kld_at[j], vld_at[j]
        ksem, vsem = f"kld{par}", f"vld{par}"
        koff = par * 2 * T
        voff = par * 2 * 4096
        for h in hs:
            if alibi:
                p.op("gpsimd", lambda e, h=h: e.dma_start(
                    out=_r(ba[:, :], "b (r c) -> b r c", r=2),
                    in_=_r(W["baT"][h, :, :, :], "r b c -> b r c")),
                    waits=[("at_sc", p.n("at_sc"))], inc="bld", inc_n=16)
                p.op("gpsimd", lambda e, h=h: e.dma_start(
                    out=bb[:, :], in_=W["bbT"][h, :, :]),
                    inc="bld", inc_n=16)
            bldv = p.n("bld") if alibi else 0
            n_i = 2 * nm
            npk = n_i // 2
            tiles = [(r01, m) for r01 in range(2) for m in range(nm)]
            sc_pair, mn_pair = {}, {}

            def emit_scores_pair(pk):
                for t_i in range(2):
                    r01, m = tiles[2 * pk + t_i]
                    i_g = 2 * pk + t_i
                    si = i_g % 4
                    bank = pb[si]
                    w0 = [(ksem, kldv), ("at_ex", p.n("at_ex"))]
                    if alibi:
                        w0.append(("bld", bldv))
                    p.op("tensor", lambda e, bank=bank, r01=r01, m=m, h=h, s=s, koff=koff: e.matmul(
                        bank[:, :], ksb[:, koff + r01 * T + m * 128:koff + r01 * T + m * 128 + 128],
                        QT[:, h * T + s * 512:h * T + s * 512 + 512],
                        start=True, stop=(not alibi)),
                        waits=w0, inc=(None if alibi else "at_sc"))
                    if alibi:
                        p.op("tensor", lambda e, bank=bank, r01=r01, m=m, s=s: e.matmul(
                            bank[:, :], ba[:, r01 * T + m * 128:r01 * T + m * 128 + 128],
                            bb[:, s * 512:s * 512 + 512],
                            start=False, stop=True), inc="at_sc")
                    if m >= MPS * s:
                        d = m - MPS * s
                        p.op("vector", lambda e, bank=bank, r01=r01, d=d: e.tensor_tensor(
                            out=bank[:, :], in0=bank[:, :],
                            in1=pat[:, (r01 * MPS + d) * QB:(r01 * MPS + d + 1) * QB],
                            op=ALU.min),
                            waits=[("at_sc", p.n("at_sc")), ("pld", b_pld)], inc="at_mn")
                sc_pair[pk] = p.n("at_sc")
                mn_pair[pk] = p.n("at_mn")

            emit_scores_pair(0)
            if pend_tail[0] is not None:
                pend_tail[0]()
                pend_tail[0] = None
            for pk in range(npk):
                if pk + 1 < npk:
                    emit_scores_pair(pk + 1)
                ew = [("at_sc", sc_pair[pk]), ("at_mn", mn_pair[pk]),
                      ("at_mm", p.n("at_mm") - 2)]
                q0 = (2 * pk % 4) * 512
                eoff = (pk % 2) * 1024
                p.op("scalar", lambda e, q0=q0, eoff=eoff: e.activation(
                    out=esb[:, eoff:eoff + 1024], in_=pbquad[:, q0:q0 + 1024],
                    func=AF.Exp, bias=0.0, scale=RSQHD),
                    waits=ew, inc="at_ex")
                for t_i in range(2):
                    r01, m = tiles[2 * pk + t_i]
                    i_g = 2 * pk + t_i
                    cw2 = [("at_ex", p.n("at_ex")), (vsem, vldv)]
                    if i_g == 0:
                        cw2.append(("at_cx", p.n("at_cx")))
                    p.op("tensor", lambda e, r01=r01, m=m, st=(i_g == 0), sp=(i_g == n_i - 1),
                         eo=eoff + t_i * 512, voff=voff: e.matmul(
                        pb[4][:, :], vsb[:, voff + r01 * 4096 + m * 128:voff + r01 * 4096 + m * 128 + 128],
                        esb[:, eo:eo + 512],
                        start=st, stop=sp), waits=cw2)
                    ow = [("binit", b_binit)]
                    if i_g == 0:
                        ow.append(("at_rc", p.n("at_rc")))
                    p.op("tensor", lambda e, st=(i_g == 0), sp=(i_g == n_i - 1),
                         eo=eoff + t_i * 512: e.matmul(
                        pb[5][0:1, :], onecB[:, :], esb[:, eo:eo + 512],
                        start=st, stop=sp), waits=ow, inc="at_mm")
            hpar = p.n("at_cx") % 2
            p.op("vector", lambda e, hpar=hpar: e.tensor_copy(
                out=ctx_sb[:, hpar * 512:(hpar + 1) * 512], in_=pb[4][:, :]),
                waits=[("at_mm", p.n("at_mm")), ("at_nm", p.n("at_nm") - 1)],
                inc="at_cx")
            p.op("vector", lambda e, hpar=hpar: e.tensor_copy(
                out=den_sb[0:1, hpar * 512:(hpar + 1) * 512], in_=pb[5][0:1, :]),
                waits=[("at_rp", p.n("at_rp") - 1)], inc="at_rc")
            p.op("scalar", lambda e, hpar=hpar: e.activation(
                out=lnrow[:, :], in_=den_sb[0:1, hpar * 512:(hpar + 1) * 512],
                func=AF.Ln, bias=0.0, scale=1.0),
                waits=[("at_rc", p.n("at_rc")), ("at_bcm", p.n("at_bcm"))])
            p.op("scalar", lambda e: e.activation(
                out=rrow[:, :], in_=lnrow[:, :], func=AF.Exp, bias=0.0, scale=-1.0),
                inc="at_rp")

            def tail_post(h=h, s=s, hpar=hpar, rp=p.n("at_rp")):
                p.op("vector", lambda e: e.tensor_copy(out=rh_[:, :], in_=rrow[:, :]),
                     waits=[("at_rp", rp), ("at_bcm", p.n("at_bcm"))])
                p.op("vector", lambda e: e.tensor_tensor(out=rl_[:, :], in0=rrow[:, :],
                     in1=rh_[:, :], op=ALU.subtract), inc="at_hl")
                p.op("tensor", lambda e: e.matmul(pb[6][:, :], onefr[0:1, :], rh_[:, :],
                     start=True, stop=False),
                     waits=[("at_hl", p.n("at_hl")), ("at_bc", p.n("at_bc"))])
                p.op("tensor", lambda e: e.matmul(pb[6][:, :], onefr[0:1, :], rl_[:, :],
                     start=False, stop=True), inc="at_bcm")
                p.op("vector", lambda e: e.tensor_copy(out=bcs[:, :], in_=pb[6][:, :]),
                     waits=[("at_bcm", p.n("at_bcm"))], inc="at_bc")
                cpar = p.n("at_nm") % 2
                p.op("vector", lambda e, cpar=cpar, hpar=hpar: e.tensor_tensor(
                    out=csg[:, cpar * 512:(cpar + 1) * 512],
                    in0=ctx_sb[:, hpar * 512:(hpar + 1) * 512],
                    in1=bcs[:, :], op=ALU.mult),
                    waits=[(f"cst{cpar}", p.n(f"cst{cpar}"))], inc="at_nm")
                p.op("gpsimd", lambda e, h=h, s=s, cpar=cpar: e.dma_start(
                    out=ctxd[h * 128:(h + 1) * 128, s * 512:s * 512 + 512],
                    in_=csg[:, cpar * 512:(cpar + 1) * 512]),
                    waits=[("at_nm", p.n("at_nm"))], inc=f"cst{cpar}", inc_n=16)
            pend_tail[0] = tail_post
            if qchunks:
                hsi = p.n("at_cx")
                dispense_quant(1 + (len(qchunks) * hsi) // (len(kv_iters) * len(hs)))
        sc_after.append(p.n("at_sc"))
        mm_after.append(p.n("at_mm"))
    if pend_tail[0] is not None:
        pend_tail[0]()
        pend_tail[0] = None
    dispense_quant(len(qchunks))

    p.fence_all()
    esB.close()
    esQ.close()

    esC = ExitStack()

    def sbC(nm, shape, dt):
        return esC.enter_context(p.nc.sbuf_tensor(p.uname(nm), list(shape), dt))
    cbuf = sbC("cbuf", [128, DC * T], BF16)
    wob = sbC("wob", [128, 2 * DC * 128], BF16)
    yvC = sbC("yvC", [128, 2 * 512], F32)
    xres = sbC("xres", [128, 2 * 512], F32)
    if layer == 1:
        boS = sbC("boS", [128, DC], F32)
        p.op("sync", lambda e: e.dma_start(out=boS[:, :], in_=W["boR"][:, :]),
             inc="cld", inc_n=16)
    p.op("sync", lambda e: e.dma_start(
        out=_r(cbuf[:, :], "p (c t) -> p c t", t=T),
        in_=_r(ctxd[:, :], "(c p) t -> p c t", p=128)), inc="cld", inc_n=16)
    cldv = p.n("cld")
    ec0 = p.n("oc_ev")
    ec = 0
    for oc in range(DC):
        opar = oc % 2
        wsemo = f"wold{opar}"
        woffo = opar * DC * 128
        p.op("gpsimd", lambda e, oc=oc, woffo=woffo: e.dma_start(
            out=_r(wob[:, woffo:woffo + DC * 128], "p (c j) -> p c j", j=128),
            in_=_r(wo[:, oc * 128:(oc + 1) * 128], "(c p) j -> p c j", p=128)),
            waits=[("oc_mm", max(0, p.n("oc_mm") - 4))], inc=wsemo, inc_n=16)
        wldv = p.n(wsemo)
        for ts4 in range(NS):
            bank = pb[ec % 4]
            for hc in range(DC):
                w0 = []
                if hc == 0:
                    w0 = [(wsemo, wldv), ("cld", cldv),
                          ("oc_ev", max(0, ec0 + ec - 3))]
                p.op("tensor", lambda e, bank=bank, hc=hc, ts4=ts4, woffo=woffo: e.matmul(
                    bank[:, :], wob[:, woffo + hc * 128:woffo + hc * 128 + 128],
                    cbuf[:, hc * T + ts4 * 512:hc * T + ts4 * 512 + 512],
                    start=(hc == 0), stop=(hc == DC - 1)),
                    waits=w0, inc=("oc_mm" if hc == DC - 1 else None))
            par = ec % 2
            p.op("sync", lambda e, oc=oc, ts4=ts4, par=par: e.dma_start(
                out=xres[:, par * 512:(par + 1) * 512],
                in_=xT_own[oc * 128:(oc + 1) * 128, ts4 * 512:ts4 * 512 + 512]),
                waits=[("oc_ev", max(0, ec0 + ec - 1))], inc=f"xrl{par}", inc_n=16)
            sa = (lambda oc=oc: boS[:, oc:oc + 1])() if layer == 1 else 0.0
            p.op("vector", lambda e, bank=bank, par=par, sa=sa: e.scalar_tensor_tensor(
                out=yvC[:, par * 512:(par + 1) * 512], in0=bank[:, :], scalar=sa,
                in1=xres[:, par * 512:(par + 1) * 512], op0=ALU.add, op1=ALU.add),
                waits=[("oc_mm", p.n("oc_mm")), (f"xrl{par}", p.n(f"xrl{par}")),
                       (f"yst{par}", p.n(f"yst{par}")), ("cld", cldv)],
                inc="oc_ev")
            p.op("scalar", lambda e, oc=oc, ts4=ts4, par=par: e.dma_start(
                out=yT[oc * 128:(oc + 1) * 128, ts4 * 512:ts4 * 512 + 512],
                in_=yvC[:, par * 512:(par + 1) * 512]),
                waits=[("oc_ev", p.n("oc_ev"))], inc=f"yst{par}", inc_n=16)
            ec += 1
    esC.close()



def attn_weight_handles(p, layer, pref=""):
    W = {}
    if layer == 0:
        W["wq"] = p.dram(pref + "wq", [D, D], F32, kind="ExternalInput")
        wk = p.dram(pref + "wk", [D, D], F32, kind="ExternalInput")
        wv = p.dram(pref + "wv", [D, D], F32, kind="ExternalInput")
        W["wo"] = p.dram(pref + "wo", [D, D], F32, kind="ExternalInput")
        W["ksrc"] = lambda kc: wk[:, kc * 128:(kc + 1) * 128]
        W["vsrc"] = lambda vf: wv[:, vf * 256:(vf + 1) * 256]
        W["cs"] = [(p.dram(pref + "cosk0", [64, T], F32, kind="ExternalInput"),
                    p.dram(pref + "sink0", [64, T], F32, kind="ExternalInput")),
                   (p.dram(pref + "cosk1", [64, T], F32, kind="ExternalInput"),
                    p.dram(pref + "sink1", [64, T], F32, kind="ExternalInput")),
                   (p.dram(pref + "cosq", [64, T], F32, kind="ExternalInput"),
                    p.dram(pref + "sinq", [64, T], F32, kind="ExternalInput"))]
    else:
        W["wq"] = p.dram(pref + "wq", [D, D], F32, kind="ExternalInput")
        wkv = p.dram(pref + "wkv", [D, 1024], F32, kind="ExternalInput")
        W["wo"] = p.dram(pref + "wo", [D, D], F32, kind="ExternalInput")
        W["boR"] = p.dram(pref + "boR", [128, DC], F32, kind="ExternalInput")
        W["baT"] = p.dram(pref + "baT", [NH, 2, 6, T], BF16, kind="ExternalInput")
        W["bbT"] = p.dram(pref + "bbT", [NH, 6, T], BF16, kind="ExternalInput")
        W["ksrc"] = lambda kc: wkv[:, kc * 128:(kc + 1) * 128]
        W["vsrc"] = lambda vf: wkv[:, 512 + vf * 256:512 + (vf + 1) * 256]
    W["maskpat"] = p.dram(pref + "maskpat", [2, MPS, 128, QB], BF16, kind="ExternalInput")
    return W


def ffn_weight_handles(p, layer, pref=""):
    if layer == 0:
        return {
            "gwT": p.dram(pref + "gwT", [D, HID], I32, kind="ExternalInput"),
            "uwT": p.dram(pref + "uwT", [D, HID], I32, kind="ExternalInput"),
            "dwT": p.dram(pref + "dwT", [HID, D], I32, kind="ExternalInput"),
            "sg": p.dram(pref + "sg", [128, NJC], F32, kind="ExternalInput"),
            "su": p.dram(pref + "su", [128, NJC], F32, kind="ExternalInput"),
            "sd": p.dram(pref + "sd", [128, DC], F32, kind="ExternalInput"),
        }
    return {
        "w1T": p.dram(pref + "w1T", [D, HID], F32, kind="ExternalInput"),
        "w2T": p.dram(pref + "w2T", [D, HID], F32, kind="ExternalInput"),
        "w3T": p.dram(pref + "w3T", [HID, D], F32, kind="ExternalInput"),
        "w1q": p.dram(pref + "w1q", [D, HID], BF16),
        "w2q": p.dram(pref + "w2q", [D, HID], BF16),
        "w3q": p.dram(pref + "w3q", [HID, D], BF16),
    }


def build_ffn(layer):
    p = K(f"ffn{layer}")
    xT = p.dram("xT", [D, T], F32, kind="ExternalInput")
    yT = p.dram("yT", [D, T], F32, kind="ExternalOutput")
    W = ffn_weight_handles(p, layer)
    pb, pbquad = p.psum_pool()
    p.constf(EPS_LN)
    S = emit_scales_l0(p, W) if layer == 0 else None
    if layer == 1:
        emit_quant_body(p, pb, [(W["w1T"], W["w1q"], D, HID),
                                (W["w2T"], W["w2q"], D, HID),
                                (W["w3T"], W["w3q"], HID, D)])
        p.fence_all()
    emit_ffn_body(p, layer, xT, yT, W, pb, S)
    dma_sems = [s for s in p.ctr if s.startswith(("sst", "wld", "wdl", "xload",
                                                  "qld", "qst", "sload"))]
    p.wait("gpsimd", [(s, p.n(s)) for s in dma_sems])
    return p


def build_attn(layer):
    p = K(f"att{layer}")
    xT_own = p.dram("xT_own", [D, T], F32, kind="ExternalInput")
    yT = p.dram("yT", [D, T], F32, kind="ExternalOutput")
    W = attn_weight_handles(p, layer)
    W["xT_p0"] = p.dram("xT_p0", [D, T], F32, kind="ExternalInput")
    W["xT_p1"] = p.dram("xT_p1", [D, T], F32, kind="ExternalInput")
    pb, pbquad = p.psum_pool()
    p.constf(EPS_LN)
    emit_attn_body(p, layer, xT_own, yT, W, (pb, pbquad), fused=False)
    dma_sems = [s for s in p.ctr if s.startswith(("yst", "xrl", "wold", "cld"))]
    p.wait("gpsimd", [(s, p.n(s)) for s in dma_sems])
    return p


def build_mega():
    p = K("mega")
    xT = p.dram("xT", [D, T], F32, kind="ExternalInput")
    yT = p.dram("yT", [D, T], F32, kind="ExternalOutput")
    Wf0 = ffn_weight_handles(p, 0, "f0_")
    Wf1 = ffn_weight_handles(p, 1, "f1_")
    Wa0 = attn_weight_handles(p, 0, "a0_")
    Wa1 = attn_weight_handles(p, 1, "a1_")
    xA = p.dram("xA", [D, T], F32)
    xB = p.dram("xB", [D, T], F32)
    xC = p.dram("xC", [D, T], F32)
    xD = p.dram("xD", [D, T], F32)
    xE = p.dram("xE", [D, T], F32)
    pb, pbquad = p.psum_pool()
    p.constf(EPS_LN)
    S0 = emit_scales_l0(p, Wf0)
    emit_quant_body(p, pb, [(Wf1["w1T"], Wf1["w1q"], D, HID),
                            (Wf1["w2T"], Wf1["w2q"], D, HID),
                            (Wf1["w3T"], Wf1["w3q"], HID, D)])
    p.fence_all()
    emit_ffn_body(p, 0, xT, xA, Wf0, pb, S0)
    p.fence_all()
    emit_attn_body(p, 0, xA, xB, Wa0, (pb, pbquad), fused=True, cc_tag="a0")
    p.fence_all()
    emit_ffn_body(p, 0, xB, xC, Wf0, pb, S0)
    p.fence_all()
    emit_ffn_body(p, 1, xC, xD, Wf1, pb)
    p.fence_all()
    emit_attn_body(p, 1, xD, xE, Wa1, (pb, pbquad), fused=True, cc_tag="a1")
    p.fence_all()
    emit_ffn_body(p, 1, xE, yT, Wf1, pb)
    p.fence_all()
    return p



def shard_x(x):
    out = []
    for c in range(N_CORES):
        b, par = c // 2, c % 2
        out.append(np.ascontiguousarray(x[b, par::2, :].T))
    return out


def unshard_x(shards):
    x = np.empty((4, 4096, 2048), np.float32)
    for c in range(N_CORES):
        b, par = c // 2, c % 2
        x[b, par::2, :] = shards[c].T
    return x


def ffn_inputs(layer, inp, pref=""):
    if layer == 0:
        return {
            pref + "gwT": np.ascontiguousarray(np.asarray(inp["L0_ffn_wq_gate"]).T),
            pref + "uwT": np.ascontiguousarray(np.asarray(inp["L0_ffn_wq_up"]).T),
            pref + "dwT": np.ascontiguousarray(np.asarray(inp["L0_ffn_wq_down"]).T),
            pref + "sg": np.ascontiguousarray(np.asarray(inp["L0_ffn_s_gate"]).reshape(NJC, 128).T),
            pref + "su": np.ascontiguousarray(np.asarray(inp["L0_ffn_s_up"]).reshape(NJC, 128).T),
            pref + "sd": np.ascontiguousarray(np.asarray(inp["L0_ffn_s_down"]).reshape(DC, 128).T),
        }
    return {
        pref + "w1T": np.ascontiguousarray(np.asarray(inp["L1_ffn_w1"]).T),
        pref + "w2T": np.ascontiguousarray(np.asarray(inp["L1_ffn_w2"]).T),
        pref + "w3T": np.ascontiguousarray(np.asarray(inp["L1_ffn_w3"]).T),
    }


def _split3(a):
    a = a.astype(np.float32)
    a0 = a.astype(BF).astype(np.float32)
    a1 = (a - a0).astype(BF).astype(np.float32)
    a2 = (a - a0 - a1).astype(BF)
    return np.stack([a0.astype(BF), a1.astype(BF), a2], 0)


def _alibi_slopes(n):
    start = 2.0 ** (-2.0 ** (-(np.log2(n) - 3.0)))
    return np.array([start ** (i + 1) for i in range(n)], np.float64)


def attn_inputs(layer, inp, par, pref=""):
    out = {}
    if layer == 0:
        out[pref + "wq"] = np.ascontiguousarray(np.asarray(inp["L0_attn_Wq"]).T)
        out[pref + "wk"] = np.ascontiguousarray(np.asarray(inp["L0_attn_Wk"]).T)
        out[pref + "wv"] = np.ascontiguousarray(np.asarray(inp["L0_attn_Wv"]).T)
        out[pref + "wo"] = np.ascontiguousarray(np.asarray(inp["L0_attn_Wo"]).T)
        inv_freq = 1.0 / 10000.0 ** (np.arange(0, 128, 2, dtype=np.float32) / 128.0)
        tpos = np.arange(4096, dtype=np.float32)
        fr = np.outer(tpos, inv_freq)
        cos = np.cos(fr).astype(np.float32)
        sin = np.sin(fr).astype(np.float32)
        out[pref + "cosk0"] = np.ascontiguousarray(cos[0::2].T)
        out[pref + "sink0"] = np.ascontiguousarray(sin[0::2].T)
        out[pref + "cosk1"] = np.ascontiguousarray(cos[1::2].T)
        out[pref + "sink1"] = np.ascontiguousarray(sin[1::2].T)
        out[pref + "cosq"] = np.ascontiguousarray(cos[par::2].T)
        out[pref + "sinq"] = np.ascontiguousarray(sin[par::2].T)
    else:
        out[pref + "wq"] = np.ascontiguousarray(np.asarray(inp["L1_attn_Wq"]))
        out[pref + "wkv"] = np.ascontiguousarray(np.asarray(inp["L1_attn_Wkv"]))
        out[pref + "wo"] = np.ascontiguousarray(np.asarray(inp["L1_attn_Wo"]))
        out[pref + "boR"] = np.ascontiguousarray(
            np.asarray(inp["L1_attn_bo"]).reshape(DC, 128).T.astype(np.float32))
        sl = _alibi_slopes(NH)
        S128 = np.sqrt(128.0)
        l = np.arange(T, dtype=np.float64)
        baT = np.zeros((NH, 2, 6, T), BF)
        bbT = np.zeros((NH, 6, T), BF)
        for h in range(NH):
            for r in range(2):
                A = S128 * sl[h] * (2 * l + r)
                baT[h, r, 0:3] = _split3(A)
                baT[h, r, 3:6] = np.ones((3, T), BF)
            Bv = -S128 * sl[h] * (2 * l + par)
            bbT[h, 0:3] = np.ones((3, T), BF)
            bbT[h, 3:6] = _split3(Bv)
        out[pref + "baT"] = baT
        out[pref + "bbT"] = bbT
    kk = np.arange(128)[:, None]
    qq = np.arange(QB)[None, :]
    mp = np.zeros((2, MPS, 128, QB), BF)
    for r in range(2):
        eps = -1 if (par == 0 and r == 1) else 0
        for d in range(MPS):
            valid = kk <= (qq - 128 * d + eps)
            mp[r, d] = np.where(valid, np.float32(1e30), np.float32(-1e30)).astype(BF)
    out[pref + "maskpat"] = mp
    return out


_prog_cache = {}


def get_prog(kind, layer=0):
    key = (kind, layer)
    if key not in _prog_cache:
        if kind == "ffn":
            _prog_cache[key] = build_ffn(layer).finish()
        elif kind == "att":
            _prog_cache[key] = build_attn(layer).finish()
        elif kind == "mega":
            _prog_cache[key] = build_mega().finish()
        else:
            raise KeyError(kind)
    return _prog_cache[key]


def run_ffn(layer, x_shards, inp):
    nc = get_prog("ffn", layer)
    wmap = ffn_inputs(layer, inp)
    in_maps = [dict(wmap, xT=xs) for xs in x_shards]
    res = run_bass_kernel_spmd(nc, in_maps, core_ids=list(range(N_CORES)))
    return [np.asarray(res.results[c]["yT"]) for c in range(N_CORES)]


def run_attn(layer, x_shards, inp):
    nc = get_prog("att", layer)
    wmaps = [attn_inputs(layer, inp, 0), attn_inputs(layer, inp, 1)]
    in_maps = []
    for c in range(N_CORES):
        b, par = c // 2, c % 2
        in_maps.append(dict(wmaps[par], xT_own=x_shards[c],
                            xT_p0=x_shards[2 * b], xT_p1=x_shards[2 * b + 1]))
    res = run_bass_kernel_spmd(nc, in_maps, core_ids=list(range(N_CORES)))
    return [np.asarray(res.results[c]["yT"]) for c in range(N_CORES)]


def kernel(**inputs):
    inputs = {k: np.asarray(v) for k, v in inputs.items()}
    nc = get_prog("mega")
    base = {}
    base.update(ffn_inputs(0, inputs, "f0_"))
    base.update(ffn_inputs(1, inputs, "f1_"))
    wa = [dict(attn_inputs(0, inputs, par, "a0_"),
               **attn_inputs(1, inputs, par, "a1_")) for par in range(2)]
    x_shards = shard_x(inputs["L0_attn_hidden_states"])
    in_maps = []
    for c in range(N_CORES):
        par = c % 2
        m = dict(base)
        m.update(wa[par])
        m["xT"] = x_shards[c]
        in_maps.append(m)
    res = run_bass_kernel_spmd(nc, in_maps, core_ids=list(range(N_CORES)))
    return unshard_x([np.asarray(res.results[c]["yT"]) for c in range(N_CORES)])


def kernel_unfused(**inputs):
    inputs = {k: np.asarray(v) for k, v in inputs.items()}
    x = shard_x(inputs["L0_attn_hidden_states"])
    x = run_ffn(0, x, inputs)
    x = run_attn(0, x, inputs)
    x = run_ffn(0, x, inputs)
    x = run_ffn(1, x, inputs)
    x = run_attn(1, x, inputs)
    x = run_ffn(1, x, inputs)
    return unshard_x(x)
